# version 14
# baseline (speedup 1.0000x reference)
import math
from contextlib import ExitStack

import numpy as np
import ml_dtypes

import concourse.bass as bass
import concourse.mybir as mybir
from concourse.bass_utils import run_bass_kernel_spmd

F32 = mybir.dt.float32
BF16 = mybir.dt.bfloat16
AF = mybir.ActivationFunctionType
ALU = mybir.AluOpType
AX = mybir.AxisListType

B = 2
N = 8192
D = 2048
NH = 16
HD = 128
DFF = 8192
WINDOW = 4096
THETA = 10000.0
NEG = -1e30

TOK = 2048
HALO = 4096
LTOK = HALO + TOK
NQT = TOK // 128
WTILES = 36
NCH = 9
NDT = D // 128
NFT = DFF // 128
NST = LTOK // 2048

C_EPSRMS, C_EPSLN, C_NHALF, C_NEG1, C_HALF, C_INV2048, C_T07W1, C_T07W2 = range(8)

LN_PARANOID = 0
EMIT_P2 = True
EMIT_P2C = True
EMIT_QUANT = True
EMIT_P3 = True
DEBUG_DUMPS = ()

ENGINES = ("sync", "tensor", "vector", "scalar", "gpsimd")


class Prog:
    def __init__(self, nc):
        self.nc = nc
        self.events = {e: [] for e in ENGINES}
        self.ctr = {}
        self.sems = {}
        self.last_wait = {e: {} for e in ENGINES}

    def sem(self, ctx, name):
        h = ctx.enter_context(self.nc.semaphore(name))
        self.sems[name] = h
        self.ctr[name] = 0
        return name

    def wait(self, eng, token):
        if token is None:
            return
        if isinstance(token, (tuple, list)) and token and isinstance(token[0], (tuple, list, type(None))):
            for t in token:
                self.wait(eng, t)
            return
        sem, val = token
        if val <= 0:
            return
        lw = self.last_wait[eng]
        if lw.get(sem, 0) >= val:
            return
        lw[sem] = val
        self.events[eng].append(("wait", self.sems[sem], val))

    def emit(self, eng, fn, inc=None, inc_n=1):
        tok = None
        if inc is not None:
            self.ctr[inc] += inc_n
            tok = (inc, self.ctr[inc])
            self.events[eng].append(("op", fn, self.sems[inc], inc_n))
        else:
            self.events[eng].append(("op", fn, None, 0))
        return tok

    def tok(self, sem):
        return (sem, self.ctr[sem])

    def barrier(self, sems):
        toks = [(s, self.ctr[s]) for s in sems]
        for eng in ENGINES:
            for t in toks:
                self.wait(eng, t)

    def replay(self, eng, engobj):
        for ev in self.events[eng]:
            if ev[0] == "wait":
                engobj.wait_ge(ev[1], ev[2])
            else:
                inst = ev[1](engobj)
                if ev[2] is not None:
                    inst.then_inc(ev[2], ev[3])


def build_program():
    nc = bass.Bass("TRN2", target_bir_lowering=False)

    x_in = nc.dram_tensor("x_pad", [LTOK, D], F32, kind="ExternalInput").ap()
    w_qkv = nc.dram_tensor("w_qkv", [D, 3 * D], F32, kind="ExternalInput").ap()
    w_out = nc.dram_tensor("w_out", [D, D], F32, kind="ExternalInput").ap()
    gam_in = nc.dram_tensor("gam4", [128, 4], F32, kind="ExternalInput").ap()
    w1 = nc.dram_tensor("w1", [DFF, D], F32, kind="ExternalInput").ap()
    b1_in = nc.dram_tensor("b1", [NFT, 128], F32, kind="ExternalInput").ap()
    w2 = nc.dram_tensor("w2", [D, DFF], F32, kind="ExternalInput").ap()
    b2_in = nc.dram_tensor("b2", [NDT, 128], F32, kind="ExternalInput").ap()
    cos_in = nc.dram_tensor("cos_t", [128, LTOK], F32, kind="ExternalInput").ap()
    sin_in = nc.dram_tensor("sin_t", [128, LTOK], F32, kind="ExternalInput").ap()
    masksT_in = nc.dram_tensor("masksT", [4, WTILES, 128, 512], BF16, kind="ExternalInput").ap()
    identb_in = nc.dram_tensor("identb", [128, 128], BF16, kind="ExternalInput").ap()
    identf_in = nc.dram_tensor("identf", [128, 128], F32, kind="ExternalInput").ap()
    onesf_in = nc.dram_tensor("onesf", [128, 128], F32, kind="ExternalInput").ap()
    onesb_in = nc.dram_tensor("onesb", [128, 128], BF16, kind="ExternalInput").ap()
    oo128_in = nc.dram_tensor("oo128", [128, 128], BF16, kind="ExternalInput").ap()
    prot_in = nc.dram_tensor("prot", [128, 128], BF16, kind="ExternalInput").ap()
    consts_in = nc.dram_tensor("consts", [128, 16], F32, kind="ExternalInput").ap()
    out = nc.dram_tensor("out", [TOK, D], F32, kind="ExternalOutput").ap()

    q_sc = nc.dram_tensor("q_sc", [NH, 128, TOK], BF16)
    k_sc = nc.dram_tensor("k_sc", [NH, 128, LTOK], BF16)
    v_sc = nc.dram_tensor("v_sc", [NH, LTOK, 128], BF16)
    ao_sc = nc.dram_tensor("ao_sc", [NH, 128, TOK], BF16)
    x2_sc = nc.dram_tensor("x2_sc", [TOK, D], F32)
    x2ln_sc = nc.dram_tensor("x2ln_sc", [NDT, 128, TOK], BF16)
    w1qT_sc = nc.dram_tensor("w1qT_sc", [NFT, NDT, 128, 128], BF16)
    w2qT_sc = nc.dram_tensor("w2qT_sc", [NDT, NFT, 128, 128], BF16)

    dbg = {}
    if "xlnT" in DEBUG_DUMPS:
        dbg["dbg_xlnT"] = nc.dram_tensor("dbg_xlnT", [128, NDT, 2048], BF16, kind="ExternalOutput").ap()
    if "qkv" in DEBUG_DUMPS:
        dbg["dbg_q"] = nc.dram_tensor("dbg_q", [NH, 128, TOK], BF16, kind="ExternalOutput").ap()
        dbg["dbg_k"] = nc.dram_tensor("dbg_k", [NH, 128, LTOK], BF16, kind="ExternalOutput").ap()
        dbg["dbg_v"] = nc.dram_tensor("dbg_v", [NH, LTOK, 128], BF16, kind="ExternalOutput").ap()
    if "attn" in DEBUG_DUMPS:
        dbg["dbg_ao"] = nc.dram_tensor("dbg_ao", [NH, 128, TOK], BF16, kind="ExternalOutput").ap()
        dbg["dbg_l"] = nc.dram_tensor("dbg_l", [1, 512], F32, kind="ExternalOutput").ap()
        dbg["dbg_g"] = nc.dram_tensor("dbg_gx", [1, 512], F32, kind="ExternalOutput").ap()
        dbg["dbg_mb"] = nc.dram_tensor("dbg_mb", [128, 1], F32, kind="ExternalOutput").ap()
        dbg["dbg_pt"] = nc.dram_tensor("dbg_pt", [128, WTILES, 512], BF16, kind="ExternalOutput").ap()
    if "x2" in DEBUG_DUMPS:
        dbg["dbg_x2"] = nc.dram_tensor("dbg_x2", [TOK, D], F32, kind="ExternalOutput").ap()
    if "ffn" in DEBUG_DUMPS:
        dbg["dbg_x2ln"] = nc.dram_tensor("dbg_x2ln", [NDT, 128, TOK], BF16, kind="ExternalOutput").ap()
        dbg["dbg_g"] = nc.dram_tensor("dbg_g", [128, NFT, 512], BF16, kind="ExternalOutput").ap()
        dbg["dbg_o2"] = nc.dram_tensor("dbg_o2", [128, NDT, 512], F32, kind="ExternalOutput").ap()
    if "wq" in DEBUG_DUMPS:
        dbg["dbg_w1q"] = nc.dram_tensor("dbg_w1q", [NFT, NDT, 128, 128], BF16, kind="ExternalOutput").ap()
        dbg["dbg_thr"] = nc.dram_tensor("dbg_thr", [128, 4], F32, kind="ExternalOutput").ap()
        dbg["dbg_w2q"] = nc.dram_tensor("dbg_w2q", [NDT, NFT, 128, 128], BF16, kind="ExternalOutput").ap()

    with ExitStack() as root:
        er = root.enter_context

        identb = er(nc.sbuf_tensor("identb_s", [128, 128], BF16))
        identf = er(nc.sbuf_tensor("identf_s", [128, 128], F32))
        onesf = er(nc.sbuf_tensor("onesf_s", [128, 128], F32))
        onesb = er(nc.sbuf_tensor("onesb_s", [128, 128], BF16))
        mbias = er(nc.sbuf_tensor("mbias", [128, 1], F32))
        oo128 = er(nc.sbuf_tensor("oo128_s", [128, 128], BF16))
        prot = er(nc.sbuf_tensor("prot_s", [128, 128], BF16))
        consts = er(nc.sbuf_tensor("consts_s", [128, 16], F32))
        gam = er(nc.sbuf_tensor("gam_s", [128, 4], F32))
        b1_sb = er(nc.sbuf_tensor("b1_sb", [128, NFT], F32))
        b2_sb = er(nc.sbuf_tensor("b2_sb", [128, NDT], F32))
        thr = er(nc.sbuf_tensor("thr", [128, 4], F32))
        junk = er(nc.sbuf_tensor("junk", [128, 2048], F32))
        acc = er(nc.sbuf_tensor("acc", [128, 136], F32))

        p = Prog(nc)
        sPE = p.sem(root, "sPE")
        sDV = p.sem(root, "sDV")
        sAC = p.sem(root, "sAC")
        dInit = p.sem(root, "dInit")
        dX = [p.sem(root, "dX0"), p.sem(root, "dX1")]
        dW = [p.sem(root, "dW0"), p.sem(root, "dW1")]
        dS = [p.sem(root, "dS0"), p.sem(root, "dS1")]
        dH = [p.sem(root, "dH0"), p.sem(root, "dH1")]
        dM = [p.sem(root, "dM0"), p.sem(root, "dM1")]
        dQ = [p.sem(root, "dQ0"), p.sem(root, "dQ1")]
        dQS = [p.sem(root, "dQS0"), p.sem(root, "dQS1")]
        dF = [p.sem(root, "dF0"), p.sem(root, "dF1")]
        dO = p.sem(root, "dO")
        ALL_SEMS = [sPE, sDV, sAC, dInit, dX[0], dX[1], dW[0], dW[1], dS[0], dS[1],
                    dH[0], dH[1], dM[0], dM[1], dQ[0], dQ[1], dQS[0], dQS[1],
                    dF[0], dF[1], dO]

        for dst, src in ((identb, identb_in), (identf, identf_in), (onesf, onesf_in),
                         (onesb, onesb_in), (oo128, oo128_in), (prot, prot_in),
                         (consts, consts_in), (gam, gam_in)):
            p.emit("sync", lambda s, d=dst, i=src: s.dma_start(out=d[:], in_=i), inc=dInit, inc_n=16)

        btmp = er(nc.sbuf_tensor("btmp", [128, 128], F32))
        bps_scope = ExitStack()
        bps = bps_scope.enter_context(nc.psum_tensor("bps", [128, 128], F32))
        p.emit("sync", lambda s: s.dma_start(out=btmp[0:32, :], in_=b1_in[0:32]), inc=dInit, inc_n=16)
        p.emit("sync", lambda s: s.dma_start(out=btmp[64:96, :], in_=b1_in[32:64]), inc=dInit, inc_n=16)
        p.wait("tensor", p.tok(dInit))
        tp = p.emit("tensor", lambda t: t.transpose(bps[:], btmp[:], identf[:]), inc=sPE)
        p.wait("vector", tp)
        p.emit("vector", lambda v: v.tensor_copy(b1_sb[:, 0:32], bps[:, 0:32]), inc=sDV)
        ev = p.emit("vector", lambda v: v.tensor_copy(b1_sb[:, 32:64], bps[:, 64:96]), inc=sDV)
        p.wait("sync", ev)
        p.emit("sync", lambda s: s.dma_start(out=btmp[0:NDT, :], in_=b2_in[:]), inc=dInit, inc_n=16)
        p.wait("tensor", p.tok(dInit))
        p.wait("tensor", ev)
        tp2 = p.emit("tensor", lambda t: t.transpose(bps[:], btmp[:], identf[:]), inc=sPE)
        p.wait("vector", tp2)
        p.emit("vector", lambda v: v.tensor_copy(b2_sb[:], bps[:, 0:NDT]), inc=sDV)
        bps_scope.close()
        tok_init = p.tok(dInit)

        p.wait("vector", tok_init)
        gscale = p.emit("vector", lambda v: v.tensor_scalar(
            gam[:, 0:2], gam[:, 0:2], 1.0 / math.sqrt(HD), None, ALU.mult), inc=sDV)

        with nc.sbuf_tensor("mtmp", [128, 4], F32) as mtmp, \
             nc.psum_tensor("mps", [128, 128], F32) as mps:
            p.wait("scalar", gscale)
            ga = p.emit("scalar", lambda s: s.activation(mtmp[:], gam[:], AF.Abs), inc=sAC)
            p.wait("tensor", ga)
            mt1 = p.emit("tensor", lambda t: t.transpose(mps[0:4, 0:128], mtmp[:, 0:4], identf[:]), inc=sPE)
            p.wait("vector", mt1)
            mr = p.emit("vector", lambda v: v.reduce_max(mtmp[0:4, 0:1], mps[0:4, 0:128], axis=AX.X), inc=sDV)
            p.wait("tensor", mr)
            mt2 = p.emit("tensor", lambda t: t.transpose(mps[0:4, 0:128], mtmp[:, 0:4], identf[:]), inc=sPE)
            p.wait("vector", mt2)
            mc2 = p.emit("vector", lambda v: v.tensor_copy(mtmp[0:1, 0:4], mps[0:1, 0:4]), inc=sDV)
            p.wait("vector", mc2)
            mm1 = p.emit("vector", lambda v: v.tensor_tensor(
                mtmp[0:1, 0:1], mtmp[0:1, 0:1], mtmp[0:1, 2:3], ALU.mult), inc=sDV)
            p.wait("vector", mm1)
            mm2 = p.emit("vector", lambda v: v.tensor_scalar(
                mtmp[0:1, 1:2], mtmp[0:1, 0:1], -128.0, None, ALU.mult), inc=sDV)
            p.wait("tensor", mm2)
            mb = p.emit("tensor", lambda t: t.matmul(mps[:, 0:1], onesf[0:1, :], mtmp[0:1, 1:2],
                                                     start=True, stop=True), inc=sPE)
            p.wait("vector", mb)
            p.emit("vector", lambda v: v.tensor_copy(mbias[:], mps[:, 0:1]), inc=sDV)

        def ln_chain(eng_xtok, xap, stats_ap, outap, slot_prev=None):
            p.wait("vector", slot_prev)
            ssum = stats_ap[:, 0:1]
            smean = stats_ap[:, 4:5]
            svar = stats_ap[:, 8:9]
            srstd = stats_ap[:, 12:13]
            p.wait("vector", eng_xtok)
            red_tok = p.emit("vector", lambda v: v.reduce_sum(ssum, xap, axis=AX.X), inc=sDV)
            p.wait("vector", red_tok)
            mean_tok = p.emit("vector", lambda v: v.tensor_scalar(
                smean, ssum, 1.0 / 2048.0, None, ALU.mult), inc=sDV)
            p.wait("vector", mean_tok)
            m2_tok = p.emit("vector", lambda v: v.tensor_tensor(svar, smean, smean, ALU.mult), inc=sDV)
            p.wait("scalar", eng_xtok)
            p.wait("scalar", mean_tok)
            sq_tok = p.emit("scalar", lambda s: s.activation(junk[:], xap, AF.Square, accum_out=ssum), inc=sAC)
            p.wait("vector", sq_tok)
            p.wait("vector", m2_tok)
            var_tok = p.emit("vector", lambda v: v.scalar_tensor_tensor(
                svar, ssum, 1.0 / 2048.0, svar, ALU.mult, ALU.subtract), inc=sDV)
            p.wait("scalar", var_tok)
            ln_tok = p.emit("scalar", lambda s: s.activation(svar, svar, AF.Ln, bias=consts[:, C_EPSLN:C_EPSLN + 1]), inc=sAC)
            p.wait("scalar", ln_tok)
            r_tok = p.emit("scalar", lambda s: s.activation(srstd, svar, AF.Exp, scale=consts[:, C_NHALF:C_NHALF + 1]), inc=sAC)
            p.wait("vector", r_tok)
            p.wait("vector", var_tok)
            ap_tok = p.emit("vector", lambda v: v.tensor_scalar(
                outap, xap, smean, srstd, ALU.subtract, ALU.mult), inc=sDV)
            return ap_tok, sq_tok, r_tok

        with ExitStack() as P1:
            e1 = P1.enter_context
            Cq = e1(nc.sbuf_tensor("Cq", [128, TOK], BF16))
            Sq = e1(nc.sbuf_tensor("Sq", [128, TOK], BF16))
            Ck = e1(nc.sbuf_tensor("Ck", [128, LTOK], BF16))
            Sk = e1(nc.sbuf_tensor("Sk", [128, LTOK], BF16))
            ropebuf = e1(nc.sbuf_tensor("ropebuf", [128, 2, 2, 512], F32))
            xsb = e1(nc.sbuf_tensor("xsb", [128, 2, 2048], F32))
            xln = e1(nc.sbuf_tensor("xln", [128, 2, 2048], BF16))
            stats = e1(nc.sbuf_tensor("stats", [128, 32], F32))
            xlnT = e1(nc.sbuf_tensor("xlnT", [128, NDT, 2048], BF16))
            wbuf = e1(nc.sbuf_tensor("wbuf", [128, 2, NDT, 512], BF16))
            sqb = e1(nc.sbuf_tensor("sqb", [128, 2, 512], BF16))
            rstdb = e1(nc.sbuf_tensor("rstdb", [128, 2, 512], F32))
            qnb = e1(nc.sbuf_tensor("qnb", [128, 2, 512], BF16))
            ropec = e1(nc.sbuf_tensor("ropec", [128, 2, 512], BF16))
            roted = e1(nc.sbuf_tensor("roted", [128, 2, 512], BF16))
            qkst = e1(nc.sbuf_tensor("qkst", [128, 2, 512], BF16))
            vst = e1(nc.sbuf_tensor("vst", [128, 2, 512], BF16))
            trps = [e1(nc.psum_tensor("trps0", [128, 1024], BF16)),
                    e1(nc.psum_tensor("trps1", [128, 1024], BF16))]
            mmps = [e1(nc.psum_tensor("mmps0", [128, 512], F32)),
                    e1(nc.psum_tensor("mmps1", [128, 512], F32))]
            ssps = e1(nc.psum_tensor("ssps", [128, 512], F32))
            rotps = [e1(nc.psum_tensor("rotps0", [128, 512], F32)),
                     e1(nc.psum_tensor("rotps1", [128, 512], F32))]

            rope_use = [None, None]
            for ci in range(LTOK // 512 + TOK // 512):
                par = ci % 2
                if ci < LTOK // 512:
                    srcs = (cos_in[:, ci * 512:(ci + 1) * 512], sin_in[:, ci * 512:(ci + 1) * 512])
                    dsts = (Ck[:, ci * 512:(ci + 1) * 512], Sk[:, ci * 512:(ci + 1) * 512])
                    gcols, gdep = (2, 3), tok_init
                else:
                    cj = ci - LTOK // 512
                    srcs = (cos_in[:, HALO + cj * 512:HALO + (cj + 1) * 512],
                            sin_in[:, HALO + cj * 512:HALO + (cj + 1) * 512])
                    dsts = (Cq[:, cj * 512:(cj + 1) * 512], Sq[:, cj * 512:(cj + 1) * 512])
                    gcols, gdep = (0, 1), gscale
                p.wait("sync", rope_use[par])
                p.emit("sync", lambda s, a=srcs[0], pr=par: s.dma_start(out=ropebuf[:, pr, 0, :], in_=a), inc=dX[par], inc_n=16)
                tl = p.emit("sync", lambda s, a=srcs[1], pr=par: s.dma_start(out=ropebuf[:, pr, 1, :], in_=a), inc=dX[par], inc_n=16)
                p.wait("vector", tl)
                p.wait("vector", gdep)
                p.emit("vector", lambda v, d=dsts[0], pr=par, g=gcols[0]: v.tensor_scalar(
                    d, ropebuf[:, pr, 0, :], gam[:, g:g + 1], None, ALU.mult), inc=sDV)
                u = p.emit("vector", lambda v, d=dsts[1], pr=par, g=gcols[1]: v.tensor_scalar(
                    d, ropebuf[:, pr, 1, :], gam[:, g:g + 1], None, ALU.mult), inc=sDV)
                rope_use[par] = u
            rope_done = p.tok(sDV)

            xsb_free = [None, None]
            stats_prev = [None, None]
            trps_free = [None, None]
            mmps_free = [None, None]
            ssps_free = None
            rotps_free = [None, None]
            sqb_free = [None, None]
            rstdb_free = [None, None]
            qnb_free = [None, None]
            ropec_free = [None, None]
            roted_free = [None, None]
            qkst_free = [None, None]
            vst_free = [None, None]
            wbuf_free = [None, None]
            xlnT_free_pe = None

            for st in range(NST):
                if xlnT_free_pe is not None:
                    p.wait("vector", xlnT_free_pe)
                evt = None
                for tt in range(16):
                    par = tt % 2
                    lt = st * 16 + tt
                    p.wait("sync", xsb_free[par])
                    xt = p.emit("sync", lambda s, lt=lt, par=par: s.dma_start(
                        out=xsb[:, par, :], in_=x_in[lt * 128:(lt + 1) * 128, :]), inc=dX[par], inc_n=16)
                    ap_tok, sq_tok, r_tok = ln_chain(xt, xsb[:, par, :], stats[:, par * 16:(par + 1) * 16],
                                                     xln[:, par, :], slot_prev=stats_prev[par])
                    stats_prev[par] = r_tok
                    xsb_free[par] = (ap_tok, sq_tok)
                    for half in range(2):
                        p.wait("tensor", trps_free[half])
                        p.wait("tensor", ap_tok)
                        for j in range(8):
                            dt = half * 8 + j
                            tp = p.emit("tensor", lambda t, par=par, dt=dt, half=half, j=j: t.transpose(
                                trps[half][:, j * 128:(j + 1) * 128],
                                xln[:, par, dt * 128:(dt + 1) * 128], identb[:]),
                                inc=(sPE if j == 7 else None))
                        p.wait("vector", tp)
                        evt = p.emit("vector", lambda v, half=half, tt=tt: v.tensor_copy(
                            xlnT[:, half * 8:(half + 1) * 8, tt * 128:(tt + 1) * 128],
                            trps[half].ap().rearrange("p (j m) -> p j m", j=8)), inc=sDV)
                        trps_free[half] = evt
                xlnT_ready = evt

                mchunks = list(range(4, 12)) if st < 2 else list(range(12))
                pscnt = 0
                wlast = None
                pending = None

                def emit_qk_epilogue(tile):
                    nonlocal ssps_free
                    ps, ps_i, mm, isq, head, tg, sp = tile
                    p.wait("scalar", mm)
                    p.wait("scalar", sqb_free[sp])
                    sq = p.emit("scalar", lambda s, ps=ps, sp=sp: s.activation(
                        sqb[:, sp, :], ps[:], AF.Square), inc=sAC)
                    p.wait("tensor", sq)
                    p.wait("tensor", ssps_free)
                    ssm = p.emit("tensor", lambda t, sp=sp: t.matmul(
                        ssps[:], oo128[:], sqb[:, sp, :], start=True, stop=True), inc=sPE)
                    sqb_free[sp] = ssm
                    p.wait("scalar", ssm)
                    p.wait("scalar", rstdb_free[sp])
                    p.emit("scalar", lambda s, sp=sp: s.activation(
                        rstdb[:, sp, :], ssps[:], AF.Ln, bias=consts[:, C_EPSRMS:C_EPSRMS + 1]), inc=sAC)
                    l1 = p.tok(sAC)
                    ssps_free = l1
                    r1 = p.emit("scalar", lambda s, sp=sp: s.activation(
                        rstdb[:, sp, :], rstdb[:, sp, :], AF.Exp, scale=consts[:, C_NHALF:C_NHALF + 1]), inc=sAC)
                    p.wait("vector", r1)
                    p.wait("vector", qnb_free[sp])
                    qn = p.emit("vector", lambda v, ps=ps, sp=sp: v.tensor_tensor(
                        qnb[:, sp, :], ps[:], rstdb[:, sp, :], ALU.mult), inc=sDV)
                    mmps_free[ps_i] = qn
                    rstdb_free[sp] = qn
                    p.wait("tensor", qn)
                    p.wait("tensor", rotps_free[sp])
                    rmm = p.emit("tensor", lambda t, sp=sp: t.matmul(
                        rotps[sp][:], prot[:], qnb[:, sp, :], start=True, stop=True), inc=sPE)
                    if isq:
                        Ct, St, base = Cq, Sq, tg * 512
                    else:
                        Ct, St, base = Ck, Sk, st * 2048 + tg * 512
                    p.wait("vector", rope_done)
                    p.wait("vector", ropec_free[sp])
                    c1 = p.emit("vector", lambda v, sp=sp, Ct=Ct, base=base: v.tensor_tensor(
                        ropec[:, sp, :], qnb[:, sp, :], Ct[:, base:base + 512], ALU.mult), inc=sDV)
                    qnb_free[sp] = c1
                    p.wait("vector", rmm)
                    p.wait("vector", roted_free[sp])
                    rt = p.emit("vector", lambda v, sp=sp, St=St, base=base: v.tensor_tensor(
                        roted[:, sp, :], rotps[sp][:], St[:, base:base + 512], ALU.mult), inc=sDV)
                    rotps_free[sp] = rt
                    p.wait("vector", qkst_free[sp])
                    fin = p.emit("vector", lambda v, sp=sp: v.tensor_tensor(
                        qkst[:, sp, :], ropec[:, sp, :], roted[:, sp, :], ALU.add), inc=sDV)
                    ropec_free[sp] = fin
                    roted_free[sp] = fin
                    p.wait("sync", fin)
                    if isq:
                        dst = q_sc.ap()[head, :, tg * 512:(tg + 1) * 512]
                    else:
                        dst = k_sc.ap()[head, :, st * 2048 + tg * 512: st * 2048 + (tg + 1) * 512]
                    stt = p.emit("sync", lambda s, dst=dst, sp=sp: s.dma_start(
                        out=dst, in_=qkst[:, sp, :]), inc=dS[sp], inc_n=16)
                    qkst_free[sp] = stt

                for mc in mchunks:
                    par = mc % 2
                    p.wait("gpsimd", wbuf_free[par])
                    wt = p.emit("gpsimd", lambda g, mc=mc, par=par: g.dma_start(
                        out=wbuf[:, par],
                        in_=w_qkv[:, mc * 512:(mc + 1) * 512].rearrange("(dt p) m -> p dt m", p=128)),
                        inc=dW[par], inc_n=16)
                    if mc < 8:
                        isq = mc < 4
                        for ms in range(4):
                            head = (mc - (0 if isq else 4)) * 4 + ms
                            for tg in range(4):
                                ps_i = pscnt % 2
                                sp = pscnt % 2
                                pscnt += 1
                                ps = mmps[ps_i]
                                p.wait("tensor", mmps_free[ps_i])
                                p.wait("tensor", wt)
                                p.wait("tensor", xlnT_ready)
                                for dt in range(NDT):
                                    mm = p.emit("tensor", lambda t, ps=ps, par=par, dt=dt, ms=ms, tg=tg: t.matmul(
                                        ps[:], wbuf[:, par, dt, ms * 128:(ms + 1) * 128],
                                        xlnT[:, dt, tg * 512:(tg + 1) * 512],
                                        start=(dt == 0), stop=(dt == NDT - 1)),
                                        inc=(sPE if dt == NDT - 1 else None))
                                if pending is not None:
                                    emit_qk_epilogue(pending)
                                pending = (ps, ps_i, mm, isq, head, tg, sp)
                                wlast = mm
                    else:
                        for tt in range(16):
                            ps_i = pscnt % 2
                            pscnt += 1
                            ps = mmps[ps_i]
                            p.wait("tensor", mmps_free[ps_i])
                            p.wait("tensor", wt)
                            p.wait("tensor", xlnT_ready)
                            for dt in range(NDT):
                                mm = p.emit("tensor", lambda t, ps=ps, par=par, dt=dt, tt=tt: t.matmul(
                                    ps[:], xlnT[:, dt, tt * 128:(tt + 1) * 128], wbuf[:, par, dt, :],
                                    start=(dt == 0), stop=(dt == NDT - 1)),
                                    inc=(sPE if dt == NDT - 1 else None))
                            if pending is not None:
                                emit_qk_epilogue(pending)
                                pending = None
                            sp = tt % 2
                            p.wait("scalar", mm)
                            p.wait("scalar", vst_free[sp])
                            cv = p.emit("scalar", lambda s, ps=ps, sp=sp: s.copy(vst[:, sp, :], ps[:]), inc=sAC)
                            mmps_free[ps_i] = cv
                            p.wait("sync", cv)
                            vlast = None
                            for hs in range(4):
                                head = (mc - 8) * 4 + hs
                                vlast = p.emit("sync", lambda s, head=head, st=st, tt=tt, sp=sp, hs=hs: s.dma_start(
                                    out=v_sc.ap()[head, st * 2048 + tt * 128: st * 2048 + (tt + 1) * 128, :],
                                    in_=vst[:, sp, hs * 128:(hs + 1) * 128]), inc=dS[sp], inc_n=16)
                            vst_free[sp] = vlast
                            wlast = mm
                    wbuf_free[par] = wlast
                if pending is not None:
                    emit_qk_epilogue(pending)
                    pending = None
                xlnT_free_pe = wlast

            if "xlnT" in DEBUG_DUMPS:
                p.barrier(ALL_SEMS)
                p.emit("sync", lambda s: s.dma_start(out=dbg["dbg_xlnT"], in_=xlnT[:]), inc=dO, inc_n=16)
            if "qkv" in DEBUG_DUMPS:
                p.barrier(ALL_SEMS)
                for name, sc in (("dbg_q", q_sc), ("dbg_k", k_sc), ("dbg_v", v_sc)):
                    p.emit("sync", lambda s, n=name, sc=sc: s.dma_start(out=dbg[n], in_=sc.ap()), inc=dO, inc_n=16)

        p.barrier(ALL_SEMS)

        if EMIT_P2:
            with ExitStack() as P2:
                e2 = P2.enter_context
                qh = e2(nc.sbuf_tensor("qh", [128, 2, TOK], BF16))
                kh = e2(nc.sbuf_tensor("kh", [128, 2, LTOK], BF16))
                vh = e2(nc.sbuf_tensor("vh", [128, 2, LTOK // 128, 128], BF16))
                mT = e2(nc.sbuf_tensor("mT", [128, WTILES, 512], BF16))
                PT = e2(nc.sbuf_tensor("PT", [128, WTILES, 512], BF16))
                aost = e2(nc.sbuf_tensor("aost", [128, 2, 512], BF16))
                lbuf = e2(nc.sbuf_tensor("lbuf", [1, 2, 512], F32))
                linv = e2(nc.sbuf_tensor("linv", [1, 2, 512], F32))
                gbuf = e2(nc.sbuf_tensor("gbuf", [1, 2, 512], BF16))
                gbc = e2(nc.sbuf_tensor("gbc", [128, 2, 512], BF16))
                scT = [e2(nc.psum_tensor("scT0", [128, 512], F32)),
                       e2(nc.psum_tensor("scT1", [128, 512], F32))]
                lps = e2(nc.psum_tensor("lps", [1, 512], F32))
                ovps = [e2(nc.psum_tensor("ovps0", [128, 512], F32)),
                        e2(nc.psum_tensor("ovps1", [128, 512], F32))]
                bcps = e2(nc.psum_tensor("bcps", [128, 512], F32))
                qtps = [e2(nc.psum_tensor("qtps0", [128, 1024], BF16)),
                        e2(nc.psum_tensor("qtps1", [128, 1024], BF16))]
                qb2 = e2(nc.sbuf_tensor("qb2", [128, 2, 2048], F32))
                s1b = e2(nc.sbuf_tensor("s1b", [128, 2, 2048], BF16))
                s2b = e2(nc.sbuf_tensor("s2b", [128, 2, 2048], BF16))
                wqb = e2(nc.sbuf_tensor("wqb", [128, 2, 2048], BF16))
                wstg = e2(nc.sbuf_tensor("wstg", [128, 2, 2048], BF16))

                nw1 = DFF // 128
                nw2 = (D // 128) * (DFF // 2048)
                NSLOT = 2

                def wtile_src(it):
                    if it < nw1:
                        return w1[it * 128:(it + 1) * 128, :], True, it
                    j = it - nw1
                    rt, cq = divmod(j, 4)
                    return w2[rt * 128:(rt + 1) * 128, cq * 2048:(cq + 1) * 2048], False, j

                qload = {"n": 0, "free": [None] * 2, "toks": {}}
                qsems = [dQ[0], dQ[1]]

                def emit_qload():
                    li = qload["n"]
                    if li >= 2 * (nw1 + nw2):
                        return
                    qload["n"] += 1
                    slot = li % NSLOT
                    srcw, _, _ = wtile_src(li % (nw1 + nw2))
                    p.wait("scalar", qload["free"][slot])
                    qload["toks"][li] = p.emit("scalar", lambda s, srcw=srcw, slot=slot: s.dma_start(
                        out=qb2[:, slot, :], in_=srcw), inc=qsems[slot], inc_n=16)

                sstate = {"i": 0}

                def emit_scale_step():
                    it = sstate["i"]
                    if it >= nw1 + nw2:
                        return
                    sstate["i"] += 1
                    while qload["n"] < min(it + NSLOT, nw1 + nw2):
                        emit_qload()
                    slot = it % NSLOT
                    p.wait("scalar", qload["toks"][it])
                    ab = p.emit("scalar", lambda s, slot=slot, it=it: s.activation(
                        junk[:], qb2[:, slot, :], AF.Abs, accum_out=acc[:, it:it + 1]), inc=sAC)
                    qload["free"][slot] = ab

                qst = {"t1": 0, "t2": 0, "t3": 0, "sfree": [None, None],
                       "wqfree": [None, None], "tpfree": [None, None], "stfree": [None, None]}

                def quant_q1():
                    t = qst["t1"]
                    if t >= nw1 + nw2:
                        return
                    qst["t1"] += 1
                    li = nw1 + nw2 + t
                    while qload["n"] < min(li + NSLOT, 2 * (nw1 + nw2)):
                        emit_qload()
                    slot = li % NSLOT
                    _, isw1, _ = wtile_src(t)
                    tcols = (1, 0) if isw1 else (3, 2)
                    sp = t % 2
                    p.wait("scalar", qload["toks"][li])
                    p.wait("scalar", thr_done[0])
                    p.wait("scalar", qst["sfree"][sp])
                    p.emit("scalar", lambda s, slot=slot, sp=sp, tc=tcols[0]: s.activation(
                        s1b[:, sp, :], qb2[:, slot, :], AF.Sign, bias=thr[:, tc:tc + 1]), inc=sAC)
                    sg2 = p.emit("scalar", lambda s, slot=slot, sp=sp, tc=tcols[1]: s.activation(
                        s2b[:, sp, :], qb2[:, slot, :], AF.Sign, bias=thr[:, tc:tc + 1]), inc=sAC)
                    qload["free"][slot] = sg2
                    qst.setdefault("sgtok", {})[t] = sg2

                def quant_q2():
                    t = qst["t2"]
                    if t >= nw1 + nw2 or t >= qst["t1"]:
                        return
                    qst["t2"] += 1
                    sp = t % 2
                    p.wait("vector", qst["sgtok"][t])
                    p.wait("vector", qst["wqfree"][sp])
                    ad = p.emit("vector", lambda v, sp=sp: v.tensor_tensor(
                        wqb[:, sp, :], s1b[:, sp, :], s2b[:, sp, :], ALU.add), inc=sDV)
                    qst["sfree"][sp] = ad
                    qst.setdefault("adtok", {})[t] = ad

                def quant_q3():
                    t = qst["t3"]
                    if t >= nw1 + nw2 or t >= qst["t2"]:
                        return
                    qst["t3"] += 1
                    sp = t % 2
                    _, isw1, j = wtile_src(t)
                    evq = None
                    tq = None
                    for half in range(2):
                        p.wait("tensor", qst["tpfree"][half])
                        p.wait("tensor", qst["adtok"][t])
                        for jj in range(8):
                            dt = half * 8 + jj
                            tq = p.emit("tensor", lambda tt_, sp=sp, half=half, jj=jj, dt=dt: tt_.transpose(
                                qtps[half][:, jj * 128:(jj + 1) * 128], wqb[:, sp, dt * 128:(dt + 1) * 128],
                                identb[:]), inc=(sPE if jj == 7 else None))
                        p.wait("vector", tq)
                        p.wait("vector", qst["stfree"][sp])
                        evq = p.emit("vector", lambda v, sp=sp, half=half: v.tensor_copy(
                            wstg[:, sp, half * 1024:(half + 1) * 1024], qtps[half].ap()), inc=sDV)
                        qst["tpfree"][half] = evq
                    qst["wqfree"][sp] = tq
                    p.wait("sync", evq)
                    if isw1:
                        dstw = w1qT_sc.ap()[t].rearrange("dt p m -> p dt m")
                    else:
                        rt, cq = divmod(j, 4)
                        dstw = w2qT_sc.ap()[rt, cq * 16:(cq + 1) * 16].rearrange("ft p m -> p ft m")
                    stok = p.emit("sync", lambda s, dstw=dstw, sp=sp: s.dma_start(
                        out=dstw, in_=wstg[:, sp, :].rearrange("p (dt m) -> p dt m", dt=16)),
                        inc=dQS[sp], inc_n=16)
                    qst["stfree"][sp] = stok

                thr_done = [None]
                pend_epi = None
                kv_free = [None, None]
                qh_free = [None, None]
                scT_free = [None, None]
                mT_free = [[None] * 4, [None] * 4]
                mT_tok = [None] * 4
                PT_free_pe = None
                lps_free = None
                ovps_free = [None, None]
                bcps_free = None
                aost_free = [None, None]
                exp_toks = [None] * WTILES

                def emit_head_epi(d):
                    nonlocal bcps_free, ovps_free, aost_free
                    ph, pg, sp, gc, pv = d["h"], d["g"], d["sp"], d["gc"], d["pv"]
                    p.wait("tensor", gc)
                    p.wait("tensor", bcps_free)
                    bc = p.emit("tensor", lambda t, sp=sp: t.matmul(
                        bcps[:], onesb[0:1, :], gbuf[0:1, sp, :], start=True, stop=True), inc=sPE)
                    p.wait("scalar", bc)
                    be = p.emit("scalar", lambda s, sp=sp: s.copy(gbc[:, sp, :], bcps[:]), inc=sAC)
                    bcps_free = be
                    p.wait("vector", be)
                    p.wait("vector", aost_free[sp])
                    evo = p.emit("vector", lambda v, sp=sp: v.tensor_tensor(
                        aost[:, sp, :], ovps[sp][:], gbc[:, sp, :], ALU.mult), inc=sDV)
                    ovps_free[sp] = evo
                    p.wait("sync", evo)
                    ast = p.emit("sync", lambda s, ph=ph, pg=pg, sp=sp: s.dma_start(
                        out=ao_sc.ap()[ph, :, pg * 512:(pg + 1) * 512], in_=aost[:, sp, :]),
                        inc=dS[sp], inc_n=16)
                    aost_free[sp] = ast

                for g in range(4):
                    if g == 2:
                        stps = scT[1]
                        abs_done = p.tok(sAC)
                        p.wait("vector", abs_done)
                        r1 = p.emit("vector", lambda v: v.reduce_sum(acc[:, 128:129], acc[:, 0:64], axis=AX.X), inc=sDV)
                        r2 = p.emit("vector", lambda v: v.reduce_sum(acc[:, 129:130], acc[:, 64:128], axis=AX.X), inc=sDV)
                        p.wait("tensor", r1)
                        p.wait("tensor", r2)
                        p.wait("tensor", scT_free[1])
                        t1m = p.emit("tensor", lambda t: t.matmul(stps[0:1, 0:1], acc[:, 128:129], onesf[:, 0:1],
                                                                  start=True, stop=True), inc=sPE)
                        t2m = p.emit("tensor", lambda t: t.matmul(stps[0:1, 1:2], acc[:, 129:130], onesf[:, 0:1],
                                                                  start=True, stop=True), inc=sPE)
                        p.wait("vector", t2m)
                        sc1 = p.emit("vector", lambda v: v.tensor_scalar(
                            acc[0:1, 130:132], stps[0:1, 0:2], 0.7 / float(DFF * D), None, ALU.mult), inc=sDV)
                        p.wait("tensor", sc1)
                        bb = p.emit("tensor", lambda t: t.matmul(stps[:, 2:3], onesf[0:1, :], acc[0:1, 130:131],
                                                                 start=True, stop=True), inc=sPE)
                        bb2 = p.emit("tensor", lambda t: t.matmul(stps[:, 3:4], onesf[0:1, :], acc[0:1, 131:132],
                                                                  start=True, stop=True), inc=sPE)
                        p.wait("vector", bb2)
                        p.emit("vector", lambda v: v.tensor_copy(thr[:, 0:1], stps[:, 2:3]), inc=sDV)
                        cp2 = p.emit("vector", lambda v: v.tensor_copy(thr[:, 2:3], stps[:, 3:4]), inc=sDV)
                        p.wait("vector", cp2)
                        th1 = p.emit("vector", lambda v: v.tensor_scalar(
                            thr[:, 1:2], thr[:, 0:1], -1.0, None, ALU.mult), inc=sDV)
                        th2 = p.emit("vector", lambda v: v.tensor_scalar(
                            thr[:, 3:4], thr[:, 2:3], -1.0, None, ALU.mult), inc=sDV)
                        thr_done[0] = th2
                        scT_free[1] = cp2
                    for r in range(4):
                        p.wait("sync", mT_free[0][r])
                        mT_tok[r] = p.emit("sync", lambda s, g=g, r=r: s.dma_start(
                            out=mT[:, r * 9:(r + 1) * 9, :],
                            in_=masksT_in[g, r * 9:(r + 1) * 9].rearrange("jj p q -> p jj q")),
                            inc=dM[r % 2], inc_n=16)
                    for h in range(NH):
                        par = h % 2
                        p.wait("sync", kv_free[par])
                        kl = p.emit("sync", lambda s, h=h, par=par: s.dma_start(
                            out=kh[:, par, :], in_=k_sc.ap()[h]), inc=dH[par], inc_n=16)
                        vl = p.emit("sync", lambda s, h=h, par=par: s.dma_start(
                            out=vh[:, par],
                            in_=v_sc.ap()[h].rearrange("(tt p) d -> p tt d", p=128)), inc=dH[par], inc_n=16)
                        p.wait("sync", qh_free[par])
                        ql = p.emit("sync", lambda s, h=h, par=par: s.dma_start(
                            out=qh[:, par, :], in_=q_sc.ap()[h]), inc=dH[par], inc_n=16)
                        hload = ql
                        last_pv = None
                        for jj in range(WTILES + 2):
                            if jj < WTILES:
                                ps_i = jj % 2
                                ps = scT[ps_i]
                                irel_open = all((L >= 32) and (i_ < jj < i_ + 32)
                                                for L in [4 * g + jj] for i_ in range(4))
                                p.wait("tensor", scT_free[ps_i])
                                p.wait("tensor", mT_tok[jj // 9])
                                p.wait("tensor", hload)
                                if not irel_open:
                                    p.emit("tensor", lambda t, ps=ps, jj=jj: t.matmul(
                                        ps[:], identb[:], mT[:, jj, :],
                                        start=True, stop=False))
                                sc = p.emit("tensor", lambda t, ps=ps, g=g, jj=jj, par=par, oa=irel_open: t.matmul(
                                    ps[:], kh[:, par, 128 * (4 * g + jj): 128 * (4 * g + jj + 1)],
                                    qh[:, par, g * 512:(g + 1) * 512],
                                    start=oa, stop=True), inc=sPE)
                                p.wait("scalar", sc)
                                p.wait("scalar", PT_free_pe)
                                ex = p.emit("scalar", lambda s, ps=ps, jj=jj: s.activation(
                                    PT[:, jj, :], ps[:], AF.Exp, bias=mbias[:]), inc=sAC)
                                exp_toks[jj] = ex
                                scT_free[ps_i] = ex
                            jl = jj - 2
                            if 0 <= jl < WTILES:
                                p.wait("tensor", exp_toks[jl])
                                if jl == 0:
                                    p.wait("tensor", lps_free)
                                    p.wait("tensor", ovps_free[h % 2])
                                    p.wait("tensor", vl)
                                p.emit("tensor", lambda t, jl=jl: t.matmul(
                                    lps[:], onesb[:, 0:1], PT[:, jl, :],
                                    start=(jl == 0), stop=(jl == WTILES - 1)),
                                    inc=(sPE if jl == WTILES - 1 else None))
                                pv = p.emit("tensor", lambda t, g=g, jl=jl, par=par, sp2=h % 2: t.matmul(
                                    ovps[sp2][:], vh[:, par, 4 * g + jl, :], PT[:, jl, :],
                                    start=(jl == 0), stop=(jl == WTILES - 1)),
                                    inc=(sPE if jl == WTILES - 1 else None))
                        PT_free_pe = pv
                        lv_tok = p.tok(sPE)
                        sp = h % 2
                        p.wait("vector", lv_tok)
                        le = p.emit("vector", lambda v, sp=sp: v.tensor_copy(lbuf[0:1, sp, :], lps[:]), inc=sDV)
                        lps_free = le
                        p.wait("vector", le)
                        gr = p.emit("vector", lambda v, sp=sp: v.reciprocal(linv[0:1, sp, :], lbuf[0:1, sp, :]), inc=sDV)
                        p.wait("vector", gr)
                        gc = p.emit("vector", lambda v, sp=sp: v.tensor_copy(gbuf[0:1, sp, :], linv[0:1, sp, :]), inc=sDV)
                        if pend_epi is not None:
                            emit_head_epi(pend_epi)
                        pend_epi = {"h": h, "g": g, "sp": sp, "gc": gc, "pv": pv}
                        kv_free[par] = pv
                        if h == 0 and g == 0 and "attn" in DEBUG_DUMPS:
                            p.emit("sync", lambda s: s.dma_start(out=dbg["dbg_l"], in_=lbuf[0:1, 0, :]), inc=dO, inc_n=16)
                            p.emit("sync", lambda s: s.dma_start(out=dbg["dbg_g"], in_=linv[0:1, 0, :]), inc=dO, inc_n=16)
                            p.emit("sync", lambda s: s.dma_start(out=dbg["dbg_mb"], in_=mbias[:]), inc=dO, inc_n=16)
                            p.emit("sync", lambda s: s.dma_start(out=dbg["dbg_pt"], in_=PT[:]), inc=dO, inc_n=16)
                            PT_free_pe = p.tok(dO)
                        qh_free[par] = sc
                        for r in range(4):
                            mT_free[0][r] = sc
                        if g < 2:
                            for _q in range(4):
                                emit_scale_step()
                        else:
                            for _q in range(4):
                                quant_q3()
                                quant_q2()
                                quant_q1()
                if pend_epi is not None:
                    emit_head_epi(pend_epi)
                    pend_epi = None
                for _ in range(3):
                    quant_q3()
                    quant_q2()
                    quant_q1()

                if "attn" in DEBUG_DUMPS:
                    p.barrier(ALL_SEMS)
                    p.emit("sync", lambda s: s.dma_start(out=dbg["dbg_ao"], in_=ao_sc.ap()), inc=dO, inc_n=16)

        p.barrier(ALL_SEMS)

        if EMIT_P2C:
            with ExitStack() as P2C:
                e3 = P2C.enter_context
                wob = e3(nc.sbuf_tensor("wob", [128, NDT, D], BF16))
                aos = e3(nc.sbuf_tensor("aos", [128, 2, NDT, 512], BF16))
                pstg = e3(nc.sbuf_tensor("pstg", [128, 2, 512], F32))
                atok = e3(nc.sbuf_tensor("atok", [128, 4, 2048], F32))
                lntmp = e3(nc.sbuf_tensor("lntmp", [128, 2048], F32))
                xre = e3(nc.sbuf_tensor("xre", [128, 2, 2048], F32))
                x2b = e3(nc.sbuf_tensor("x2b", [128, 2, 2048], F32))
                x2l = e3(nc.sbuf_tensor("x2l", [128, 2, 2048], BF16))
                x2tstg = e3(nc.sbuf_tensor("x2tstg", [128, 2, 2048], BF16))
                stats3 = e3(nc.sbuf_tensor("stats3", [128, 64], F32))
                prps = [e3(nc.psum_tensor("prps0", [128, 512], F32)),
                        e3(nc.psum_tensor("prps1", [128, 512], F32))]
                ttps = [e3(nc.psum_tensor("ttps0", [128, 512], F32)),
                        e3(nc.psum_tensor("ttps1", [128, 512], F32))]
                tbps = [e3(nc.psum_tensor("tbps0", [128, 1024], BF16)),
                        e3(nc.psum_tensor("tbps1", [128, 1024], BF16))]

                for dc in range(4):
                    p.emit("gpsimd", lambda g, dc=dc: g.dma_start(
                        out=wob[:, :, dc * 512:(dc + 1) * 512],
                        in_=w_out[:, dc * 512:(dc + 1) * 512].rearrange("(dt p) m -> p dt m", p=128)),
                        inc=dW[dc % 2], inc_n=16)
                wol_toks = [p.tok(dW[0]), p.tok(dW[1])]

                aos_free = [None, None]
                pstg_free = [None, None]
                prps_free = [None, None]
                ttps_free = [None, None]
                tbps_free = [None, None]
                xre_free = [None, None]
                x2b_free = [None, None]
                x2l_free = [None, None]
                x2tstg_free = [None, None]
                st3_prev = [None, None]
                st3b_prev = [None, None]
                atok_free = None

                for tg in range(4):
                    ap_ = tg % 2
                    p.wait("sync", aos_free[ap_])
                    al = p.emit("sync", lambda s, tg=tg, ap_=ap_: s.dma_start(
                        out=aos[:, ap_],
                        in_=ao_sc.ap()[:, :, tg * 512:(tg + 1) * 512].rearrange("dt p m -> p dt m")),
                        inc=dH[ap_], inc_n=16)
                    aos_last_mm = None
                    ev2 = None
                    for dpt in range(NDT):
                        ps_i = dpt % 2
                        ps = prps[ps_i]
                        p.wait("tensor", prps_free[ps_i])
                        p.wait("tensor", al)
                        for tok2 in wol_toks:
                            p.wait("tensor", tok2)
                        for dint in range(NDT):
                            mm = p.emit("tensor", lambda t, ps=ps, ap_=ap_, dint=dint, dpt=dpt: t.matmul(
                                ps[:], wob[:, dint, dpt * 128:(dpt + 1) * 128], aos[:, ap_, dint, :],
                                start=(dint == 0), stop=(dint == NDT - 1)),
                                inc=(sPE if dint == NDT - 1 else None))
                        aos_last_mm = mm
                        sp = dpt % 2
                        p.wait("vector", mm)
                        p.wait("vector", pstg_free[sp])
                        ev = p.emit("vector", lambda v, ps=ps, sp=sp: v.tensor_copy(pstg[:, sp, :], ps[:]), inc=sDV)
                        prps_free[ps_i] = ev
                        p.wait("tensor", ttps_free[sp])
                        p.wait("tensor", ev)
                        tp = None
                        for t4 in range(4):
                            tp = p.emit("tensor", lambda t, sp=sp, t4=t4: t.transpose(
                                ttps[sp][:, t4 * 128:(t4 + 1) * 128], pstg[:, sp, t4 * 128:(t4 + 1) * 128],
                                identf[:]), inc=(sPE if t4 == 3 else None))
                        pstg_free[sp] = tp
                        p.wait("vector", tp)
                        if atok_free is not None:
                            p.wait("vector", atok_free[0])
                            p.wait("vector", atok_free[1])
                        ev2 = p.emit("vector", lambda v, sp=sp, dpt=dpt: v.tensor_copy(
                            atok[:, :, dpt * 128:(dpt + 1) * 128],
                            ttps[sp].ap().rearrange("p (t m) -> p t m", t=4)), inc=sDV)
                        ttps_free[sp] = ev2
                    aos_free[ap_] = aos_last_mm
                    atok_ready = ev2
                    last_tmp = None
                    last_sq = None
                    x2st2 = None
                    for t4 in range(4):
                        lt = tg * 4 + t4
                        xp = t4 % 2
                        p.wait("sync", xre_free[xp])
                        xl = p.emit("sync", lambda s, lt=lt, xp=xp: s.dma_start(
                            out=xre[:, xp, :], in_=x_in[HALO + lt * 128: HALO + (lt + 1) * 128, :]),
                            inc=dX[xp], inc_n=16)
                        asl = atok[:, t4, :]
                        sl = stats3[:, (t4 % 2) * 16:(t4 % 2) * 16 + 16]
                        p.wait("vector", last_tmp)
                        ap_tok, sq_tok, r2t = ln_chain(atok_ready, asl, sl, lntmp[:],
                                                       slot_prev=st3_prev[t4 % 2])
                        st3_prev[t4 % 2] = r2t
                        last_tmp = ap_tok
                        last_sq = sq_tok
                        p.wait("vector", xl)
                        p.wait("vector", x2b_free[xp])
                        x2t = p.emit("vector", lambda v, xp=xp: v.tensor_tensor(
                            x2b[:, xp, :], lntmp[:], xre[:, xp, :], ALU.add), inc=sDV)
                        xre_free[xp] = x2t
                        p.wait("sync", x2t)
                        x2st = p.emit("sync", lambda s, lt=lt, xp=xp: s.dma_start(
                            out=x2_sc.ap()[lt * 128:(lt + 1) * 128, :], in_=x2b[:, xp, :]),
                            inc=dS[xp], inc_n=16)
                        sl3 = stats3[:, 32 + (t4 % 2) * 16: 32 + (t4 % 2) * 16 + 16]
                        p.wait("vector", x2l_free[xp])
                        ap3, sq3, r3t = ln_chain(x2t, x2b[:, xp, :], sl3, x2l[:, xp, :],
                                                 slot_prev=st3b_prev[t4 % 2])
                        st3b_prev[t4 % 2] = r3t
                        x2b_free[xp] = (ap3, sq3, x2st)
                        for half in range(2):
                            p.wait("tensor", tbps_free[half])
                            p.wait("tensor", ap3)
                            tp2 = None
                            for j in range(8):
                                dt = half * 8 + j
                                tp2 = p.emit("tensor", lambda t, xp=xp, half=half, j=j, dt=dt: t.transpose(
                                    tbps[half][:, j * 128:(j + 1) * 128], x2l[:, xp, dt * 128:(dt + 1) * 128],
                                    identb[:]), inc=(sPE if j == 7 else None))
                            p.wait("vector", tp2)
                            p.wait("vector", x2tstg_free[xp])
                            evt2 = p.emit("vector", lambda v, half=half, xp=xp: v.tensor_copy(
                                x2tstg[:, xp, half * 1024:(half + 1) * 1024], tbps[half].ap()), inc=sDV)
                            tbps_free[half] = evt2
                            p.wait("sync", evt2)
                            x2st2 = p.emit("sync", lambda s, lt=lt, half=half, xp=xp: s.dma_start(
                                out=x2ln_sc.ap()[half * 8:(half + 1) * 8, :, lt * 128:(lt + 1) * 128]
                                    .rearrange("dt p m -> p dt m"),
                                in_=x2tstg[:, xp, half * 1024:(half + 1) * 1024]
                                    .rearrange("p (dt m) -> p dt m", dt=8)), inc=dS[xp], inc_n=16)
                            x2tstg_free[xp] = x2st2
                        x2l_free[xp] = tp2
                    atok_free = (last_tmp, last_sq)

                if "x2" in DEBUG_DUMPS:
                    p.barrier(ALL_SEMS)
                    p.emit("sync", lambda s: s.dma_start(out=dbg["dbg_x2"], in_=x2_sc.ap()), inc=dO, inc_n=16)

        p.barrier(ALL_SEMS)

        p.barrier(ALL_SEMS)

        if EMIT_P3:
            with ExitStack() as P3:
                e4 = P3.enter_context
                x2T = e4(nc.sbuf_tensor("x2T", [128, 1, NDT, 512], BF16))
                gT = e4(nc.sbuf_tensor("gT", [128, NFT, 512], BF16))
                l1b = e4(nc.sbuf_tensor("l1b", [128, 2, NDT, 128], BF16))
                l2b = e4(nc.sbuf_tensor("l2b", [128, 2, 32, 128], BF16))
                o2T = e4(nc.sbuf_tensor("o2T", [128, NDT, 512], F32))
                o2tok = e4(nc.sbuf_tensor("o2tok", [128, 2048], F32))
                lntmp4 = e4(nc.sbuf_tensor("lntmp4", [128, 2048], F32))
                x2re = e4(nc.sbuf_tensor("x2re", [128, 2, 2048], F32))
                stats4 = e4(nc.sbuf_tensor("stats4", [128, 32], F32))
                hps = [e4(nc.psum_tensor("hps0", [128, 512], F32)),
                       e4(nc.psum_tensor("hps1", [128, 512], F32))]
                o2ps = [e4(nc.psum_tensor("o2ps0", [128, 512], F32)),
                        e4(nc.psum_tensor("o2ps1", [128, 512], F32))]
                t4ps = [e4(nc.psum_tensor("t4ps0", [128, 512], F32)),
                        e4(nc.psum_tensor("t4ps1", [128, 512], F32))]

                x2T_free = [None, None]
                l1b_free = [None, None]
                l2b_free = [None, None]
                hps_free = [None, None]
                o2ps_free = [None, None]
                t4ps_free = [None, None]
                gT_free_pe = None
                o2T_free = None
                o2tok_free = None
                st4_prev = [None, None]
                x2re_free = [None, None]
                obuf_free = [None, None]
                lntmp4_free = None

                for tg in range(4):
                    tp_ = 0
                    p.wait("sync", x2T_free[tp_])
                    xtl = p.emit("sync", lambda s, tg=tg, tp_=tp_: s.dma_start(
                        out=x2T[:, tp_],
                        in_=x2ln_sc.ap()[:, :, tg * 512:(tg + 1) * 512].rearrange("dt p m -> p dt m")),
                        inc=dH[tp_], inc_n=16)
                    p.wait("scalar", gT_free_pe)
                    hmm = None
                    for ft in range(NFT):
                        lp = ft % 2
                        p.wait("gpsimd", l1b_free[lp])
                        wl = p.emit("gpsimd", lambda g, ft=ft, lp=lp: g.dma_start(
                            out=l1b[:, lp],
                            in_=w1qT_sc.ap()[ft].rearrange("dt p m -> p dt m")), inc=dF[lp], inc_n=16)
                        ps_i = ft % 2
                        ps = hps[ps_i]
                        p.wait("tensor", hps_free[ps_i])
                        p.wait("tensor", wl)
                        p.wait("tensor", xtl)
                        for dt in range(NDT):
                            hmm = p.emit("tensor", lambda t, ps=ps, lp=lp, dt=dt, tp_=tp_: t.matmul(
                                ps[:], l1b[:, lp, dt, :], x2T[:, tp_, dt, :],
                                start=(dt == 0), stop=(dt == NDT - 1)),
                                inc=(sPE if dt == NDT - 1 else None))
                        l1b_free[lp] = hmm
                        p.wait("scalar", hmm)
                        gl = p.emit("scalar", lambda s, ps=ps, ft=ft: s.activation(
                            gT[:, ft, :], ps[:], AF.Gelu,
                            bias=b1_sb[:, ft:ft + 1], scale=consts[:, C_HALF:C_HALF + 1]), inc=sAC)
                        hps_free[ps_i] = gl
                    x2T_free[tp_] = hmm
                    gelu_done = p.tok(sAC)
                    p.wait("scalar", o2T_free)
                    omm = None
                    ol = None
                    for dpt in range(NDT):
                        for hlf in range(2):
                            lp = (dpt * 2 + hlf) % 2
                            p.wait("gpsimd", l2b_free[lp])
                            wl2 = p.emit("gpsimd", lambda g, dpt=dpt, hlf=hlf, lp=lp: g.dma_start(
                                out=l2b[:, lp],
                                in_=w2qT_sc.ap()[dpt, hlf * 32:(hlf + 1) * 32].rearrange("ft p m -> p ft m")),
                                inc=dF[lp], inc_n=16)
                            ps_i = dpt % 2
                            ps = o2ps[ps_i]
                            if hlf == 0:
                                p.wait("tensor", o2ps_free[ps_i])
                            p.wait("tensor", wl2)
                            p.wait("tensor", gelu_done)
                            for fi in range(32):
                                ft = hlf * 32 + fi
                                omm = p.emit("tensor", lambda t, ps=ps, lp=lp, fi=fi, ft=ft: t.matmul(
                                    ps[:], l2b[:, lp, fi, :], gT[:, ft, :],
                                    start=(ft == 0), stop=(ft == NFT - 1)),
                                    inc=(sPE if fi == 31 else None))
                            l2b_free[lp] = omm
                        p.wait("scalar", omm)
                        ol = p.emit("scalar", lambda s, ps=ps, dpt=dpt: s.activation(
                            o2T[:, dpt, :], ps[:], AF.Identity,
                            bias=b2_sb[:, dpt:dpt + 1], scale=consts[:, C_HALF:C_HALF + 1]), inc=sAC)
                        o2ps_free[dpt % 2] = ol
                    gT_free_pe = omm
                    o2_done = ol
                    if "ffn" in DEBUG_DUMPS and tg == 0:
                        p.wait("sync", p.tok(sAC))
                        p.emit("sync", lambda s: s.dma_start(out=dbg["dbg_g"], in_=gT[:]), inc=dO, inc_n=16)
                        p.emit("sync", lambda s: s.dma_start(out=dbg["dbg_o2"], in_=o2T[:]), inc=dO, inc_n=16)
                    last_tpq = None
                    for t4 in range(4):
                        lt = tg * 4 + t4
                        xp = t4 % 2
                        o2tok_ready = None
                        for q4 in range(4):
                            p.wait("tensor", t4ps_free[xp])
                            p.wait("tensor", o2_done)
                            tpq = None
                            for j in range(4):
                                dt = q4 * 4 + j
                                tpq = p.emit("tensor", lambda t, xp=xp, dt=dt, j=j, t4=t4: t.transpose(
                                    t4ps[xp][:, j * 128:(j + 1) * 128],
                                    o2T[:, dt, t4 * 128:(t4 + 1) * 128], identf[:]),
                                    inc=(sPE if j == 3 else None))
                            last_tpq = tpq
                            p.wait("vector", tpq)
                            p.wait("vector", o2tok_free)
                            evq4 = p.emit("vector", lambda v, q4=q4, xp=xp: v.tensor_copy(
                                o2tok[:, q4 * 512:(q4 + 1) * 512], t4ps[xp].ap()), inc=sDV)
                            t4ps_free[xp] = evq4
                            o2tok_ready = evq4
                        p.wait("sync", x2re_free[xp])
                        xrl = p.emit("sync", lambda s, lt=lt, xp=xp: s.dma_start(
                            out=x2re[:, xp, :], in_=x2_sc.ap()[lt * 128:(lt + 1) * 128, :]),
                            inc=dX[xp], inc_n=16)
                        p.wait("vector", lntmp4_free)
                        ap4, sq4, r4t = ln_chain(o2tok_ready, o2tok[:], stats4[:, xp * 16:(xp + 1) * 16],
                                                 lntmp4[:], slot_prev=st4_prev[xp])
                        st4_prev[xp] = r4t
                        o2tok_free = sq4
                        p.wait("vector", ap4)
                        p.wait("vector", xrl)
                        p.wait("vector", obuf_free[xp])
                        fo = p.emit("vector", lambda v, xp=xp: v.tensor_tensor(
                            x2re[:, xp, :], lntmp4[:], x2re[:, xp, :], ALU.add), inc=sDV)
                        lntmp4_free = fo
                        p.wait("sync", fo)
                        ost = p.emit("sync", lambda s, lt=lt, xp=xp: s.dma_start(
                            out=out[lt * 128:(lt + 1) * 128, :], in_=x2re[:, xp, :]), inc=dO, inc_n=16)
                        x2re_free[xp] = ost
                        obuf_free[xp] = ost
                    o2T_free = last_tpq

        if "ffn" in DEBUG_DUMPS:
            p.barrier(ALL_SEMS)
            p.emit("sync", lambda s: s.dma_start(out=dbg["dbg_x2ln"], in_=x2ln_sc.ap()), inc=dO, inc_n=16)

        p.wait("sync", p.tok(dO))

        with nc.Block() as block:
            @block.sync
            def _(s):
                p.replay("sync", s)

            @block.tensor
            def _(t):
                p.replay("tensor", t)

            @block.vector
            def _(v):
                p.replay("vector", v)

            @block.scalar
            def _(s):
                p.replay("scalar", s)

            @block.gpsimd
            def _(g):
                p.replay("gpsimd", g)

    return nc, dbg



def _host_inputs(L0_attn_x, L0_attn_w_qkv, L0_attn_w_out, L0_attn_q_gamma,
                 L0_attn_k_gamma, L0_ffn_w1, L0_ffn_b1, L0_ffn_w2, L0_ffn_b2):
    x = np.asarray(L0_attn_x, dtype=np.float32)
    inv_freq = 1.0 / THETA ** (np.arange(0, HD, 2, dtype=np.float32) / HD)

    ident_b = np.eye(128, dtype=np.float32).astype(ml_dtypes.bfloat16)
    ident_f = np.eye(128, dtype=np.float32)
    ones_f = np.ones((128, 128), np.float32)
    ones_b = np.ones((128, 128), np.float32).astype(ml_dtypes.bfloat16)
    oo128 = np.full((128, 128), 1.0 / 128.0, dtype=np.float32).astype(ml_dtypes.bfloat16)
    protm = np.zeros((128, 128), np.float32)
    for j in range(64):
        protm[j + 64, j] = -1.0
    for j in range(64, 128):
        protm[j - 64, j] = 1.0
    protm = protm.astype(ml_dtypes.bfloat16)
    consts = np.zeros((128, 16), np.float32)
    consts[:, C_EPSRMS] = 1e-6
    consts[:, C_EPSLN] = 1e-5
    consts[:, C_NHALF] = -0.5
    consts[:, C_NEG1] = -1.0
    consts[:, C_HALF] = 0.5
    consts[:, C_INV2048] = 1.0 / 2048.0

    wqkv = np.ascontiguousarray(np.asarray(L0_attn_w_qkv, np.float32))
    wout = np.ascontiguousarray(np.asarray(L0_attn_w_out, np.float32))
    w1 = np.ascontiguousarray(np.asarray(L0_ffn_w1, np.float32))
    w2 = np.ascontiguousarray(np.asarray(L0_ffn_w2, np.float32))
    b1 = np.ascontiguousarray(np.asarray(L0_ffn_b1, np.float32).reshape(NFT, 128))
    b2 = np.ascontiguousarray(np.asarray(L0_ffn_b2, np.float32).reshape(NDT, 128))
    qga = np.asarray(L0_attn_q_gamma, np.float32)
    kga = np.asarray(L0_attn_k_gamma, np.float32)
    gam4 = np.stack([qga, np.concatenate([qga[64:], qga[:64]]),
                     kga, np.concatenate([kga[64:], kga[:64]])], axis=1)
    gam4 = np.ascontiguousarray(gam4)

    maps = []
    for c in range(8):
        beta, i = divmod(c, 4)
        a = TOK * i
        lo = a - HALO
        xs = np.zeros((LTOK, D), np.float32)
        src_lo = max(lo, 0)
        xs[src_lo - lo:, :] = x[beta, src_lo:a + TOK, :]
        pos = np.arange(lo, a + TOK, dtype=np.float32)
        pos = np.maximum(pos, 0.0)
        ang = pos[None, :] * np.concatenate([inv_freq, inv_freq])[:, None]
        cos_t = np.cos(ang).astype(np.float32)
        sin_t = np.sin(ang).astype(np.float32)
        mkT = np.full((4, WTILES, 128, 512), NEG, np.float32)
        qq = np.arange(512)
        for g in range(4):
            qglob = a + 512 * g + qq
            iq = qq // 128
            for jj in range(WTILES):
                kglob = lo + 128 * (4 * g + jj) + np.arange(128)
                m = (qglob[None, :] >= kglob[:, None]) & \
                    (qglob[None, :] - kglob[:, None] <= WINDOW) & \
                    (kglob[:, None] >= 0) & \
                    (iq[None, :] <= jj) & (jj <= iq[None, :] + 32)
                mkT[g, jj] = np.where(m, 0.0, NEG)
        maps.append({
            "x_pad": xs, "w_qkv": wqkv, "w_out": wout, "gam4": gam4,
            "w1": w1, "b1": b1, "w2": w2, "b2": b2,
            "cos_t": cos_t, "sin_t": sin_t,
            "masksT": mkT.astype(ml_dtypes.bfloat16),
            "identb": ident_b, "identf": ident_f, "onesf": ones_f, "onesb": ones_b,
            "oo128": oo128, "prot": protm, "consts": consts,
        })
    return maps


_RES_HOOK = [None]


def kernel(**inputs):
    maps = _host_inputs(**inputs)
    nc, dbg = build_program()
    res = run_bass_kernel_spmd(nc, maps, core_ids=list(range(8)))
    _RES_HOOK[0] = res
    full = np.zeros((B, N, D), np.float32)
    for c in range(8):
        beta, i = divmod(c, 4)
        full[beta, TOK * i:TOK * (i + 1), :] = res.results[c]["out"]
    return full


# version 15
# speedup vs baseline: 1.1239x; 1.1239x over previous
import math
from contextlib import ExitStack

import numpy as np
import ml_dtypes

import concourse.bass as bass
import concourse.mybir as mybir
from concourse.bass_utils import run_bass_kernel_spmd

F32 = mybir.dt.float32
BF16 = mybir.dt.bfloat16
AF = mybir.ActivationFunctionType
ALU = mybir.AluOpType
AX = mybir.AxisListType

B = 2
N = 8192
D = 2048
NH = 16
HD = 128
DFF = 8192
WINDOW = 4096
THETA = 10000.0
NEG = -1e30

TOK = 2048
HALO = 4096
LTOK = HALO + TOK
NQT = TOK // 128
WTILES = 36
NCH = 9
NDT = D // 128
NFT = DFF // 128
NST = LTOK // 2048

C_EPSRMS, C_EPSLN, C_NHALF, C_NEG1, C_HALF, C_INV2048, C_T07W1, C_T07W2 = range(8)

LN_PARANOID = 0
EMIT_P2 = True
EMIT_P2C = True
EMIT_QUANT = True
EMIT_P3 = True
DEBUG_DUMPS = ()

ENGINES = ("sync", "tensor", "vector", "scalar", "gpsimd")


class Prog:
    def __init__(self, nc):
        self.nc = nc
        self.events = {e: [] for e in ENGINES}
        self.ctr = {}
        self.sems = {}
        self.last_wait = {e: {} for e in ENGINES}

    def sem(self, ctx, name):
        h = ctx.enter_context(self.nc.semaphore(name))
        self.sems[name] = h
        self.ctr[name] = 0
        return name

    def wait(self, eng, token):
        if token is None:
            return
        if isinstance(token, (tuple, list)) and token and isinstance(token[0], (tuple, list, type(None))):
            for t in token:
                self.wait(eng, t)
            return
        sem, val = token
        if val <= 0:
            return
        lw = self.last_wait[eng]
        if lw.get(sem, 0) >= val:
            return
        lw[sem] = val
        self.events[eng].append(("wait", self.sems[sem], val))

    def emit(self, eng, fn, inc=None, inc_n=1):
        tok = None
        if inc is not None:
            self.ctr[inc] += inc_n
            tok = (inc, self.ctr[inc])
            self.events[eng].append(("op", fn, self.sems[inc], inc_n))
        else:
            self.events[eng].append(("op", fn, None, 0))
        return tok

    def tok(self, sem):
        return (sem, self.ctr[sem])

    def barrier(self, sems):
        toks = [(s, self.ctr[s]) for s in sems]
        for eng in ENGINES:
            for t in toks:
                self.wait(eng, t)

    def replay(self, eng, engobj):
        for ev in self.events[eng]:
            if ev[0] == "wait":
                engobj.wait_ge(ev[1], ev[2])
            else:
                inst = ev[1](engobj)
                if ev[2] is not None:
                    inst.then_inc(ev[2], ev[3])


def build_program():
    nc = bass.Bass("TRN2", target_bir_lowering=False)

    x_in = nc.dram_tensor("x_pad", [LTOK, D], F32, kind="ExternalInput").ap()
    w_qkv = nc.dram_tensor("w_qkv", [D, 3 * D], F32, kind="ExternalInput").ap()
    w_out = nc.dram_tensor("w_out", [D, D], F32, kind="ExternalInput").ap()
    gam_in = nc.dram_tensor("gam4", [128, 4], F32, kind="ExternalInput").ap()
    w1 = nc.dram_tensor("w1", [DFF, D], F32, kind="ExternalInput").ap()
    b1_in = nc.dram_tensor("b1", [NFT, 128], F32, kind="ExternalInput").ap()
    w2 = nc.dram_tensor("w2", [D, DFF], F32, kind="ExternalInput").ap()
    b2_in = nc.dram_tensor("b2", [NDT, 128], F32, kind="ExternalInput").ap()
    cos_in = nc.dram_tensor("cos_t", [128, LTOK], F32, kind="ExternalInput").ap()
    sin_in = nc.dram_tensor("sin_t", [128, LTOK], F32, kind="ExternalInput").ap()
    masksT_in = nc.dram_tensor("masksT", [4, WTILES, 128, 512], BF16, kind="ExternalInput").ap()
    identb_in = nc.dram_tensor("identb", [128, 128], BF16, kind="ExternalInput").ap()
    identf_in = nc.dram_tensor("identf", [128, 128], F32, kind="ExternalInput").ap()
    onesf_in = nc.dram_tensor("onesf", [128, 128], F32, kind="ExternalInput").ap()
    onesb_in = nc.dram_tensor("onesb", [128, 128], BF16, kind="ExternalInput").ap()
    oo128_in = nc.dram_tensor("oo128", [128, 128], BF16, kind="ExternalInput").ap()
    prot_in = nc.dram_tensor("prot", [128, 128], BF16, kind="ExternalInput").ap()
    consts_in = nc.dram_tensor("consts", [128, 16], F32, kind="ExternalInput").ap()
    out = nc.dram_tensor("out", [TOK, D], F32, kind="ExternalOutput").ap()

    q_sc = nc.dram_tensor("q_sc", [NH, 128, TOK], BF16)
    k_sc = nc.dram_tensor("k_sc", [NH, 128, LTOK], BF16)
    v_sc = nc.dram_tensor("v_sc", [NH, LTOK, 128], BF16)
    ao_sc = nc.dram_tensor("ao_sc", [NH, 128, TOK], BF16)
    x2_sc = nc.dram_tensor("x2_sc", [TOK, D], F32)
    x2ln_sc = nc.dram_tensor("x2ln_sc", [NDT, 128, TOK], BF16)
    w1qT_sc = nc.dram_tensor("w1qT_sc", [NFT, NDT, 128, 128], BF16)
    w2qT_sc = nc.dram_tensor("w2qT_sc", [NDT, NFT, 128, 128], BF16)

    dbg = {}
    if "xlnT" in DEBUG_DUMPS:
        dbg["dbg_xlnT"] = nc.dram_tensor("dbg_xlnT", [128, NDT, 2048], BF16, kind="ExternalOutput").ap()
    if "qkv" in DEBUG_DUMPS:
        dbg["dbg_q"] = nc.dram_tensor("dbg_q", [NH, 128, TOK], BF16, kind="ExternalOutput").ap()
        dbg["dbg_k"] = nc.dram_tensor("dbg_k", [NH, 128, LTOK], BF16, kind="ExternalOutput").ap()
        dbg["dbg_v"] = nc.dram_tensor("dbg_v", [NH, LTOK, 128], BF16, kind="ExternalOutput").ap()
    if "attn" in DEBUG_DUMPS:
        dbg["dbg_ao"] = nc.dram_tensor("dbg_ao", [NH, 128, TOK], BF16, kind="ExternalOutput").ap()
        dbg["dbg_l"] = nc.dram_tensor("dbg_l", [1, 512], F32, kind="ExternalOutput").ap()
        dbg["dbg_g"] = nc.dram_tensor("dbg_gx", [1, 512], F32, kind="ExternalOutput").ap()
        dbg["dbg_mb"] = nc.dram_tensor("dbg_mb", [128, 1], F32, kind="ExternalOutput").ap()
        dbg["dbg_pt"] = nc.dram_tensor("dbg_pt", [128, WTILES, 512], BF16, kind="ExternalOutput").ap()
    if "x2" in DEBUG_DUMPS:
        dbg["dbg_x2"] = nc.dram_tensor("dbg_x2", [TOK, D], F32, kind="ExternalOutput").ap()
    if "ffn" in DEBUG_DUMPS:
        dbg["dbg_x2ln"] = nc.dram_tensor("dbg_x2ln", [NDT, 128, TOK], BF16, kind="ExternalOutput").ap()
        dbg["dbg_g"] = nc.dram_tensor("dbg_g", [128, NFT, 512], BF16, kind="ExternalOutput").ap()
        dbg["dbg_o2"] = nc.dram_tensor("dbg_o2", [128, NDT, 512], F32, kind="ExternalOutput").ap()
    if "wq" in DEBUG_DUMPS:
        dbg["dbg_w1q"] = nc.dram_tensor("dbg_w1q", [NFT, NDT, 128, 128], BF16, kind="ExternalOutput").ap()
        dbg["dbg_thr"] = nc.dram_tensor("dbg_thr", [128, 4], F32, kind="ExternalOutput").ap()
        dbg["dbg_w2q"] = nc.dram_tensor("dbg_w2q", [NDT, NFT, 128, 128], BF16, kind="ExternalOutput").ap()

    with ExitStack() as root:
        er = root.enter_context

        identb = er(nc.sbuf_tensor("identb_s", [128, 128], BF16))
        identf = er(nc.sbuf_tensor("identf_s", [128, 128], F32))
        onesf = er(nc.sbuf_tensor("onesf_s", [128, 128], F32))
        onesb = er(nc.sbuf_tensor("onesb_s", [128, 128], BF16))
        mbias = er(nc.sbuf_tensor("mbias", [128, 1], F32))
        oo128 = er(nc.sbuf_tensor("oo128_s", [128, 128], BF16))
        prot = er(nc.sbuf_tensor("prot_s", [128, 128], BF16))
        consts = er(nc.sbuf_tensor("consts_s", [128, 16], F32))
        gam = er(nc.sbuf_tensor("gam_s", [128, 4], F32))
        b1_sb = er(nc.sbuf_tensor("b1_sb", [128, NFT], F32))
        b2_sb = er(nc.sbuf_tensor("b2_sb", [128, NDT], F32))
        thr = er(nc.sbuf_tensor("thr", [128, 4], F32))
        junk = er(nc.sbuf_tensor("junk", [128, 2048], F32))
        acc = er(nc.sbuf_tensor("acc", [128, 136], F32))

        p = Prog(nc)
        sPE = p.sem(root, "sPE")
        sDV = p.sem(root, "sDV")
        sAC = p.sem(root, "sAC")
        dInit = p.sem(root, "dInit")
        dX = [p.sem(root, "dX0"), p.sem(root, "dX1")]
        dW = [p.sem(root, "dW0"), p.sem(root, "dW1")]
        dS = [p.sem(root, "dS0"), p.sem(root, "dS1")]
        dH = [p.sem(root, "dH0"), p.sem(root, "dH1")]
        dM = [p.sem(root, "dM0"), p.sem(root, "dM1")]
        dQ = [p.sem(root, "dQ0"), p.sem(root, "dQ1")]
        dQS = [p.sem(root, "dQS0"), p.sem(root, "dQS1")]
        dF = [p.sem(root, "dF0"), p.sem(root, "dF1")]
        dO = p.sem(root, "dO")
        ALL_SEMS = [sPE, sDV, sAC, dInit, dX[0], dX[1], dW[0], dW[1], dS[0], dS[1],
                    dH[0], dH[1], dM[0], dM[1], dQ[0], dQ[1], dQS[0], dQS[1],
                    dF[0], dF[1], dO]

        for dst, src in ((identb, identb_in), (identf, identf_in), (onesf, onesf_in),
                         (onesb, onesb_in), (oo128, oo128_in), (prot, prot_in),
                         (consts, consts_in), (gam, gam_in)):
            p.emit("sync", lambda s, d=dst, i=src: s.dma_start(out=d[:], in_=i), inc=dInit, inc_n=16)

        btmp = er(nc.sbuf_tensor("btmp", [128, 128], F32))
        bps_scope = ExitStack()
        bps = bps_scope.enter_context(nc.psum_tensor("bps", [128, 128], F32))
        p.emit("sync", lambda s: s.dma_start(out=btmp[0:32, :], in_=b1_in[0:32]), inc=dInit, inc_n=16)
        p.emit("sync", lambda s: s.dma_start(out=btmp[64:96, :], in_=b1_in[32:64]), inc=dInit, inc_n=16)
        p.wait("tensor", p.tok(dInit))
        tp = p.emit("tensor", lambda t: t.transpose(bps[:], btmp[:], identf[:]), inc=sPE)
        p.wait("vector", tp)
        p.emit("vector", lambda v: v.tensor_copy(b1_sb[:, 0:32], bps[:, 0:32]), inc=sDV)
        ev = p.emit("vector", lambda v: v.tensor_copy(b1_sb[:, 32:64], bps[:, 64:96]), inc=sDV)
        p.wait("sync", ev)
        p.emit("sync", lambda s: s.dma_start(out=btmp[0:NDT, :], in_=b2_in[:]), inc=dInit, inc_n=16)
        p.wait("tensor", p.tok(dInit))
        p.wait("tensor", ev)
        tp2 = p.emit("tensor", lambda t: t.transpose(bps[:], btmp[:], identf[:]), inc=sPE)
        p.wait("vector", tp2)
        p.emit("vector", lambda v: v.tensor_copy(b2_sb[:], bps[:, 0:NDT]), inc=sDV)
        bps_scope.close()
        tok_init = p.tok(dInit)

        p.wait("vector", tok_init)
        gscale = p.emit("vector", lambda v: v.tensor_scalar(
            gam[:, 0:2], gam[:, 0:2], 1.0 / math.sqrt(HD), None, ALU.mult), inc=sDV)

        with nc.sbuf_tensor("mtmp", [128, 4], F32) as mtmp, \
             nc.psum_tensor("mps", [128, 128], F32) as mps:
            p.wait("scalar", gscale)
            ga = p.emit("scalar", lambda s: s.activation(mtmp[:], gam[:], AF.Abs), inc=sAC)
            p.wait("tensor", ga)
            mt1 = p.emit("tensor", lambda t: t.transpose(mps[0:4, 0:128], mtmp[:, 0:4], identf[:]), inc=sPE)
            p.wait("vector", mt1)
            mr = p.emit("vector", lambda v: v.reduce_max(mtmp[0:4, 0:1], mps[0:4, 0:128], axis=AX.X), inc=sDV)
            p.wait("tensor", mr)
            mt2 = p.emit("tensor", lambda t: t.transpose(mps[0:4, 0:128], mtmp[:, 0:4], identf[:]), inc=sPE)
            p.wait("vector", mt2)
            mc2 = p.emit("vector", lambda v: v.tensor_copy(mtmp[0:1, 0:4], mps[0:1, 0:4]), inc=sDV)
            p.wait("vector", mc2)
            mm1 = p.emit("vector", lambda v: v.tensor_tensor(
                mtmp[0:1, 0:1], mtmp[0:1, 0:1], mtmp[0:1, 2:3], ALU.mult), inc=sDV)
            p.wait("vector", mm1)
            mm2 = p.emit("vector", lambda v: v.tensor_scalar(
                mtmp[0:1, 1:2], mtmp[0:1, 0:1], -128.0, None, ALU.mult), inc=sDV)
            p.wait("tensor", mm2)
            mb = p.emit("tensor", lambda t: t.matmul(mps[:, 0:1], onesf[0:1, :], mtmp[0:1, 1:2],
                                                     start=True, stop=True), inc=sPE)
            p.wait("vector", mb)
            p.emit("vector", lambda v: v.tensor_copy(mbias[:], mps[:, 0:1]), inc=sDV)

        def ln_chain(eng_xtok, xap, stats_ap, outap, slot_prev=None):
            p.wait("vector", slot_prev)
            ssum = stats_ap[:, 0:1]
            smean = stats_ap[:, 4:5]
            svar = stats_ap[:, 8:9]
            srstd = stats_ap[:, 12:13]
            p.wait("vector", eng_xtok)
            red_tok = p.emit("vector", lambda v: v.reduce_sum(ssum, xap, axis=AX.X), inc=sDV)
            p.wait("vector", red_tok)
            mean_tok = p.emit("vector", lambda v: v.tensor_scalar(
                smean, ssum, 1.0 / 2048.0, None, ALU.mult), inc=sDV)
            p.wait("vector", mean_tok)
            m2_tok = p.emit("vector", lambda v: v.tensor_tensor(svar, smean, smean, ALU.mult), inc=sDV)
            p.wait("scalar", eng_xtok)
            p.wait("scalar", mean_tok)
            sq_tok = p.emit("scalar", lambda s: s.activation(junk[:], xap, AF.Square, accum_out=ssum), inc=sAC)
            p.wait("vector", sq_tok)
            p.wait("vector", m2_tok)
            var_tok = p.emit("vector", lambda v: v.scalar_tensor_tensor(
                svar, ssum, 1.0 / 2048.0, svar, ALU.mult, ALU.subtract), inc=sDV)
            p.wait("scalar", var_tok)
            ln_tok = p.emit("scalar", lambda s: s.activation(svar, svar, AF.Ln, bias=consts[:, C_EPSLN:C_EPSLN + 1]), inc=sAC)
            p.wait("scalar", ln_tok)
            r_tok = p.emit("scalar", lambda s: s.activation(srstd, svar, AF.Exp, scale=consts[:, C_NHALF:C_NHALF + 1]), inc=sAC)
            p.wait("vector", r_tok)
            p.wait("vector", var_tok)
            ap_tok = p.emit("vector", lambda v: v.tensor_scalar(
                outap, xap, smean, srstd, ALU.subtract, ALU.mult), inc=sDV)
            return ap_tok, sq_tok, r_tok

        with ExitStack() as P1:
            e1 = P1.enter_context
            Cq = e1(nc.sbuf_tensor("Cq", [128, TOK], BF16))
            Sq = e1(nc.sbuf_tensor("Sq", [128, TOK], BF16))
            Ck = e1(nc.sbuf_tensor("Ck", [128, LTOK], BF16))
            Sk = e1(nc.sbuf_tensor("Sk", [128, LTOK], BF16))
            ropebuf = e1(nc.sbuf_tensor("ropebuf", [128, 2, 2, 512], F32))
            xsb = e1(nc.sbuf_tensor("xsb", [128, 2, 2048], F32))
            xln = e1(nc.sbuf_tensor("xln", [128, 2, 2048], BF16))
            stats = e1(nc.sbuf_tensor("stats", [128, 32], F32))
            xlnT = e1(nc.sbuf_tensor("xlnT", [128, NDT, 2048], BF16))
            wbuf = e1(nc.sbuf_tensor("wbuf", [128, 2, NDT, 512], BF16))
            sqb = e1(nc.sbuf_tensor("sqb", [128, 2, 512], BF16))
            rstdb = e1(nc.sbuf_tensor("rstdb", [128, 2, 512], F32))
            qnb = e1(nc.sbuf_tensor("qnb", [128, 2, 512], BF16))
            ropec = e1(nc.sbuf_tensor("ropec", [128, 2, 512], BF16))
            roted = e1(nc.sbuf_tensor("roted", [128, 2, 512], BF16))
            qkst = e1(nc.sbuf_tensor("qkst", [128, 2, 512], BF16))
            vst = e1(nc.sbuf_tensor("vst", [128, 2, 512], BF16))
            trps = [e1(nc.psum_tensor("trps0", [128, 1024], BF16)),
                    e1(nc.psum_tensor("trps1", [128, 1024], BF16))]
            mmps = [e1(nc.psum_tensor("mmps0", [128, 512], F32)),
                    e1(nc.psum_tensor("mmps1", [128, 512], F32))]
            ssps = e1(nc.psum_tensor("ssps", [128, 512], F32))
            rotps = [e1(nc.psum_tensor("rotps0", [128, 512], F32)),
                     e1(nc.psum_tensor("rotps1", [128, 512], F32))]

            rope_use = [None, None]
            for ci in range(LTOK // 512 + TOK // 512):
                par = ci % 2
                if ci < LTOK // 512:
                    srcs = (cos_in[:, ci * 512:(ci + 1) * 512], sin_in[:, ci * 512:(ci + 1) * 512])
                    dsts = (Ck[:, ci * 512:(ci + 1) * 512], Sk[:, ci * 512:(ci + 1) * 512])
                    gcols, gdep = (2, 3), tok_init
                else:
                    cj = ci - LTOK // 512
                    srcs = (cos_in[:, HALO + cj * 512:HALO + (cj + 1) * 512],
                            sin_in[:, HALO + cj * 512:HALO + (cj + 1) * 512])
                    dsts = (Cq[:, cj * 512:(cj + 1) * 512], Sq[:, cj * 512:(cj + 1) * 512])
                    gcols, gdep = (0, 1), gscale
                p.wait("sync", rope_use[par])
                p.emit("sync", lambda s, a=srcs[0], pr=par: s.dma_start(out=ropebuf[:, pr, 0, :], in_=a), inc=dX[par], inc_n=16)
                tl = p.emit("sync", lambda s, a=srcs[1], pr=par: s.dma_start(out=ropebuf[:, pr, 1, :], in_=a), inc=dX[par], inc_n=16)
                p.wait("vector", tl)
                p.wait("vector", gdep)
                p.emit("vector", lambda v, d=dsts[0], pr=par, g=gcols[0]: v.tensor_scalar(
                    d, ropebuf[:, pr, 0, :], gam[:, g:g + 1], None, ALU.mult), inc=sDV)
                u = p.emit("vector", lambda v, d=dsts[1], pr=par, g=gcols[1]: v.tensor_scalar(
                    d, ropebuf[:, pr, 1, :], gam[:, g:g + 1], None, ALU.mult), inc=sDV)
                rope_use[par] = u
            rope_done = p.tok(sDV)

            xsb_free = [None, None]
            stats_prev = [None, None]
            trps_free = [None, None]
            mmps_free = [None, None]
            ssps_free = None
            rotps_free = [None, None]
            sqb_free = [None, None]
            rstdb_free = [None, None]
            qnb_free = [None, None]
            ropec_free = [None, None]
            roted_free = [None, None]
            qkst_free = [None, None]
            vst_free = [None, None]
            wbuf_free = [None, None]
            xlnT_free_pe = None

            for st in range(NST):
                if xlnT_free_pe is not None:
                    p.wait("vector", xlnT_free_pe)
                evt = None
                for tt in range(16):
                    par = tt % 2
                    lt = st * 16 + tt
                    p.wait("sync", xsb_free[par])
                    xt = p.emit("sync", lambda s, lt=lt, par=par: s.dma_start(
                        out=xsb[:, par, :], in_=x_in[lt * 128:(lt + 1) * 128, :]), inc=dX[par], inc_n=16)
                    ap_tok, sq_tok, r_tok = ln_chain(xt, xsb[:, par, :], stats[:, par * 16:(par + 1) * 16],
                                                     xln[:, par, :], slot_prev=stats_prev[par])
                    stats_prev[par] = r_tok
                    xsb_free[par] = (ap_tok, sq_tok)
                    for half in range(2):
                        p.wait("tensor", trps_free[half])
                        p.wait("tensor", ap_tok)
                        for j in range(8):
                            dt = half * 8 + j
                            tp = p.emit("tensor", lambda t, par=par, dt=dt, half=half, j=j: t.transpose(
                                trps[half][:, j * 128:(j + 1) * 128],
                                xln[:, par, dt * 128:(dt + 1) * 128], identb[:]),
                                inc=(sPE if j == 7 else None))
                        p.wait("vector", tp)
                        evt = p.emit("vector", lambda v, half=half, tt=tt: v.tensor_copy(
                            xlnT[:, half * 8:(half + 1) * 8, tt * 128:(tt + 1) * 128],
                            trps[half].ap().rearrange("p (j m) -> p j m", j=8)), inc=sDV)
                        trps_free[half] = evt
                xlnT_ready = evt

                mchunks = list(range(4, 12)) if st < 2 else list(range(12))
                pscnt = 0
                wlast = None
                pending = None

                def emit_qk_epilogue(tile):
                    nonlocal ssps_free
                    ps, ps_i, mm, isq, head, tg, sp = tile
                    p.wait("scalar", mm)
                    p.wait("scalar", sqb_free[sp])
                    sq = p.emit("scalar", lambda s, ps=ps, sp=sp: s.activation(
                        sqb[:, sp, :], ps[:], AF.Square), inc=sAC)
                    p.wait("tensor", sq)
                    p.wait("tensor", ssps_free)
                    ssm = p.emit("tensor", lambda t, sp=sp: t.matmul(
                        ssps[:], oo128[:], sqb[:, sp, :], start=True, stop=True), inc=sPE)
                    sqb_free[sp] = ssm
                    p.wait("scalar", ssm)
                    p.wait("scalar", rstdb_free[sp])
                    p.emit("scalar", lambda s, sp=sp: s.activation(
                        rstdb[:, sp, :], ssps[:], AF.Ln, bias=consts[:, C_EPSRMS:C_EPSRMS + 1]), inc=sAC)
                    l1 = p.tok(sAC)
                    ssps_free = l1
                    r1 = p.emit("scalar", lambda s, sp=sp: s.activation(
                        rstdb[:, sp, :], rstdb[:, sp, :], AF.Exp, scale=consts[:, C_NHALF:C_NHALF + 1]), inc=sAC)
                    p.wait("vector", r1)
                    p.wait("vector", qnb_free[sp])
                    qn = p.emit("vector", lambda v, ps=ps, sp=sp: v.tensor_tensor(
                        qnb[:, sp, :], ps[:], rstdb[:, sp, :], ALU.mult), inc=sDV)
                    mmps_free[ps_i] = qn
                    rstdb_free[sp] = qn
                    p.wait("tensor", qn)
                    p.wait("tensor", rotps_free[sp])
                    rmm = p.emit("tensor", lambda t, sp=sp: t.matmul(
                        rotps[sp][:], prot[:], qnb[:, sp, :], start=True, stop=True), inc=sPE)
                    if isq:
                        Ct, St, base = Cq, Sq, tg * 512
                    else:
                        Ct, St, base = Ck, Sk, st * 2048 + tg * 512
                    p.wait("vector", rope_done)
                    p.wait("vector", ropec_free[sp])
                    c1 = p.emit("vector", lambda v, sp=sp, Ct=Ct, base=base: v.tensor_tensor(
                        ropec[:, sp, :], qnb[:, sp, :], Ct[:, base:base + 512], ALU.mult), inc=sDV)
                    qnb_free[sp] = c1
                    p.wait("vector", rmm)
                    p.wait("vector", roted_free[sp])
                    rt = p.emit("vector", lambda v, sp=sp, St=St, base=base: v.tensor_tensor(
                        roted[:, sp, :], rotps[sp][:], St[:, base:base + 512], ALU.mult), inc=sDV)
                    rotps_free[sp] = rt
                    p.wait("vector", qkst_free[sp])
                    fin = p.emit("vector", lambda v, sp=sp: v.tensor_tensor(
                        qkst[:, sp, :], ropec[:, sp, :], roted[:, sp, :], ALU.add), inc=sDV)
                    ropec_free[sp] = fin
                    roted_free[sp] = fin
                    p.wait("sync", fin)
                    if isq:
                        dst = q_sc.ap()[head, :, tg * 512:(tg + 1) * 512]
                    else:
                        dst = k_sc.ap()[head, :, st * 2048 + tg * 512: st * 2048 + (tg + 1) * 512]
                    stt = p.emit("sync", lambda s, dst=dst, sp=sp: s.dma_start(
                        out=dst, in_=qkst[:, sp, :]), inc=dS[sp], inc_n=16)
                    qkst_free[sp] = stt

                for mc in mchunks:
                    par = mc % 2
                    p.wait("gpsimd", wbuf_free[par])
                    wt = p.emit("gpsimd", lambda g, mc=mc, par=par: g.dma_start(
                        out=wbuf[:, par],
                        in_=w_qkv[:, mc * 512:(mc + 1) * 512].rearrange("(dt p) m -> p dt m", p=128)),
                        inc=dW[par], inc_n=16)
                    if mc < 8:
                        isq = mc < 4
                        for ms in range(4):
                            head = (mc - (0 if isq else 4)) * 4 + ms
                            for tg in range(4):
                                ps_i = pscnt % 2
                                sp = pscnt % 2
                                pscnt += 1
                                ps = mmps[ps_i]
                                p.wait("tensor", mmps_free[ps_i])
                                p.wait("tensor", wt)
                                p.wait("tensor", xlnT_ready)
                                for dt in range(NDT):
                                    mm = p.emit("tensor", lambda t, ps=ps, par=par, dt=dt, ms=ms, tg=tg: t.matmul(
                                        ps[:], wbuf[:, par, dt, ms * 128:(ms + 1) * 128],
                                        xlnT[:, dt, tg * 512:(tg + 1) * 512],
                                        start=(dt == 0), stop=(dt == NDT - 1)),
                                        inc=(sPE if dt == NDT - 1 else None))
                                if pending is not None:
                                    emit_qk_epilogue(pending)
                                pending = (ps, ps_i, mm, isq, head, tg, sp)
                                wlast = mm
                    else:
                        for tt in range(16):
                            ps_i = pscnt % 2
                            pscnt += 1
                            ps = mmps[ps_i]
                            p.wait("tensor", mmps_free[ps_i])
                            p.wait("tensor", wt)
                            p.wait("tensor", xlnT_ready)
                            for dt in range(NDT):
                                mm = p.emit("tensor", lambda t, ps=ps, par=par, dt=dt, tt=tt: t.matmul(
                                    ps[:], xlnT[:, dt, tt * 128:(tt + 1) * 128], wbuf[:, par, dt, :],
                                    start=(dt == 0), stop=(dt == NDT - 1)),
                                    inc=(sPE if dt == NDT - 1 else None))
                            if pending is not None:
                                emit_qk_epilogue(pending)
                                pending = None
                            sp = tt % 2
                            p.wait("scalar", mm)
                            p.wait("scalar", vst_free[sp])
                            cv = p.emit("scalar", lambda s, ps=ps, sp=sp: s.copy(vst[:, sp, :], ps[:]), inc=sAC)
                            mmps_free[ps_i] = cv
                            p.wait("sync", cv)
                            vlast = None
                            for hs in range(4):
                                head = (mc - 8) * 4 + hs
                                vlast = p.emit("sync", lambda s, head=head, st=st, tt=tt, sp=sp, hs=hs: s.dma_start(
                                    out=v_sc.ap()[head, st * 2048 + tt * 128: st * 2048 + (tt + 1) * 128, :],
                                    in_=vst[:, sp, hs * 128:(hs + 1) * 128]), inc=dS[sp], inc_n=16)
                            vst_free[sp] = vlast
                            wlast = mm
                    wbuf_free[par] = wlast
                if pending is not None:
                    emit_qk_epilogue(pending)
                    pending = None
                xlnT_free_pe = wlast

            if "xlnT" in DEBUG_DUMPS:
                p.barrier(ALL_SEMS)
                p.emit("sync", lambda s: s.dma_start(out=dbg["dbg_xlnT"], in_=xlnT[:]), inc=dO, inc_n=16)
            if "qkv" in DEBUG_DUMPS:
                p.barrier(ALL_SEMS)
                for name, sc in (("dbg_q", q_sc), ("dbg_k", k_sc), ("dbg_v", v_sc)):
                    p.emit("sync", lambda s, n=name, sc=sc: s.dma_start(out=dbg[n], in_=sc.ap()), inc=dO, inc_n=16)

        p.barrier(ALL_SEMS)

        if EMIT_P2:
            with ExitStack() as P2:
                e2 = P2.enter_context
                qh = e2(nc.sbuf_tensor("qh", [128, 2, TOK], BF16))
                kh = e2(nc.sbuf_tensor("kh", [128, 2, LTOK], BF16))
                vh = e2(nc.sbuf_tensor("vh", [128, 2, LTOK // 128, 128], BF16))
                mT = e2(nc.sbuf_tensor("mT", [128, WTILES, 512], BF16))
                PT = e2(nc.sbuf_tensor("PT", [128, WTILES, 512], BF16))
                aost = e2(nc.sbuf_tensor("aost", [128, 2, 512], BF16))
                lbuf = e2(nc.sbuf_tensor("lbuf", [1, 2, 512], F32))
                linv = e2(nc.sbuf_tensor("linv", [1, 2, 512], F32))
                gbuf = e2(nc.sbuf_tensor("gbuf", [1, 2, 512], BF16))
                gbc = e2(nc.sbuf_tensor("gbc", [128, 2, 512], BF16))
                scT = [e2(nc.psum_tensor("scT0", [128, 512], F32)),
                       e2(nc.psum_tensor("scT1", [128, 512], F32))]
                lps = e2(nc.psum_tensor("lps", [1, 512], F32))
                ovps = [e2(nc.psum_tensor("ovps0", [128, 512], F32)),
                        e2(nc.psum_tensor("ovps1", [128, 512], F32))]
                bcps = e2(nc.psum_tensor("bcps", [128, 512], F32))
                qtps = [e2(nc.psum_tensor("qtps0", [128, 1024], BF16)),
                        e2(nc.psum_tensor("qtps1", [128, 1024], BF16))]
                qb2 = e2(nc.sbuf_tensor("qb2", [128, 2, 2048], F32))
                s1b = e2(nc.sbuf_tensor("s1b", [128, 2, 2048], BF16))
                s2b = e2(nc.sbuf_tensor("s2b", [128, 2, 2048], BF16))
                wqb = e2(nc.sbuf_tensor("wqb", [128, 2, 2048], BF16))
                wstg = e2(nc.sbuf_tensor("wstg", [128, 2, 2048], BF16))

                nw1 = DFF // 128
                nw2 = (D // 128) * (DFF // 2048)
                NSLOT = 2

                def wtile_src(it):
                    if it < nw1:
                        return w1[it * 128:(it + 1) * 128, :], True, it
                    j = it - nw1
                    rt, cq = divmod(j, 4)
                    return w2[rt * 128:(rt + 1) * 128, cq * 2048:(cq + 1) * 2048], False, j

                qload = {"n": 0, "free": [None] * 2, "toks": {}}
                qsems = [dQ[0], dQ[1]]

                def emit_qload():
                    li = qload["n"]
                    if li >= 2 * (nw1 + nw2):
                        return
                    qload["n"] += 1
                    slot = li % NSLOT
                    srcw, _, _ = wtile_src(li % (nw1 + nw2))
                    p.wait("scalar", qload["free"][slot])
                    qload["toks"][li] = p.emit("scalar", lambda s, srcw=srcw, slot=slot: s.dma_start(
                        out=qb2[:, slot, :], in_=srcw), inc=qsems[slot], inc_n=16)

                sstate = {"i": 0}

                def emit_scale_step():
                    it = sstate["i"]
                    if it >= nw1 + nw2:
                        return
                    sstate["i"] += 1
                    while qload["n"] < min(it + NSLOT, nw1 + nw2):
                        emit_qload()
                    slot = it % NSLOT
                    p.wait("scalar", qload["toks"][it])
                    ab = p.emit("scalar", lambda s, slot=slot, it=it: s.activation(
                        junk[:], qb2[:, slot, :], AF.Abs, accum_out=acc[:, it:it + 1]), inc=sAC)
                    qload["free"][slot] = ab

                qst = {"t1": 0, "t2": 0, "t3": 0, "sfree": [None, None],
                       "wqfree": [None, None], "tpfree": [None, None], "stfree": [None, None]}

                def quant_q1():
                    t = qst["t1"]
                    if t >= nw1 + nw2:
                        return
                    qst["t1"] += 1
                    li = nw1 + nw2 + t
                    while qload["n"] < min(li + NSLOT, 2 * (nw1 + nw2)):
                        emit_qload()
                    slot = li % NSLOT
                    _, isw1, _ = wtile_src(t)
                    tcols = (1, 0) if isw1 else (3, 2)
                    sp = t % 2
                    p.wait("scalar", qload["toks"][li])
                    p.wait("scalar", thr_done[0])
                    p.wait("scalar", qst["sfree"][sp])
                    p.emit("scalar", lambda s, slot=slot, sp=sp, tc=tcols[0]: s.activation(
                        s1b[:, sp, :], qb2[:, slot, :], AF.Sign, bias=thr[:, tc:tc + 1]), inc=sAC)
                    sg2 = p.emit("scalar", lambda s, slot=slot, sp=sp, tc=tcols[1]: s.activation(
                        s2b[:, sp, :], qb2[:, slot, :], AF.Sign, bias=thr[:, tc:tc + 1]), inc=sAC)
                    qload["free"][slot] = sg2
                    qst.setdefault("sgtok", {})[t] = sg2

                def quant_q2():
                    t = qst["t2"]
                    if t >= nw1 + nw2 or t >= qst["t1"]:
                        return
                    qst["t2"] += 1
                    sp = t % 2
                    p.wait("vector", qst["sgtok"][t])
                    p.wait("vector", qst["wqfree"][sp])
                    ad = p.emit("vector", lambda v, sp=sp: v.tensor_tensor(
                        wqb[:, sp, :], s1b[:, sp, :], s2b[:, sp, :], ALU.add), inc=sDV)
                    qst["sfree"][sp] = ad
                    qst.setdefault("adtok", {})[t] = ad

                def quant_q3():
                    t = qst["t3"]
                    if t >= nw1 + nw2 or t >= qst["t2"]:
                        return
                    qst["t3"] += 1
                    sp = t % 2
                    _, isw1, j = wtile_src(t)
                    evq = None
                    tq = None
                    for half in range(2):
                        p.wait("tensor", qst["tpfree"][half])
                        p.wait("tensor", qst["adtok"][t])
                        for jj in range(8):
                            dt = half * 8 + jj
                            tq = p.emit("tensor", lambda tt_, sp=sp, half=half, jj=jj, dt=dt: tt_.transpose(
                                qtps[half][:, jj * 128:(jj + 1) * 128], wqb[:, sp, dt * 128:(dt + 1) * 128],
                                identb[:]), inc=(sPE if jj == 7 else None))
                        p.wait("vector", tq)
                        p.wait("vector", qst["stfree"][sp])
                        evq = p.emit("vector", lambda v, sp=sp, half=half: v.tensor_copy(
                            wstg[:, sp, half * 1024:(half + 1) * 1024], qtps[half].ap()), inc=sDV)
                        qst["tpfree"][half] = evq
                    qst["wqfree"][sp] = tq
                    p.wait("sync", evq)
                    if isw1:
                        dstw = w1qT_sc.ap()[t].rearrange("dt p m -> p dt m")
                    else:
                        rt, cq = divmod(j, 4)
                        dstw = w2qT_sc.ap()[rt, cq * 16:(cq + 1) * 16].rearrange("ft p m -> p ft m")
                    stok = p.emit("sync", lambda s, dstw=dstw, sp=sp: s.dma_start(
                        out=dstw, in_=wstg[:, sp, :].rearrange("p (dt m) -> p dt m", dt=16)),
                        inc=dQS[sp], inc_n=16)
                    qst["stfree"][sp] = stok

                thr_done = [None]
                pend_epi = None
                kv_free = [None, None]
                qh_free = [None, None]
                scT_free = [None, None]
                mT_free = [[None] * 4, [None] * 4]
                mT_tok = [None] * 4
                PT_free_pe = None
                lps_free = None
                ovps_free = [None, None]
                bcps_free = None
                aost_free = [None, None]
                exp_toks = [None] * WTILES

                def emit_head_epi(d):
                    nonlocal bcps_free, ovps_free, aost_free
                    ph, pg, sp, gc, pv = d["h"], d["g"], d["sp"], d["gc"], d["pv"]
                    p.wait("tensor", gc)
                    p.wait("tensor", bcps_free)
                    bc = p.emit("tensor", lambda t, sp=sp: t.matmul(
                        bcps[:], onesb[0:1, :], gbuf[0:1, sp, :], start=True, stop=True), inc=sPE)
                    p.wait("scalar", bc)
                    be = p.emit("scalar", lambda s, sp=sp: s.copy(gbc[:, sp, :], bcps[:]), inc=sAC)
                    bcps_free = be
                    p.wait("vector", be)
                    p.wait("vector", aost_free[sp])
                    evo = p.emit("vector", lambda v, sp=sp: v.tensor_tensor(
                        aost[:, sp, :], ovps[sp][:], gbc[:, sp, :], ALU.mult), inc=sDV)
                    ovps_free[sp] = evo
                    p.wait("sync", evo)
                    ast = p.emit("sync", lambda s, ph=ph, pg=pg, sp=sp: s.dma_start(
                        out=ao_sc.ap()[ph, :, pg * 512:(pg + 1) * 512], in_=aost[:, sp, :]),
                        inc=dS[sp], inc_n=16)
                    aost_free[sp] = ast

                for g in range(4):
                    if g == 2:
                        stps = scT[1]
                        abs_done = p.tok(sAC)
                        p.wait("vector", abs_done)
                        r1 = p.emit("vector", lambda v: v.reduce_sum(acc[:, 128:129], acc[:, 0:64], axis=AX.X), inc=sDV)
                        r2 = p.emit("vector", lambda v: v.reduce_sum(acc[:, 129:130], acc[:, 64:128], axis=AX.X), inc=sDV)
                        p.wait("tensor", r1)
                        p.wait("tensor", r2)
                        p.wait("tensor", scT_free[1])
                        t1m = p.emit("tensor", lambda t: t.matmul(stps[0:1, 0:1], acc[:, 128:129], onesf[:, 0:1],
                                                                  start=True, stop=True), inc=sPE)
                        t2m = p.emit("tensor", lambda t: t.matmul(stps[0:1, 1:2], acc[:, 129:130], onesf[:, 0:1],
                                                                  start=True, stop=True), inc=sPE)
                        p.wait("vector", t2m)
                        sc1 = p.emit("vector", lambda v: v.tensor_scalar(
                            acc[0:1, 130:132], stps[0:1, 0:2], 0.7 / float(DFF * D), None, ALU.mult), inc=sDV)
                        p.wait("tensor", sc1)
                        bb = p.emit("tensor", lambda t: t.matmul(stps[:, 2:3], onesf[0:1, :], acc[0:1, 130:131],
                                                                 start=True, stop=True), inc=sPE)
                        bb2 = p.emit("tensor", lambda t: t.matmul(stps[:, 3:4], onesf[0:1, :], acc[0:1, 131:132],
                                                                  start=True, stop=True), inc=sPE)
                        p.wait("vector", bb2)
                        p.emit("vector", lambda v: v.tensor_copy(thr[:, 0:1], stps[:, 2:3]), inc=sDV)
                        cp2 = p.emit("vector", lambda v: v.tensor_copy(thr[:, 2:3], stps[:, 3:4]), inc=sDV)
                        p.wait("vector", cp2)
                        th1 = p.emit("vector", lambda v: v.tensor_scalar(
                            thr[:, 1:2], thr[:, 0:1], -1.0, None, ALU.mult), inc=sDV)
                        th2 = p.emit("vector", lambda v: v.tensor_scalar(
                            thr[:, 3:4], thr[:, 2:3], -1.0, None, ALU.mult), inc=sDV)
                        thr_done[0] = th2
                        scT_free[1] = cp2
                    for r in range(4):
                        p.wait("sync", mT_free[0][r])
                        mT_tok[r] = p.emit("sync", lambda s, g=g, r=r: s.dma_start(
                            out=mT[:, r * 9:(r + 1) * 9, :],
                            in_=masksT_in[g, r * 9:(r + 1) * 9].rearrange("jj p q -> p jj q")),
                            inc=dM[r % 2], inc_n=16)
                    for h in range(NH):
                        par = h % 2
                        if pend_epi is not None:
                            emit_head_epi(pend_epi)
                            pend_epi = None
                        p.wait("sync", kv_free[par])
                        kl = p.emit("sync", lambda s, h=h, par=par: s.dma_start(
                            out=kh[:, par, :], in_=k_sc.ap()[h]), inc=dH[par], inc_n=16)
                        vl = p.emit("sync", lambda s, h=h, par=par: s.dma_start(
                            out=vh[:, par],
                            in_=v_sc.ap()[h].rearrange("(tt p) d -> p tt d", p=128)), inc=dH[par], inc_n=16)
                        p.wait("sync", qh_free[par])
                        ql = p.emit("sync", lambda s, h=h, par=par: s.dma_start(
                            out=qh[:, par, :], in_=q_sc.ap()[h]), inc=dH[par], inc_n=16)
                        hload = ql
                        last_pv = None
                        for jj in range(WTILES + 2):
                            if jj < WTILES:
                                ps_i = jj % 2
                                ps = scT[ps_i]
                                irel_open = all((L >= 32) and (i_ < jj < i_ + 32)
                                                for L in [4 * g + jj] for i_ in range(4))
                                p.wait("tensor", scT_free[ps_i])
                                p.wait("tensor", mT_tok[jj // 9])
                                p.wait("tensor", hload)
                                if not irel_open:
                                    p.emit("tensor", lambda t, ps=ps, jj=jj: t.matmul(
                                        ps[:], identb[:], mT[:, jj, :],
                                        start=True, stop=False))
                                sc = p.emit("tensor", lambda t, ps=ps, g=g, jj=jj, par=par, oa=irel_open: t.matmul(
                                    ps[:], kh[:, par, 128 * (4 * g + jj): 128 * (4 * g + jj + 1)],
                                    qh[:, par, g * 512:(g + 1) * 512],
                                    start=oa, stop=True), inc=sPE)
                                p.wait("scalar", sc)
                                p.wait("scalar", PT_free_pe)
                                ex = p.emit("scalar", lambda s, ps=ps, jj=jj: s.activation(
                                    PT[:, jj, :], ps[:], AF.Exp, bias=mbias[:]), inc=sAC)
                                exp_toks[jj] = ex
                                scT_free[ps_i] = ex
                            jl = jj - 2
                            if 0 <= jl < WTILES:
                                p.wait("tensor", exp_toks[jl])
                                if jl == 0:
                                    p.wait("tensor", lps_free)
                                    p.wait("tensor", ovps_free[h % 2])
                                    p.wait("tensor", vl)
                                p.emit("tensor", lambda t, jl=jl: t.matmul(
                                    lps[:], onesb[:, 0:1], PT[:, jl, :],
                                    start=(jl == 0), stop=(jl == WTILES - 1)),
                                    inc=(sPE if jl == WTILES - 1 else None))
                                pv = p.emit("tensor", lambda t, g=g, jl=jl, par=par, sp2=h % 2: t.matmul(
                                    ovps[sp2][:], vh[:, par, 4 * g + jl, :], PT[:, jl, :],
                                    start=(jl == 0), stop=(jl == WTILES - 1)),
                                    inc=(sPE if jl == WTILES - 1 else None))
                        PT_free_pe = pv
                        lv_tok = p.tok(sPE)
                        sp = h % 2
                        p.wait("vector", lv_tok)
                        le = p.emit("vector", lambda v, sp=sp: v.tensor_copy(lbuf[0:1, sp, :], lps[:]), inc=sDV)
                        lps_free = le
                        p.wait("vector", le)
                        gr = p.emit("vector", lambda v, sp=sp: v.reciprocal(linv[0:1, sp, :], lbuf[0:1, sp, :]), inc=sDV)
                        p.wait("vector", gr)
                        gc = p.emit("vector", lambda v, sp=sp: v.tensor_copy(gbuf[0:1, sp, :], linv[0:1, sp, :]), inc=sDV)
                        pend_epi = {"h": h, "g": g, "sp": sp, "gc": gc, "pv": pv}
                        kv_free[par] = pv
                        if h == 0 and g == 0 and "attn" in DEBUG_DUMPS:
                            p.emit("sync", lambda s: s.dma_start(out=dbg["dbg_l"], in_=lbuf[0:1, 0, :]), inc=dO, inc_n=16)
                            p.emit("sync", lambda s: s.dma_start(out=dbg["dbg_g"], in_=linv[0:1, 0, :]), inc=dO, inc_n=16)
                            p.emit("sync", lambda s: s.dma_start(out=dbg["dbg_mb"], in_=mbias[:]), inc=dO, inc_n=16)
                            p.emit("sync", lambda s: s.dma_start(out=dbg["dbg_pt"], in_=PT[:]), inc=dO, inc_n=16)
                            PT_free_pe = p.tok(dO)
                        qh_free[par] = sc
                        for r in range(4):
                            mT_free[0][r] = sc
                        if g < 2:
                            for _q in range(4):
                                emit_scale_step()
                        else:
                            for _q in range(4):
                                quant_q3()
                                quant_q2()
                                quant_q1()
                if pend_epi is not None:
                    emit_head_epi(pend_epi)
                    pend_epi = None
                for _ in range(3):
                    quant_q3()
                    quant_q2()
                    quant_q1()

                if "attn" in DEBUG_DUMPS:
                    p.barrier(ALL_SEMS)
                    p.emit("sync", lambda s: s.dma_start(out=dbg["dbg_ao"], in_=ao_sc.ap()), inc=dO, inc_n=16)

        p.barrier(ALL_SEMS)

        if EMIT_P2C:
            with ExitStack() as P2C:
                e3 = P2C.enter_context
                wob = e3(nc.sbuf_tensor("wob", [128, NDT, D], BF16))
                aos = e3(nc.sbuf_tensor("aos", [128, 2, NDT, 512], BF16))
                pstg = e3(nc.sbuf_tensor("pstg", [128, 2, 512], F32))
                atok = e3(nc.sbuf_tensor("atok", [128, 4, 2048], F32))
                lntmp = e3(nc.sbuf_tensor("lntmp", [128, 2048], F32))
                xre = e3(nc.sbuf_tensor("xre", [128, 2, 2048], F32))
                x2b = e3(nc.sbuf_tensor("x2b", [128, 2, 2048], F32))
                x2l = e3(nc.sbuf_tensor("x2l", [128, 2, 2048], BF16))
                x2tstg = e3(nc.sbuf_tensor("x2tstg", [128, 2, 2048], BF16))
                stats3 = e3(nc.sbuf_tensor("stats3", [128, 64], F32))
                prps = [e3(nc.psum_tensor("prps0", [128, 512], F32)),
                        e3(nc.psum_tensor("prps1", [128, 512], F32))]
                ttps = [e3(nc.psum_tensor("ttps0", [128, 512], F32)),
                        e3(nc.psum_tensor("ttps1", [128, 512], F32))]
                tbps = [e3(nc.psum_tensor("tbps0", [128, 1024], BF16)),
                        e3(nc.psum_tensor("tbps1", [128, 1024], BF16))]

                for dc in range(4):
                    p.emit("gpsimd", lambda g, dc=dc: g.dma_start(
                        out=wob[:, :, dc * 512:(dc + 1) * 512],
                        in_=w_out[:, dc * 512:(dc + 1) * 512].rearrange("(dt p) m -> p dt m", p=128)),
                        inc=dW[dc % 2], inc_n=16)
                wol_toks = [p.tok(dW[0]), p.tok(dW[1])]

                aos_free = [None, None]
                pstg_free = [None, None]
                prps_free = [None, None]
                ttps_free = [None, None]
                tbps_free = [None, None]
                xre_free = [None, None]
                x2b_free = [None, None]
                x2l_free = [None, None]
                x2tstg_free = [None, None]
                st3_prev = [None, None]
                st3b_prev = [None, None]
                atok_free = None

                for tg in range(4):
                    ap_ = tg % 2
                    p.wait("sync", aos_free[ap_])
                    al = p.emit("sync", lambda s, tg=tg, ap_=ap_: s.dma_start(
                        out=aos[:, ap_],
                        in_=ao_sc.ap()[:, :, tg * 512:(tg + 1) * 512].rearrange("dt p m -> p dt m")),
                        inc=dH[ap_], inc_n=16)
                    aos_last_mm = None
                    ev2 = None
                    for dpt in range(NDT):
                        ps_i = dpt % 2
                        ps = prps[ps_i]
                        p.wait("tensor", prps_free[ps_i])
                        p.wait("tensor", al)
                        for tok2 in wol_toks:
                            p.wait("tensor", tok2)
                        for dint in range(NDT):
                            mm = p.emit("tensor", lambda t, ps=ps, ap_=ap_, dint=dint, dpt=dpt: t.matmul(
                                ps[:], wob[:, dint, dpt * 128:(dpt + 1) * 128], aos[:, ap_, dint, :],
                                start=(dint == 0), stop=(dint == NDT - 1)),
                                inc=(sPE if dint == NDT - 1 else None))
                        aos_last_mm = mm
                        sp = dpt % 2
                        p.wait("vector", mm)
                        p.wait("vector", pstg_free[sp])
                        ev = p.emit("vector", lambda v, ps=ps, sp=sp: v.tensor_copy(pstg[:, sp, :], ps[:]), inc=sDV)
                        prps_free[ps_i] = ev
                        p.wait("tensor", ttps_free[sp])
                        p.wait("tensor", ev)
                        tp = None
                        for t4 in range(4):
                            tp = p.emit("tensor", lambda t, sp=sp, t4=t4: t.transpose(
                                ttps[sp][:, t4 * 128:(t4 + 1) * 128], pstg[:, sp, t4 * 128:(t4 + 1) * 128],
                                identf[:]), inc=(sPE if t4 == 3 else None))
                        pstg_free[sp] = tp
                        p.wait("vector", tp)
                        if atok_free is not None:
                            p.wait("vector", atok_free[0])
                            p.wait("vector", atok_free[1])
                        ev2 = p.emit("vector", lambda v, sp=sp, dpt=dpt: v.tensor_copy(
                            atok[:, :, dpt * 128:(dpt + 1) * 128],
                            ttps[sp].ap().rearrange("p (t m) -> p t m", t=4)), inc=sDV)
                        ttps_free[sp] = ev2
                    aos_free[ap_] = aos_last_mm
                    atok_ready = ev2
                    last_tmp = None
                    last_sq = None
                    x2st2 = None
                    for t4 in range(4):
                        lt = tg * 4 + t4
                        xp = t4 % 2
                        p.wait("sync", xre_free[xp])
                        xl = p.emit("sync", lambda s, lt=lt, xp=xp: s.dma_start(
                            out=xre[:, xp, :], in_=x_in[HALO + lt * 128: HALO + (lt + 1) * 128, :]),
                            inc=dX[xp], inc_n=16)
                        asl = atok[:, t4, :]
                        sl = stats3[:, (t4 % 2) * 16:(t4 % 2) * 16 + 16]
                        p.wait("vector", last_tmp)
                        ap_tok, sq_tok, r2t = ln_chain(atok_ready, asl, sl, lntmp[:],
                                                       slot_prev=st3_prev[t4 % 2])
                        st3_prev[t4 % 2] = r2t
                        last_tmp = ap_tok
                        last_sq = sq_tok
                        p.wait("vector", xl)
                        p.wait("vector", x2b_free[xp])
                        x2t = p.emit("vector", lambda v, xp=xp: v.tensor_tensor(
                            x2b[:, xp, :], lntmp[:], xre[:, xp, :], ALU.add), inc=sDV)
                        xre_free[xp] = x2t
                        p.wait("sync", x2t)
                        x2st = p.emit("sync", lambda s, lt=lt, xp=xp: s.dma_start(
                            out=x2_sc.ap()[lt * 128:(lt + 1) * 128, :], in_=x2b[:, xp, :]),
                            inc=dS[xp], inc_n=16)
                        sl3 = stats3[:, 32 + (t4 % 2) * 16: 32 + (t4 % 2) * 16 + 16]
                        p.wait("vector", x2l_free[xp])
                        ap3, sq3, r3t = ln_chain(x2t, x2b[:, xp, :], sl3, x2l[:, xp, :],
                                                 slot_prev=st3b_prev[t4 % 2])
                        st3b_prev[t4 % 2] = r3t
                        x2b_free[xp] = (ap3, sq3, x2st)
                        for half in range(2):
                            p.wait("tensor", tbps_free[half])
                            p.wait("tensor", ap3)
                            tp2 = None
                            for j in range(8):
                                dt = half * 8 + j
                                tp2 = p.emit("tensor", lambda t, xp=xp, half=half, j=j, dt=dt: t.transpose(
                                    tbps[half][:, j * 128:(j + 1) * 128], x2l[:, xp, dt * 128:(dt + 1) * 128],
                                    identb[:]), inc=(sPE if j == 7 else None))
                            p.wait("vector", tp2)
                            p.wait("vector", x2tstg_free[xp])
                            evt2 = p.emit("vector", lambda v, half=half, xp=xp: v.tensor_copy(
                                x2tstg[:, xp, half * 1024:(half + 1) * 1024], tbps[half].ap()), inc=sDV)
                            tbps_free[half] = evt2
                            p.wait("sync", evt2)
                            x2st2 = p.emit("sync", lambda s, lt=lt, half=half, xp=xp: s.dma_start(
                                out=x2ln_sc.ap()[half * 8:(half + 1) * 8, :, lt * 128:(lt + 1) * 128]
                                    .rearrange("dt p m -> p dt m"),
                                in_=x2tstg[:, xp, half * 1024:(half + 1) * 1024]
                                    .rearrange("p (dt m) -> p dt m", dt=8)), inc=dS[xp], inc_n=16)
                            x2tstg_free[xp] = x2st2
                        x2l_free[xp] = tp2
                    atok_free = (last_tmp, last_sq)

                if "x2" in DEBUG_DUMPS:
                    p.barrier(ALL_SEMS)
                    p.emit("sync", lambda s: s.dma_start(out=dbg["dbg_x2"], in_=x2_sc.ap()), inc=dO, inc_n=16)

        p.barrier(ALL_SEMS)

        p.barrier(ALL_SEMS)

        if EMIT_P3:
            with ExitStack() as P3:
                e4 = P3.enter_context
                x2T = e4(nc.sbuf_tensor("x2T", [128, 1, NDT, 512], BF16))
                gT = e4(nc.sbuf_tensor("gT", [128, NFT, 512], BF16))
                l1b = e4(nc.sbuf_tensor("l1b", [128, 2, NDT, 128], BF16))
                l2b = e4(nc.sbuf_tensor("l2b", [128, 2, 32, 128], BF16))
                o2T = e4(nc.sbuf_tensor("o2T", [128, NDT, 512], F32))
                o2tok = e4(nc.sbuf_tensor("o2tok", [128, 2048], F32))
                lntmp4 = e4(nc.sbuf_tensor("lntmp4", [128, 2048], F32))
                x2re = e4(nc.sbuf_tensor("x2re", [128, 2, 2048], F32))
                stats4 = e4(nc.sbuf_tensor("stats4", [128, 32], F32))
                hps = [e4(nc.psum_tensor("hps0", [128, 512], F32)),
                       e4(nc.psum_tensor("hps1", [128, 512], F32))]
                o2ps = [e4(nc.psum_tensor("o2ps0", [128, 512], F32)),
                        e4(nc.psum_tensor("o2ps1", [128, 512], F32))]
                t4ps = [e4(nc.psum_tensor("t4ps0", [128, 512], F32)),
                        e4(nc.psum_tensor("t4ps1", [128, 512], F32))]

                x2T_free = [None, None]
                l1b_free = [None, None]
                l2b_free = [None, None]
                hps_free = [None, None]
                o2ps_free = [None, None]
                t4ps_free = [None, None]
                gT_free_pe = None
                o2T_free = None
                o2tok_free = None
                st4_prev = [None, None]
                x2re_free = [None, None]
                obuf_free = [None, None]
                lntmp4_free = None

                for tg in range(4):
                    tp_ = 0
                    p.wait("sync", x2T_free[tp_])
                    xtl = p.emit("sync", lambda s, tg=tg, tp_=tp_: s.dma_start(
                        out=x2T[:, tp_],
                        in_=x2ln_sc.ap()[:, :, tg * 512:(tg + 1) * 512].rearrange("dt p m -> p dt m")),
                        inc=dH[tp_], inc_n=16)
                    p.wait("scalar", gT_free_pe)
                    hmm = None
                    for ft in range(NFT):
                        lp = ft % 2
                        p.wait("gpsimd", l1b_free[lp])
                        wl = p.emit("gpsimd", lambda g, ft=ft, lp=lp: g.dma_start(
                            out=l1b[:, lp],
                            in_=w1qT_sc.ap()[ft].rearrange("dt p m -> p dt m")), inc=dF[lp], inc_n=16)
                        ps_i = ft % 2
                        ps = hps[ps_i]
                        p.wait("tensor", hps_free[ps_i])
                        p.wait("tensor", wl)
                        p.wait("tensor", xtl)
                        for dt in range(NDT):
                            hmm = p.emit("tensor", lambda t, ps=ps, lp=lp, dt=dt, tp_=tp_: t.matmul(
                                ps[:], l1b[:, lp, dt, :], x2T[:, tp_, dt, :],
                                start=(dt == 0), stop=(dt == NDT - 1)),
                                inc=(sPE if dt == NDT - 1 else None))
                        l1b_free[lp] = hmm
                        p.wait("scalar", hmm)
                        gl = p.emit("scalar", lambda s, ps=ps, ft=ft: s.activation(
                            gT[:, ft, :], ps[:], AF.Gelu,
                            bias=b1_sb[:, ft:ft + 1], scale=consts[:, C_HALF:C_HALF + 1]), inc=sAC)
                        hps_free[ps_i] = gl
                    x2T_free[tp_] = hmm
                    gelu_done = p.tok(sAC)
                    p.wait("scalar", o2T_free)
                    omm = None
                    ol = None
                    for dpt in range(NDT):
                        for hlf in range(2):
                            lp = (dpt * 2 + hlf) % 2
                            p.wait("gpsimd", l2b_free[lp])
                            wl2 = p.emit("gpsimd", lambda g, dpt=dpt, hlf=hlf, lp=lp: g.dma_start(
                                out=l2b[:, lp],
                                in_=w2qT_sc.ap()[dpt, hlf * 32:(hlf + 1) * 32].rearrange("ft p m -> p ft m")),
                                inc=dF[lp], inc_n=16)
                            ps_i = dpt % 2
                            ps = o2ps[ps_i]
                            if hlf == 0:
                                p.wait("tensor", o2ps_free[ps_i])
                            p.wait("tensor", wl2)
                            p.wait("tensor", gelu_done)
                            for fi in range(32):
                                ft = hlf * 32 + fi
                                omm = p.emit("tensor", lambda t, ps=ps, lp=lp, fi=fi, ft=ft: t.matmul(
                                    ps[:], l2b[:, lp, fi, :], gT[:, ft, :],
                                    start=(ft == 0), stop=(ft == NFT - 1)),
                                    inc=(sPE if fi == 31 else None))
                            l2b_free[lp] = omm
                        p.wait("scalar", omm)
                        ol = p.emit("scalar", lambda s, ps=ps, dpt=dpt: s.activation(
                            o2T[:, dpt, :], ps[:], AF.Identity,
                            bias=b2_sb[:, dpt:dpt + 1], scale=consts[:, C_HALF:C_HALF + 1]), inc=sAC)
                        o2ps_free[dpt % 2] = ol
                    gT_free_pe = omm
                    o2_done = ol
                    if "ffn" in DEBUG_DUMPS and tg == 0:
                        p.wait("sync", p.tok(sAC))
                        p.emit("sync", lambda s: s.dma_start(out=dbg["dbg_g"], in_=gT[:]), inc=dO, inc_n=16)
                        p.emit("sync", lambda s: s.dma_start(out=dbg["dbg_o2"], in_=o2T[:]), inc=dO, inc_n=16)
                    last_tpq = None
                    for t4 in range(4):
                        lt = tg * 4 + t4
                        xp = t4 % 2
                        o2tok_ready = None
                        for q4 in range(4):
                            p.wait("tensor", t4ps_free[xp])
                            p.wait("tensor", o2_done)
                            tpq = None
                            for j in range(4):
                                dt = q4 * 4 + j
                                tpq = p.emit("tensor", lambda t, xp=xp, dt=dt, j=j, t4=t4: t.transpose(
                                    t4ps[xp][:, j * 128:(j + 1) * 128],
                                    o2T[:, dt, t4 * 128:(t4 + 1) * 128], identf[:]),
                                    inc=(sPE if j == 3 else None))
                            last_tpq = tpq
                            p.wait("vector", tpq)
                            p.wait("vector", o2tok_free)
                            evq4 = p.emit("vector", lambda v, q4=q4, xp=xp: v.tensor_copy(
                                o2tok[:, q4 * 512:(q4 + 1) * 512], t4ps[xp].ap()), inc=sDV)
                            t4ps_free[xp] = evq4
                            o2tok_ready = evq4
                        p.wait("sync", x2re_free[xp])
                        xrl = p.emit("sync", lambda s, lt=lt, xp=xp: s.dma_start(
                            out=x2re[:, xp, :], in_=x2_sc.ap()[lt * 128:(lt + 1) * 128, :]),
                            inc=dX[xp], inc_n=16)
                        p.wait("vector", lntmp4_free)
                        ap4, sq4, r4t = ln_chain(o2tok_ready, o2tok[:], stats4[:, xp * 16:(xp + 1) * 16],
                                                 lntmp4[:], slot_prev=st4_prev[xp])
                        st4_prev[xp] = r4t
                        o2tok_free = sq4
                        p.wait("vector", ap4)
                        p.wait("vector", xrl)
                        p.wait("vector", obuf_free[xp])
                        fo = p.emit("vector", lambda v, xp=xp: v.tensor_tensor(
                            x2re[:, xp, :], lntmp4[:], x2re[:, xp, :], ALU.add), inc=sDV)
                        lntmp4_free = fo
                        p.wait("sync", fo)
                        ost = p.emit("sync", lambda s, lt=lt, xp=xp: s.dma_start(
                            out=out[lt * 128:(lt + 1) * 128, :], in_=x2re[:, xp, :]), inc=dO, inc_n=16)
                        x2re_free[xp] = ost
                        obuf_free[xp] = ost
                    o2T_free = last_tpq

        if "ffn" in DEBUG_DUMPS:
            p.barrier(ALL_SEMS)
            p.emit("sync", lambda s: s.dma_start(out=dbg["dbg_x2ln"], in_=x2ln_sc.ap()), inc=dO, inc_n=16)

        p.wait("sync", p.tok(dO))

        with nc.Block() as block:
            @block.sync
            def _(s):
                p.replay("sync", s)

            @block.tensor
            def _(t):
                p.replay("tensor", t)

            @block.vector
            def _(v):
                p.replay("vector", v)

            @block.scalar
            def _(s):
                p.replay("scalar", s)

            @block.gpsimd
            def _(g):
                p.replay("gpsimd", g)

    return nc, dbg



def _host_inputs(L0_attn_x, L0_attn_w_qkv, L0_attn_w_out, L0_attn_q_gamma,
                 L0_attn_k_gamma, L0_ffn_w1, L0_ffn_b1, L0_ffn_w2, L0_ffn_b2):
    x = np.asarray(L0_attn_x, dtype=np.float32)
    inv_freq = 1.0 / THETA ** (np.arange(0, HD, 2, dtype=np.float32) / HD)

    ident_b = np.eye(128, dtype=np.float32).astype(ml_dtypes.bfloat16)
    ident_f = np.eye(128, dtype=np.float32)
    ones_f = np.ones((128, 128), np.float32)
    ones_b = np.ones((128, 128), np.float32).astype(ml_dtypes.bfloat16)
    oo128 = np.full((128, 128), 1.0 / 128.0, dtype=np.float32).astype(ml_dtypes.bfloat16)
    protm = np.zeros((128, 128), np.float32)
    for j in range(64):
        protm[j + 64, j] = -1.0
    for j in range(64, 128):
        protm[j - 64, j] = 1.0
    protm = protm.astype(ml_dtypes.bfloat16)
    consts = np.zeros((128, 16), np.float32)
    consts[:, C_EPSRMS] = 1e-6
    consts[:, C_EPSLN] = 1e-5
    consts[:, C_NHALF] = -0.5
    consts[:, C_NEG1] = -1.0
    consts[:, C_HALF] = 0.5
    consts[:, C_INV2048] = 1.0 / 2048.0

    wqkv = np.ascontiguousarray(np.asarray(L0_attn_w_qkv, np.float32))
    wout = np.ascontiguousarray(np.asarray(L0_attn_w_out, np.float32))
    w1 = np.ascontiguousarray(np.asarray(L0_ffn_w1, np.float32))
    w2 = np.ascontiguousarray(np.asarray(L0_ffn_w2, np.float32))
    b1 = np.ascontiguousarray(np.asarray(L0_ffn_b1, np.float32).reshape(NFT, 128))
    b2 = np.ascontiguousarray(np.asarray(L0_ffn_b2, np.float32).reshape(NDT, 128))
    qga = np.asarray(L0_attn_q_gamma, np.float32)
    kga = np.asarray(L0_attn_k_gamma, np.float32)
    gam4 = np.stack([qga, np.concatenate([qga[64:], qga[:64]]),
                     kga, np.concatenate([kga[64:], kga[:64]])], axis=1)
    gam4 = np.ascontiguousarray(gam4)

    maps = []
    for c in range(8):
        beta, i = divmod(c, 4)
        a = TOK * i
        lo = a - HALO
        xs = np.zeros((LTOK, D), np.float32)
        src_lo = max(lo, 0)
        xs[src_lo - lo:, :] = x[beta, src_lo:a + TOK, :]
        pos = np.arange(lo, a + TOK, dtype=np.float32)
        pos = np.maximum(pos, 0.0)
        ang = pos[None, :] * np.concatenate([inv_freq, inv_freq])[:, None]
        cos_t = np.cos(ang).astype(np.float32)
        sin_t = np.sin(ang).astype(np.float32)
        mkT = np.full((4, WTILES, 128, 512), NEG, np.float32)
        qq = np.arange(512)
        for g in range(4):
            qglob = a + 512 * g + qq
            iq = qq // 128
            for jj in range(WTILES):
                kglob = lo + 128 * (4 * g + jj) + np.arange(128)
                m = (qglob[None, :] >= kglob[:, None]) & \
                    (qglob[None, :] - kglob[:, None] <= WINDOW) & \
                    (kglob[:, None] >= 0) & \
                    (iq[None, :] <= jj) & (jj <= iq[None, :] + 32)
                mkT[g, jj] = np.where(m, 0.0, NEG)
        maps.append({
            "x_pad": xs, "w_qkv": wqkv, "w_out": wout, "gam4": gam4,
            "w1": w1, "b1": b1, "w2": w2, "b2": b2,
            "cos_t": cos_t, "sin_t": sin_t,
            "masksT": mkT.astype(ml_dtypes.bfloat16),
            "identb": ident_b, "identf": ident_f, "onesf": ones_f, "onesb": ones_b,
            "oo128": oo128, "prot": protm, "consts": consts,
        })
    return maps


_RES_HOOK = [None]


def kernel(**inputs):
    maps = _host_inputs(**inputs)
    nc, dbg = build_program()
    res = run_bass_kernel_spmd(nc, maps, core_ids=list(range(8)))
    _RES_HOOK[0] = res
    full = np.zeros((B, N, D), np.float32)
    for c in range(8):
        beta, i = divmod(c, 4)
        full[beta, TOK * i:TOK * (i + 1), :] = res.results[c]["out"]
    return full


# version 17 (final)
# speedup vs baseline: 1.2390x; 1.1024x over previous
import math
from contextlib import ExitStack

import numpy as np
import ml_dtypes

import concourse.bass as bass
import concourse.mybir as mybir
from concourse.bass_utils import run_bass_kernel_spmd

F32 = mybir.dt.float32
BF16 = mybir.dt.bfloat16
AF = mybir.ActivationFunctionType
ALU = mybir.AluOpType
AX = mybir.AxisListType

B = 2
N = 8192
D = 2048
NH = 16
HD = 128
DFF = 8192
WINDOW = 4096
THETA = 10000.0
NEG = -1e30

TOK = 2048
HALO = 4096
LTOK = HALO + TOK
NQT = TOK // 128
WTILES = 36
NCH = 9
NDT = D // 128
NFT = DFF // 128
NST = LTOK // 2048

C_EPSRMS, C_EPSLN, C_NHALF, C_NEG1, C_HALF, C_INV2048, C_T07W1, C_T07W2 = range(8)

LN_PARANOID = 0
EMIT_P2 = True
EMIT_P2C = True
EMIT_QUANT = True
EMIT_P3 = True
DEBUG_DUMPS = ()

ENGINES = ("sync", "tensor", "vector", "scalar", "gpsimd")


class Prog:
    def __init__(self, nc):
        self.nc = nc
        self.events = {e: [] for e in ENGINES}
        self.ctr = {}
        self.sems = {}
        self.last_wait = {e: {} for e in ENGINES}

    def sem(self, ctx, name):
        h = ctx.enter_context(self.nc.semaphore(name))
        self.sems[name] = h
        self.ctr[name] = 0
        return name

    def wait(self, eng, token):
        if token is None:
            return
        if isinstance(token, (tuple, list)) and token and isinstance(token[0], (tuple, list, type(None))):
            for t in token:
                self.wait(eng, t)
            return
        sem, val = token
        if val <= 0:
            return
        lw = self.last_wait[eng]
        if lw.get(sem, 0) >= val:
            return
        lw[sem] = val
        self.events[eng].append(("wait", self.sems[sem], val))

    def emit(self, eng, fn, inc=None, inc_n=1):
        tok = None
        if inc is not None:
            self.ctr[inc] += inc_n
            tok = (inc, self.ctr[inc])
            self.events[eng].append(("op", fn, self.sems[inc], inc_n))
        else:
            self.events[eng].append(("op", fn, None, 0))
        return tok

    def tok(self, sem):
        return (sem, self.ctr[sem])

    def barrier(self, sems):
        toks = [(s, self.ctr[s]) for s in sems]
        for eng in ENGINES:
            for t in toks:
                self.wait(eng, t)

    def replay(self, eng, engobj):
        for ev in self.events[eng]:
            if ev[0] == "wait":
                engobj.wait_ge(ev[1], ev[2])
            else:
                inst = ev[1](engobj)
                if ev[2] is not None:
                    inst.then_inc(ev[2], ev[3])


def build_program():
    nc = bass.Bass("TRN2", target_bir_lowering=False)

    x_in = nc.dram_tensor("x_pad", [LTOK, D], F32, kind="ExternalInput").ap()
    w_qkv = nc.dram_tensor("w_qkv", [D, 3 * D], F32, kind="ExternalInput").ap()
    w_out = nc.dram_tensor("w_out", [D, D], F32, kind="ExternalInput").ap()
    gam_in = nc.dram_tensor("gam4", [128, 4], F32, kind="ExternalInput").ap()
    w1 = nc.dram_tensor("w1", [DFF, D], F32, kind="ExternalInput").ap()
    b1_in = nc.dram_tensor("b1", [NFT, 128], F32, kind="ExternalInput").ap()
    w2 = nc.dram_tensor("w2", [D, DFF], F32, kind="ExternalInput").ap()
    b2_in = nc.dram_tensor("b2", [NDT, 128], F32, kind="ExternalInput").ap()
    cos_in = nc.dram_tensor("cos_t", [128, LTOK], F32, kind="ExternalInput").ap()
    sin_in = nc.dram_tensor("sin_t", [128, LTOK], F32, kind="ExternalInput").ap()
    masksT_in = nc.dram_tensor("masksT", [4, WTILES, 128, 512], BF16, kind="ExternalInput").ap()
    identb_in = nc.dram_tensor("identb", [128, 128], BF16, kind="ExternalInput").ap()
    identf_in = nc.dram_tensor("identf", [128, 128], F32, kind="ExternalInput").ap()
    onesf_in = nc.dram_tensor("onesf", [128, 128], F32, kind="ExternalInput").ap()
    onesb_in = nc.dram_tensor("onesb", [128, 128], BF16, kind="ExternalInput").ap()
    oo128_in = nc.dram_tensor("oo128", [128, 128], BF16, kind="ExternalInput").ap()
    prot_in = nc.dram_tensor("prot", [128, 128], BF16, kind="ExternalInput").ap()
    consts_in = nc.dram_tensor("consts", [128, 16], F32, kind="ExternalInput").ap()
    out = nc.dram_tensor("out", [TOK, D], F32, kind="ExternalOutput").ap()

    q_sc = nc.dram_tensor("q_sc", [NH, 128, TOK], BF16)
    k_sc = nc.dram_tensor("k_sc", [NH, 128, LTOK], BF16)
    v_sc = nc.dram_tensor("v_sc", [NH, LTOK, 128], BF16)
    ao_sc = nc.dram_tensor("ao_sc", [NH, 128, TOK], BF16)
    x2_sc = nc.dram_tensor("x2_sc", [TOK, D], F32)
    x2ln_sc = nc.dram_tensor("x2ln_sc", [NDT, 128, TOK], BF16)
    w1qT_sc = nc.dram_tensor("w1qT_sc", [NFT, NDT, 128, 128], BF16)
    w2qT_sc = nc.dram_tensor("w2qT_sc", [NDT, NFT, 128, 128], BF16)

    dbg = {}
    if "xlnT" in DEBUG_DUMPS:
        dbg["dbg_xlnT"] = nc.dram_tensor("dbg_xlnT", [128, NDT, 2048], BF16, kind="ExternalOutput").ap()
    if "qkv" in DEBUG_DUMPS:
        dbg["dbg_q"] = nc.dram_tensor("dbg_q", [NH, 128, TOK], BF16, kind="ExternalOutput").ap()
        dbg["dbg_k"] = nc.dram_tensor("dbg_k", [NH, 128, LTOK], BF16, kind="ExternalOutput").ap()
        dbg["dbg_v"] = nc.dram_tensor("dbg_v", [NH, LTOK, 128], BF16, kind="ExternalOutput").ap()
    if "attn" in DEBUG_DUMPS:
        dbg["dbg_ao"] = nc.dram_tensor("dbg_ao", [NH, 128, TOK], BF16, kind="ExternalOutput").ap()
        dbg["dbg_l"] = nc.dram_tensor("dbg_l", [1, 512], F32, kind="ExternalOutput").ap()
        dbg["dbg_g"] = nc.dram_tensor("dbg_gx", [1, 512], F32, kind="ExternalOutput").ap()
        dbg["dbg_mb"] = nc.dram_tensor("dbg_mb", [128, 1], F32, kind="ExternalOutput").ap()
        dbg["dbg_pt"] = nc.dram_tensor("dbg_pt", [128, WTILES, 512], BF16, kind="ExternalOutput").ap()
    if "x2" in DEBUG_DUMPS:
        dbg["dbg_x2"] = nc.dram_tensor("dbg_x2", [TOK, D], F32, kind="ExternalOutput").ap()
    if "ffn" in DEBUG_DUMPS:
        dbg["dbg_x2ln"] = nc.dram_tensor("dbg_x2ln", [NDT, 128, TOK], BF16, kind="ExternalOutput").ap()
        dbg["dbg_g"] = nc.dram_tensor("dbg_g", [128, NFT, 512], BF16, kind="ExternalOutput").ap()
        dbg["dbg_o2"] = nc.dram_tensor("dbg_o2", [128, NDT, 512], F32, kind="ExternalOutput").ap()
    if "wq" in DEBUG_DUMPS:
        dbg["dbg_w1q"] = nc.dram_tensor("dbg_w1q", [NFT, NDT, 128, 128], BF16, kind="ExternalOutput").ap()
        dbg["dbg_thr"] = nc.dram_tensor("dbg_thr", [128, 4], F32, kind="ExternalOutput").ap()
        dbg["dbg_w2q"] = nc.dram_tensor("dbg_w2q", [NDT, NFT, 128, 128], BF16, kind="ExternalOutput").ap()

    with ExitStack() as root:
        er = root.enter_context

        identb = er(nc.sbuf_tensor("identb_s", [128, 128], BF16))
        identf = er(nc.sbuf_tensor("identf_s", [128, 128], F32))
        onesf = er(nc.sbuf_tensor("onesf_s", [128, 128], F32))
        onesb = er(nc.sbuf_tensor("onesb_s", [128, 128], BF16))
        mbias = er(nc.sbuf_tensor("mbias", [128, 1], F32))
        oo128 = er(nc.sbuf_tensor("oo128_s", [128, 128], BF16))
        prot = er(nc.sbuf_tensor("prot_s", [128, 128], BF16))
        consts = er(nc.sbuf_tensor("consts_s", [128, 16], F32))
        gam = er(nc.sbuf_tensor("gam_s", [128, 4], F32))
        b1_sb = er(nc.sbuf_tensor("b1_sb", [128, NFT], F32))
        b2_sb = er(nc.sbuf_tensor("b2_sb", [128, NDT], F32))
        thr = er(nc.sbuf_tensor("thr", [128, 4], F32))
        junk = er(nc.sbuf_tensor("junk", [128, 2048], F32))
        acc = er(nc.sbuf_tensor("acc", [128, 136], F32))

        p = Prog(nc)
        sPE = p.sem(root, "sPE")
        sDV = p.sem(root, "sDV")
        sAC = p.sem(root, "sAC")
        dInit = p.sem(root, "dInit")
        dX = [p.sem(root, "dX0"), p.sem(root, "dX1")]
        dW = [p.sem(root, "dW0"), p.sem(root, "dW1")]
        dS = [p.sem(root, "dS0"), p.sem(root, "dS1")]
        dH = [p.sem(root, "dH0"), p.sem(root, "dH1")]
        dM = [p.sem(root, "dM0"), p.sem(root, "dM1")]
        dQ = [p.sem(root, "dQ0"), p.sem(root, "dQ1")]
        dQS = [p.sem(root, "dQS0"), p.sem(root, "dQS1")]
        dF = [p.sem(root, "dF0"), p.sem(root, "dF1")]
        dO = p.sem(root, "dO")
        ALL_SEMS = [sPE, sDV, sAC, dInit, dX[0], dX[1], dW[0], dW[1], dS[0], dS[1],
                    dH[0], dH[1], dM[0], dM[1], dQ[0], dQ[1], dQS[0], dQS[1],
                    dF[0], dF[1], dO]

        for dst, src in ((identb, identb_in), (identf, identf_in), (onesf, onesf_in),
                         (onesb, onesb_in), (oo128, oo128_in), (prot, prot_in),
                         (consts, consts_in), (gam, gam_in)):
            p.emit("sync", lambda s, d=dst, i=src: s.dma_start(out=d[:], in_=i), inc=dInit, inc_n=16)

        btmp = er(nc.sbuf_tensor("btmp", [128, 128], F32))
        bps_scope = ExitStack()
        bps = bps_scope.enter_context(nc.psum_tensor("bps", [128, 128], F32))
        p.emit("sync", lambda s: s.dma_start(out=btmp[0:32, :], in_=b1_in[0:32]), inc=dInit, inc_n=16)
        p.emit("sync", lambda s: s.dma_start(out=btmp[64:96, :], in_=b1_in[32:64]), inc=dInit, inc_n=16)
        p.wait("tensor", p.tok(dInit))
        tp = p.emit("tensor", lambda t: t.transpose(bps[:], btmp[:], identf[:]), inc=sPE)
        p.wait("vector", tp)
        p.emit("vector", lambda v: v.tensor_copy(b1_sb[:, 0:32], bps[:, 0:32]), inc=sDV)
        ev = p.emit("vector", lambda v: v.tensor_copy(b1_sb[:, 32:64], bps[:, 64:96]), inc=sDV)
        p.wait("sync", ev)
        p.emit("sync", lambda s: s.dma_start(out=btmp[0:NDT, :], in_=b2_in[:]), inc=dInit, inc_n=16)
        p.wait("tensor", p.tok(dInit))
        p.wait("tensor", ev)
        tp2 = p.emit("tensor", lambda t: t.transpose(bps[:], btmp[:], identf[:]), inc=sPE)
        p.wait("vector", tp2)
        p.emit("vector", lambda v: v.tensor_copy(b2_sb[:], bps[:, 0:NDT]), inc=sDV)
        bps_scope.close()
        tok_init = p.tok(dInit)

        p.wait("vector", tok_init)
        gscale = p.emit("vector", lambda v: v.tensor_scalar(
            gam[:, 0:2], gam[:, 0:2], 1.0 / math.sqrt(HD), None, ALU.mult), inc=sDV)

        with nc.sbuf_tensor("mtmp", [128, 4], F32) as mtmp, \
             nc.psum_tensor("mps", [128, 128], F32) as mps:
            p.wait("scalar", gscale)
            ga = p.emit("scalar", lambda s: s.activation(mtmp[:], gam[:], AF.Abs), inc=sAC)
            p.wait("tensor", ga)
            mt1 = p.emit("tensor", lambda t: t.transpose(mps[0:4, 0:128], mtmp[:, 0:4], identf[:]), inc=sPE)
            p.wait("vector", mt1)
            mr = p.emit("vector", lambda v: v.reduce_max(mtmp[0:4, 0:1], mps[0:4, 0:128], axis=AX.X), inc=sDV)
            p.wait("tensor", mr)
            mt2 = p.emit("tensor", lambda t: t.transpose(mps[0:4, 0:128], mtmp[:, 0:4], identf[:]), inc=sPE)
            p.wait("vector", mt2)
            mc2 = p.emit("vector", lambda v: v.tensor_copy(mtmp[0:1, 0:4], mps[0:1, 0:4]), inc=sDV)
            p.wait("vector", mc2)
            mm1 = p.emit("vector", lambda v: v.tensor_tensor(
                mtmp[0:1, 0:1], mtmp[0:1, 0:1], mtmp[0:1, 2:3], ALU.mult), inc=sDV)
            p.wait("vector", mm1)
            mm2 = p.emit("vector", lambda v: v.tensor_scalar(
                mtmp[0:1, 1:2], mtmp[0:1, 0:1], -128.0, None, ALU.mult), inc=sDV)
            p.wait("tensor", mm2)
            mb = p.emit("tensor", lambda t: t.matmul(mps[:, 0:1], onesf[0:1, :], mtmp[0:1, 1:2],
                                                     start=True, stop=True), inc=sPE)
            p.wait("vector", mb)
            p.emit("vector", lambda v: v.tensor_copy(mbias[:], mps[:, 0:1]), inc=sDV)

        def ln_chain(eng_xtok, xap, stats_ap, outap, slot_prev=None):
            p.wait("vector", slot_prev)
            ssum = stats_ap[:, 0:1]
            smean = stats_ap[:, 4:5]
            svar = stats_ap[:, 8:9]
            srstd = stats_ap[:, 12:13]
            p.wait("vector", eng_xtok)
            red_tok = p.emit("vector", lambda v: v.reduce_sum(ssum, xap, axis=AX.X), inc=sDV)
            p.wait("vector", red_tok)
            mean_tok = p.emit("vector", lambda v: v.tensor_scalar(
                smean, ssum, 1.0 / 2048.0, None, ALU.mult), inc=sDV)
            p.wait("vector", mean_tok)
            m2_tok = p.emit("vector", lambda v: v.tensor_tensor(svar, smean, smean, ALU.mult), inc=sDV)
            p.wait("scalar", eng_xtok)
            p.wait("scalar", mean_tok)
            sq_tok = p.emit("scalar", lambda s: s.activation(junk[:], xap, AF.Square, accum_out=ssum), inc=sAC)
            p.wait("vector", sq_tok)
            p.wait("vector", m2_tok)
            var_tok = p.emit("vector", lambda v: v.scalar_tensor_tensor(
                svar, ssum, 1.0 / 2048.0, svar, ALU.mult, ALU.subtract), inc=sDV)
            p.wait("scalar", var_tok)
            ln_tok = p.emit("scalar", lambda s: s.activation(svar, svar, AF.Ln, bias=consts[:, C_EPSLN:C_EPSLN + 1]), inc=sAC)
            p.wait("scalar", ln_tok)
            r_tok = p.emit("scalar", lambda s: s.activation(srstd, svar, AF.Exp, scale=consts[:, C_NHALF:C_NHALF + 1]), inc=sAC)
            p.wait("vector", r_tok)
            p.wait("vector", var_tok)
            ap_tok = p.emit("vector", lambda v: v.tensor_scalar(
                outap, xap, smean, srstd, ALU.subtract, ALU.mult), inc=sDV)
            return ap_tok, sq_tok, r_tok

        with ExitStack() as P1:
            e1 = P1.enter_context
            Cq = e1(nc.sbuf_tensor("Cq", [128, TOK], BF16))
            Sq = e1(nc.sbuf_tensor("Sq", [128, TOK], BF16))
            Ck = e1(nc.sbuf_tensor("Ck", [128, LTOK], BF16))
            Sk = e1(nc.sbuf_tensor("Sk", [128, LTOK], BF16))
            ropebuf = e1(nc.sbuf_tensor("ropebuf", [128, 2, 2, 512], F32))
            xsb = e1(nc.sbuf_tensor("xsb", [128, 2, 2048], F32))
            xln = e1(nc.sbuf_tensor("xln", [128, 2, 2048], BF16))
            stats = e1(nc.sbuf_tensor("stats", [128, 32], F32))
            xlnT = e1(nc.sbuf_tensor("xlnT", [128, NDT, 2048], BF16))
            wbuf = e1(nc.sbuf_tensor("wbuf", [128, 2, NDT, 512], BF16))
            sqb = e1(nc.sbuf_tensor("sqb", [128, 2, 512], BF16))
            rstdb = e1(nc.sbuf_tensor("rstdb", [128, 2, 512], F32))
            qnb = e1(nc.sbuf_tensor("qnb", [128, 2, 512], BF16))
            ropec = e1(nc.sbuf_tensor("ropec", [128, 2, 512], BF16))
            roted = e1(nc.sbuf_tensor("roted", [128, 2, 512], BF16))
            qkst = e1(nc.sbuf_tensor("qkst", [128, 2, 512], BF16))
            vst = e1(nc.sbuf_tensor("vst", [128, 2, 512], BF16))
            trps = [e1(nc.psum_tensor("trps0", [128, 1024], BF16)),
                    e1(nc.psum_tensor("trps1", [128, 1024], BF16))]
            mmps = [e1(nc.psum_tensor("mmps0", [128, 512], F32)),
                    e1(nc.psum_tensor("mmps1", [128, 512], F32))]
            ssps = e1(nc.psum_tensor("ssps", [128, 512], F32))
            rotps = [e1(nc.psum_tensor("rotps0", [128, 512], F32)),
                     e1(nc.psum_tensor("rotps1", [128, 512], F32))]

            rope_use = [None, None]
            for ci in range(LTOK // 512 + TOK // 512):
                par = ci % 2
                if ci < LTOK // 512:
                    srcs = (cos_in[:, ci * 512:(ci + 1) * 512], sin_in[:, ci * 512:(ci + 1) * 512])
                    dsts = (Ck[:, ci * 512:(ci + 1) * 512], Sk[:, ci * 512:(ci + 1) * 512])
                    gcols, gdep = (2, 3), tok_init
                else:
                    cj = ci - LTOK // 512
                    srcs = (cos_in[:, HALO + cj * 512:HALO + (cj + 1) * 512],
                            sin_in[:, HALO + cj * 512:HALO + (cj + 1) * 512])
                    dsts = (Cq[:, cj * 512:(cj + 1) * 512], Sq[:, cj * 512:(cj + 1) * 512])
                    gcols, gdep = (0, 1), gscale
                p.wait("sync", rope_use[par])
                p.emit("sync", lambda s, a=srcs[0], pr=par: s.dma_start(out=ropebuf[:, pr, 0, :], in_=a), inc=dX[par], inc_n=16)
                tl = p.emit("sync", lambda s, a=srcs[1], pr=par: s.dma_start(out=ropebuf[:, pr, 1, :], in_=a), inc=dX[par], inc_n=16)
                p.wait("vector", tl)
                p.wait("vector", gdep)
                p.emit("vector", lambda v, d=dsts[0], pr=par, g=gcols[0]: v.tensor_scalar(
                    d, ropebuf[:, pr, 0, :], gam[:, g:g + 1], None, ALU.mult), inc=sDV)
                u = p.emit("vector", lambda v, d=dsts[1], pr=par, g=gcols[1]: v.tensor_scalar(
                    d, ropebuf[:, pr, 1, :], gam[:, g:g + 1], None, ALU.mult), inc=sDV)
                rope_use[par] = u
            rope_done = p.tok(sDV)

            xsb_free = [None, None]
            stats_prev = [None, None]
            trps_free = [None, None]
            mmps_free = [None, None]
            ssps_free = None
            rotps_free = [None, None]
            sqb_free = [None, None]
            rstdb_free = [None, None]
            qnb_free = [None, None]
            ropec_free = [None, None]
            roted_free = [None, None]
            qkst_free = [None, None]
            vst_free = [None, None]
            wbuf_free = [None, None]
            xlnT_free_pe = None

            for st in range(NST):
                if xlnT_free_pe is not None:
                    p.wait("vector", xlnT_free_pe)
                evt = None
                for tt in range(16):
                    par = tt % 2
                    lt = st * 16 + tt
                    p.wait("sync", xsb_free[par])
                    xt = p.emit("sync", lambda s, lt=lt, par=par: s.dma_start(
                        out=xsb[:, par, :], in_=x_in[lt * 128:(lt + 1) * 128, :]), inc=dX[par], inc_n=16)
                    ap_tok, sq_tok, r_tok = ln_chain(xt, xsb[:, par, :], stats[:, par * 16:(par + 1) * 16],
                                                     xln[:, par, :], slot_prev=stats_prev[par])
                    stats_prev[par] = r_tok
                    xsb_free[par] = (ap_tok, sq_tok)
                    for half in range(2):
                        p.wait("tensor", trps_free[half])
                        p.wait("tensor", ap_tok)
                        for j in range(8):
                            dt = half * 8 + j
                            tp = p.emit("tensor", lambda t, par=par, dt=dt, half=half, j=j: t.transpose(
                                trps[half][:, j * 128:(j + 1) * 128],
                                xln[:, par, dt * 128:(dt + 1) * 128], identb[:]),
                                inc=(sPE if j == 7 else None))
                        p.wait("vector", tp)
                        evt = p.emit("vector", lambda v, half=half, tt=tt: v.tensor_copy(
                            xlnT[:, half * 8:(half + 1) * 8, tt * 128:(tt + 1) * 128],
                            trps[half].ap().rearrange("p (j m) -> p j m", j=8)), inc=sDV)
                        trps_free[half] = evt
                xlnT_ready = evt

                mchunks = list(range(4, 12)) if st < 2 else list(range(12))
                pscnt = 0
                wlast = None
                pending = None

                def emit_qk_epilogue(tile):
                    nonlocal ssps_free
                    ps, ps_i, mm, isq, head, tg, sp = tile
                    p.wait("scalar", mm)
                    p.wait("scalar", sqb_free[sp])
                    sq = p.emit("scalar", lambda s, ps=ps, sp=sp: s.activation(
                        sqb[:, sp, :], ps[:], AF.Square), inc=sAC)
                    p.wait("tensor", sq)
                    p.wait("tensor", ssps_free)
                    ssm = p.emit("tensor", lambda t, sp=sp: t.matmul(
                        ssps[:], oo128[:], sqb[:, sp, :], start=True, stop=True), inc=sPE)
                    sqb_free[sp] = ssm
                    p.wait("scalar", ssm)
                    p.wait("scalar", rstdb_free[sp])
                    p.emit("scalar", lambda s, sp=sp: s.activation(
                        rstdb[:, sp, :], ssps[:], AF.Ln, bias=consts[:, C_EPSRMS:C_EPSRMS + 1]), inc=sAC)
                    l1 = p.tok(sAC)
                    ssps_free = l1
                    r1 = p.emit("scalar", lambda s, sp=sp: s.activation(
                        rstdb[:, sp, :], rstdb[:, sp, :], AF.Exp, scale=consts[:, C_NHALF:C_NHALF + 1]), inc=sAC)
                    p.wait("vector", r1)
                    p.wait("vector", qnb_free[sp])
                    qn = p.emit("vector", lambda v, ps=ps, sp=sp: v.tensor_tensor(
                        qnb[:, sp, :], ps[:], rstdb[:, sp, :], ALU.mult), inc=sDV)
                    mmps_free[ps_i] = qn
                    rstdb_free[sp] = qn
                    p.wait("tensor", qn)
                    p.wait("tensor", rotps_free[sp])
                    rmm = p.emit("tensor", lambda t, sp=sp: t.matmul(
                        rotps[sp][:], prot[:], qnb[:, sp, :], start=True, stop=True), inc=sPE)
                    if isq:
                        Ct, St, base = Cq, Sq, tg * 512
                    else:
                        Ct, St, base = Ck, Sk, st * 2048 + tg * 512
                    p.wait("vector", rope_done)
                    p.wait("vector", ropec_free[sp])
                    c1 = p.emit("vector", lambda v, sp=sp, Ct=Ct, base=base: v.tensor_tensor(
                        ropec[:, sp, :], qnb[:, sp, :], Ct[:, base:base + 512], ALU.mult), inc=sDV)
                    qnb_free[sp] = c1
                    p.wait("vector", rmm)
                    p.wait("vector", roted_free[sp])
                    rt = p.emit("vector", lambda v, sp=sp, St=St, base=base: v.tensor_tensor(
                        roted[:, sp, :], rotps[sp][:], St[:, base:base + 512], ALU.mult), inc=sDV)
                    rotps_free[sp] = rt
                    p.wait("vector", qkst_free[sp])
                    fin = p.emit("vector", lambda v, sp=sp: v.tensor_tensor(
                        qkst[:, sp, :], ropec[:, sp, :], roted[:, sp, :], ALU.add), inc=sDV)
                    ropec_free[sp] = fin
                    roted_free[sp] = fin
                    p.wait("sync", fin)
                    if isq:
                        dst = q_sc.ap()[head, :, tg * 512:(tg + 1) * 512]
                    else:
                        dst = k_sc.ap()[head, :, st * 2048 + tg * 512: st * 2048 + (tg + 1) * 512]
                    stt = p.emit("sync", lambda s, dst=dst, sp=sp: s.dma_start(
                        out=dst, in_=qkst[:, sp, :]), inc=dS[sp], inc_n=16)
                    qkst_free[sp] = stt

                for mc in mchunks:
                    par = mc % 2
                    p.wait("gpsimd", wbuf_free[par])
                    wt = p.emit("gpsimd", lambda g, mc=mc, par=par: g.dma_start(
                        out=wbuf[:, par],
                        in_=w_qkv[:, mc * 512:(mc + 1) * 512].rearrange("(dt p) m -> p dt m", p=128)),
                        inc=dW[par], inc_n=16)
                    if mc < 8:
                        isq = mc < 4
                        for ms in range(4):
                            head = (mc - (0 if isq else 4)) * 4 + ms
                            for tg in range(4):
                                ps_i = pscnt % 2
                                sp = pscnt % 2
                                pscnt += 1
                                ps = mmps[ps_i]
                                p.wait("tensor", mmps_free[ps_i])
                                p.wait("tensor", wt)
                                p.wait("tensor", xlnT_ready)
                                for dt in range(NDT):
                                    mm = p.emit("tensor", lambda t, ps=ps, par=par, dt=dt, ms=ms, tg=tg: t.matmul(
                                        ps[:], wbuf[:, par, dt, ms * 128:(ms + 1) * 128],
                                        xlnT[:, dt, tg * 512:(tg + 1) * 512],
                                        start=(dt == 0), stop=(dt == NDT - 1)),
                                        inc=(sPE if dt == NDT - 1 else None))
                                if pending is not None:
                                    emit_qk_epilogue(pending)
                                pending = (ps, ps_i, mm, isq, head, tg, sp)
                                wlast = mm
                    else:
                        for tt in range(16):
                            ps_i = pscnt % 2
                            pscnt += 1
                            ps = mmps[ps_i]
                            p.wait("tensor", mmps_free[ps_i])
                            p.wait("tensor", wt)
                            p.wait("tensor", xlnT_ready)
                            for dt in range(NDT):
                                mm = p.emit("tensor", lambda t, ps=ps, par=par, dt=dt, tt=tt: t.matmul(
                                    ps[:], xlnT[:, dt, tt * 128:(tt + 1) * 128], wbuf[:, par, dt, :],
                                    start=(dt == 0), stop=(dt == NDT - 1)),
                                    inc=(sPE if dt == NDT - 1 else None))
                            if pending is not None:
                                emit_qk_epilogue(pending)
                                pending = None
                            sp = tt % 2
                            p.wait("scalar", mm)
                            p.wait("scalar", vst_free[sp])
                            cv = p.emit("scalar", lambda s, ps=ps, sp=sp: s.copy(vst[:, sp, :], ps[:]), inc=sAC)
                            mmps_free[ps_i] = cv
                            p.wait("sync", cv)
                            vlast = None
                            for hs in range(4):
                                head = (mc - 8) * 4 + hs
                                vlast = p.emit("sync", lambda s, head=head, st=st, tt=tt, sp=sp, hs=hs: s.dma_start(
                                    out=v_sc.ap()[head, st * 2048 + tt * 128: st * 2048 + (tt + 1) * 128, :],
                                    in_=vst[:, sp, hs * 128:(hs + 1) * 128]), inc=dS[sp], inc_n=16)
                            vst_free[sp] = vlast
                            wlast = mm
                    wbuf_free[par] = wlast
                if pending is not None:
                    emit_qk_epilogue(pending)
                    pending = None
                xlnT_free_pe = wlast

            if "xlnT" in DEBUG_DUMPS:
                p.barrier(ALL_SEMS)
                p.emit("sync", lambda s: s.dma_start(out=dbg["dbg_xlnT"], in_=xlnT[:]), inc=dO, inc_n=16)
            if "qkv" in DEBUG_DUMPS:
                p.barrier(ALL_SEMS)
                for name, sc in (("dbg_q", q_sc), ("dbg_k", k_sc), ("dbg_v", v_sc)):
                    p.emit("sync", lambda s, n=name, sc=sc: s.dma_start(out=dbg[n], in_=sc.ap()), inc=dO, inc_n=16)

        p.barrier(ALL_SEMS)

        if EMIT_P2:
            with ExitStack() as P2:
                e2 = P2.enter_context
                qh = e2(nc.sbuf_tensor("qh", [128, 2, TOK], BF16))
                kh = e2(nc.sbuf_tensor("kh", [128, 2, LTOK], BF16))
                vh = e2(nc.sbuf_tensor("vh", [128, 2, LTOK // 128, 128], BF16))
                mT = e2(nc.sbuf_tensor("mT", [128, WTILES, 512], BF16))
                PT = e2(nc.sbuf_tensor("PT", [128, WTILES, 512], BF16))
                aost = e2(nc.sbuf_tensor("aost", [128, 2, 512], BF16))
                lbuf = e2(nc.sbuf_tensor("lbuf", [1, 2, 512], F32))
                linv = e2(nc.sbuf_tensor("linv", [1, 2, 512], F32))
                gbuf = e2(nc.sbuf_tensor("gbuf", [1, 2, 512], BF16))
                gbc = e2(nc.sbuf_tensor("gbc", [128, 2, 512], BF16))
                scT = [e2(nc.psum_tensor("scT0", [128, 512], F32)),
                       e2(nc.psum_tensor("scT1", [128, 512], F32))]
                lps = e2(nc.psum_tensor("lps", [1, 512], F32))
                ovps = [e2(nc.psum_tensor("ovps0", [128, 512], F32)),
                        e2(nc.psum_tensor("ovps1", [128, 512], F32))]
                bcps = e2(nc.psum_tensor("bcps", [128, 512], F32))
                qtps = [e2(nc.psum_tensor("qtps0", [128, 1024], BF16)),
                        e2(nc.psum_tensor("qtps1", [128, 1024], BF16))]
                qb2 = e2(nc.sbuf_tensor("qb2", [128, 2, 2048], F32))
                s1b = e2(nc.sbuf_tensor("s1b", [128, 2, 2048], BF16))
                s2b = e2(nc.sbuf_tensor("s2b", [128, 2, 2048], BF16))
                wqb = e2(nc.sbuf_tensor("wqb", [128, 2, 2048], BF16))
                wstg = e2(nc.sbuf_tensor("wstg", [128, 2, 2048], BF16))

                nw1 = DFF // 128
                nw2 = (D // 128) * (DFF // 2048)
                NSLOT = 2

                def wtile_src(it):
                    if it < nw1:
                        return w1[it * 128:(it + 1) * 128, :], True, it
                    j = it - nw1
                    rt, cq = divmod(j, 4)
                    return w2[rt * 128:(rt + 1) * 128, cq * 2048:(cq + 1) * 2048], False, j

                qload = {"n": 0, "free": [None] * 2, "toks": {}}
                qsems = [dQ[0], dQ[1]]

                def emit_qload():
                    li = qload["n"]
                    if li >= 2 * (nw1 + nw2):
                        return
                    qload["n"] += 1
                    slot = li % NSLOT
                    srcw, _, _ = wtile_src(li % (nw1 + nw2))
                    p.wait("scalar", qload["free"][slot])
                    qload["toks"][li] = p.emit("scalar", lambda s, srcw=srcw, slot=slot: s.dma_start(
                        out=qb2[:, slot, :], in_=srcw), inc=qsems[slot], inc_n=16)

                sstate = {"i": 0}

                def emit_scale_step():
                    it = sstate["i"]
                    if it >= nw1 + nw2:
                        return
                    sstate["i"] += 1
                    while qload["n"] < min(it + NSLOT, nw1 + nw2):
                        emit_qload()
                    slot = it % NSLOT
                    p.wait("scalar", qload["toks"][it])
                    ab = p.emit("scalar", lambda s, slot=slot, it=it: s.activation(
                        junk[:], qb2[:, slot, :], AF.Abs, accum_out=acc[:, it:it + 1]), inc=sAC)
                    qload["free"][slot] = ab

                qst = {"t1": 0, "t2": 0, "t3": 0, "sfree": [None, None],
                       "wqfree": [None, None], "tpfree": [None, None], "stfree": [None, None]}

                def quant_q1():
                    t = qst["t1"]
                    if t >= nw1 + nw2:
                        return
                    qst["t1"] += 1
                    li = nw1 + nw2 + t
                    while qload["n"] < min(li + NSLOT, 2 * (nw1 + nw2)):
                        emit_qload()
                    slot = li % NSLOT
                    _, isw1, _ = wtile_src(t)
                    tcols = (1, 0) if isw1 else (3, 2)
                    sp = t % 2
                    p.wait("scalar", qload["toks"][li])
                    p.wait("scalar", thr_done[0])
                    p.wait("scalar", qst["sfree"][sp])
                    p.emit("scalar", lambda s, slot=slot, sp=sp, tc=tcols[0]: s.activation(
                        s1b[:, sp, :], qb2[:, slot, :], AF.Sign, bias=thr[:, tc:tc + 1]), inc=sAC)
                    sg2 = p.emit("scalar", lambda s, slot=slot, sp=sp, tc=tcols[1]: s.activation(
                        s2b[:, sp, :], qb2[:, slot, :], AF.Sign, bias=thr[:, tc:tc + 1]), inc=sAC)
                    qload["free"][slot] = sg2
                    qst.setdefault("sgtok", {})[t] = sg2

                def quant_q2():
                    t = qst["t2"]
                    if t >= nw1 + nw2 or t >= qst["t1"]:
                        return
                    qst["t2"] += 1
                    sp = t % 2
                    p.wait("vector", qst["sgtok"][t])
                    p.wait("vector", qst["wqfree"][sp])
                    ad = p.emit("vector", lambda v, sp=sp: v.tensor_tensor(
                        wqb[:, sp, :], s1b[:, sp, :], s2b[:, sp, :], ALU.add), inc=sDV)
                    qst["sfree"][sp] = ad
                    qst.setdefault("adtok", {})[t] = ad

                def quant_q3():
                    t = qst["t3"]
                    if t >= nw1 + nw2 or t >= qst["t2"]:
                        return
                    qst["t3"] += 1
                    sp = t % 2
                    _, isw1, j = wtile_src(t)
                    evq = None
                    tq = None
                    for half in range(2):
                        p.wait("tensor", qst["tpfree"][half])
                        p.wait("tensor", qst["adtok"][t])
                        for jj in range(8):
                            dt = half * 8 + jj
                            tq = p.emit("tensor", lambda tt_, sp=sp, half=half, jj=jj, dt=dt: tt_.transpose(
                                qtps[half][:, jj * 128:(jj + 1) * 128], wqb[:, sp, dt * 128:(dt + 1) * 128],
                                identb[:]), inc=(sPE if jj == 7 else None))
                        p.wait("vector", tq)
                        p.wait("vector", qst["stfree"][sp])
                        evq = p.emit("vector", lambda v, sp=sp, half=half: v.tensor_copy(
                            wstg[:, sp, half * 1024:(half + 1) * 1024], qtps[half].ap()), inc=sDV)
                        qst["tpfree"][half] = evq
                    qst["wqfree"][sp] = tq
                    p.wait("sync", evq)
                    if isw1:
                        dstw = w1qT_sc.ap()[t].rearrange("dt p m -> p dt m")
                    else:
                        rt, cq = divmod(j, 4)
                        dstw = w2qT_sc.ap()[rt, cq * 16:(cq + 1) * 16].rearrange("ft p m -> p ft m")
                    stok = p.emit("sync", lambda s, dstw=dstw, sp=sp: s.dma_start(
                        out=dstw, in_=wstg[:, sp, :].rearrange("p (dt m) -> p dt m", dt=16)),
                        inc=dQS[sp], inc_n=16)
                    qst["stfree"][sp] = stok

                thr_done = [None]
                pend_epi = None
                kv_free = [None, None]
                qh_free = [None, None]
                scT_free = [None, None]
                mT_free = [[None] * 4, [None] * 4]
                mT_tok = [None] * 4
                PT_free_pe = None
                lps_free = None
                ovps_free = [None, None]
                bcps_free = None
                aost_free = [None, None]
                exp_toks = [None] * WTILES

                def emit_head_epi(d):
                    nonlocal bcps_free, ovps_free, aost_free
                    ph, pg, sp, gc, pv = d["h"], d["g"], d["sp"], d["gc"], d["pv"]
                    p.wait("tensor", gc)
                    p.wait("tensor", bcps_free)
                    bc = p.emit("tensor", lambda t, sp=sp: t.matmul(
                        bcps[:], onesb[0:1, :], gbuf[0:1, sp, :], start=True, stop=True), inc=sPE)
                    p.wait("scalar", bc)
                    be = p.emit("scalar", lambda s, sp=sp: s.copy(gbc[:, sp, :], bcps[:]), inc=sAC)
                    bcps_free = be
                    p.wait("vector", be)
                    p.wait("vector", aost_free[sp])
                    evo = p.emit("vector", lambda v, sp=sp: v.tensor_tensor(
                        aost[:, sp, :], ovps[sp][:], gbc[:, sp, :], ALU.mult), inc=sDV)
                    ovps_free[sp] = evo
                    p.wait("gpsimd", evo)
                    ast = p.emit("gpsimd", lambda s, ph=ph, pg=pg, sp=sp: s.dma_start(
                        out=ao_sc.ap()[ph, :, pg * 512:(pg + 1) * 512], in_=aost[:, sp, :]),
                        inc=dS[sp], inc_n=16)
                    aost_free[sp] = ast

                for g in range(4):
                    if g == 2:
                        stps = scT[1]
                        abs_done = p.tok(sAC)
                        p.wait("vector", abs_done)
                        r1 = p.emit("vector", lambda v: v.reduce_sum(acc[:, 128:129], acc[:, 0:64], axis=AX.X), inc=sDV)
                        r2 = p.emit("vector", lambda v: v.reduce_sum(acc[:, 129:130], acc[:, 64:128], axis=AX.X), inc=sDV)
                        p.wait("tensor", r1)
                        p.wait("tensor", r2)
                        p.wait("tensor", scT_free[1])
                        t1m = p.emit("tensor", lambda t: t.matmul(stps[0:1, 0:1], acc[:, 128:129], onesf[:, 0:1],
                                                                  start=True, stop=True), inc=sPE)
                        t2m = p.emit("tensor", lambda t: t.matmul(stps[0:1, 1:2], acc[:, 129:130], onesf[:, 0:1],
                                                                  start=True, stop=True), inc=sPE)
                        p.wait("vector", t2m)
                        sc1 = p.emit("vector", lambda v: v.tensor_scalar(
                            acc[0:1, 130:132], stps[0:1, 0:2], 0.7 / float(DFF * D), None, ALU.mult), inc=sDV)
                        p.wait("tensor", sc1)
                        bb = p.emit("tensor", lambda t: t.matmul(stps[:, 2:3], onesf[0:1, :], acc[0:1, 130:131],
                                                                 start=True, stop=True), inc=sPE)
                        bb2 = p.emit("tensor", lambda t: t.matmul(stps[:, 3:4], onesf[0:1, :], acc[0:1, 131:132],
                                                                  start=True, stop=True), inc=sPE)
                        p.wait("vector", bb2)
                        p.emit("vector", lambda v: v.tensor_copy(thr[:, 0:1], stps[:, 2:3]), inc=sDV)
                        cp2 = p.emit("vector", lambda v: v.tensor_copy(thr[:, 2:3], stps[:, 3:4]), inc=sDV)
                        p.wait("vector", cp2)
                        th1 = p.emit("vector", lambda v: v.tensor_scalar(
                            thr[:, 1:2], thr[:, 0:1], -1.0, None, ALU.mult), inc=sDV)
                        th2 = p.emit("vector", lambda v: v.tensor_scalar(
                            thr[:, 3:4], thr[:, 2:3], -1.0, None, ALU.mult), inc=sDV)
                        thr_done[0] = th2
                        scT_free[1] = cp2
                    for r in range(4):
                        p.wait("sync", mT_free[0][r])
                        mT_tok[r] = p.emit("sync", lambda s, g=g, r=r: s.dma_start(
                            out=mT[:, r * 9:(r + 1) * 9, :],
                            in_=masksT_in[g, r * 9:(r + 1) * 9].rearrange("jj p q -> p jj q")),
                            inc=dM[r % 2], inc_n=16)
                    for h in range(NH):
                        par = h % 2
                        p.wait("sync", kv_free[par])
                        kl = p.emit("sync", lambda s, h=h, par=par: s.dma_start(
                            out=kh[:, par, :], in_=k_sc.ap()[h]), inc=dH[par], inc_n=16)
                        vl = p.emit("sync", lambda s, h=h, par=par: s.dma_start(
                            out=vh[:, par],
                            in_=v_sc.ap()[h].rearrange("(tt p) d -> p tt d", p=128)), inc=dH[par], inc_n=16)
                        p.wait("sync", qh_free[par])
                        ql = p.emit("sync", lambda s, h=h, par=par: s.dma_start(
                            out=qh[:, par, :], in_=q_sc.ap()[h]), inc=dH[par], inc_n=16)
                        hload = ql
                        if pend_epi is not None:
                            emit_head_epi(pend_epi)
                            pend_epi = None
                        last_pv = None
                        for jj in range(WTILES + 2):
                            if jj < WTILES:
                                ps_i = jj % 2
                                ps = scT[ps_i]
                                irel_open = all((L >= 32) and (i_ < jj < i_ + 32)
                                                for L in [4 * g + jj] for i_ in range(4))
                                p.wait("tensor", scT_free[ps_i])
                                p.wait("tensor", mT_tok[jj // 9])
                                p.wait("tensor", hload)
                                if not irel_open:
                                    p.emit("tensor", lambda t, ps=ps, jj=jj: t.matmul(
                                        ps[:], identb[:], mT[:, jj, :],
                                        start=True, stop=False))
                                sc = p.emit("tensor", lambda t, ps=ps, g=g, jj=jj, par=par, oa=irel_open: t.matmul(
                                    ps[:], kh[:, par, 128 * (4 * g + jj): 128 * (4 * g + jj + 1)],
                                    qh[:, par, g * 512:(g + 1) * 512],
                                    start=oa, stop=True), inc=sPE)
                                p.wait("scalar", sc)
                                p.wait("scalar", PT_free_pe)
                                ex = p.emit("scalar", lambda s, ps=ps, jj=jj: s.activation(
                                    PT[:, jj, :], ps[:], AF.Exp, bias=mbias[:]), inc=sAC)
                                exp_toks[jj] = ex
                                scT_free[ps_i] = ex
                            if jj in (8, 17, 26, 35):
                                if g < 2:
                                    emit_scale_step()
                                else:
                                    quant_q3()
                                    quant_q2()
                                    quant_q1()
                            jl = jj - 2
                            if 0 <= jl < WTILES:
                                p.wait("tensor", exp_toks[jl])
                                if jl == 0:
                                    p.wait("tensor", lps_free)
                                    p.wait("tensor", ovps_free[h % 2])
                                    p.wait("tensor", vl)
                                p.emit("tensor", lambda t, jl=jl: t.matmul(
                                    lps[:], onesb[:, 0:1], PT[:, jl, :],
                                    start=(jl == 0), stop=(jl == WTILES - 1)),
                                    inc=(sPE if jl == WTILES - 1 else None))
                                pv = p.emit("tensor", lambda t, g=g, jl=jl, par=par, sp2=h % 2: t.matmul(
                                    ovps[sp2][:], vh[:, par, 4 * g + jl, :], PT[:, jl, :],
                                    start=(jl == 0), stop=(jl == WTILES - 1)),
                                    inc=(sPE if jl == WTILES - 1 else None))
                        PT_free_pe = pv
                        lv_tok = p.tok(sPE)
                        sp = h % 2
                        p.wait("scalar", lv_tok)
                        le = p.emit("scalar", lambda s, sp=sp: s.activation(
                            lbuf[0:1, sp, :], lps[:], AF.Ln), inc=sAC)
                        lps_free = le
                        p.wait("scalar", le)
                        gc = p.emit("scalar", lambda s, sp=sp: s.activation(
                            gbuf[0:1, sp, :], lbuf[0:1, sp, :], AF.Exp,
                            scale=consts[0:1, C_NEG1:C_NEG1 + 1]), inc=sAC)
                        pend_epi = {"h": h, "g": g, "sp": sp, "gc": gc, "pv": pv}
                        if h == 0 and g == 0 and "attn" in DEBUG_DUMPS:
                            p.emit("sync", lambda s: s.dma_start(out=dbg["dbg_l"], in_=lbuf[0:1, 0, :]), inc=dO, inc_n=16)
                            p.emit("sync", lambda s: s.dma_start(out=dbg["dbg_g"], in_=linv[0:1, 0, :]), inc=dO, inc_n=16)
                            p.emit("sync", lambda s: s.dma_start(out=dbg["dbg_mb"], in_=mbias[:]), inc=dO, inc_n=16)
                            p.emit("sync", lambda s: s.dma_start(out=dbg["dbg_pt"], in_=PT[:]), inc=dO, inc_n=16)
                            PT_free_pe = p.tok(dO)
                        kv_free[par] = pv
                        qh_free[par] = sc
                        for r in range(4):
                            mT_free[0][r] = sc
                if pend_epi is not None:
                    emit_head_epi(pend_epi)
                    pend_epi = None
                for _ in range(3):
                    quant_q3()
                    quant_q2()
                    quant_q1()

                if "attn" in DEBUG_DUMPS:
                    p.barrier(ALL_SEMS)
                    p.emit("sync", lambda s: s.dma_start(out=dbg["dbg_ao"], in_=ao_sc.ap()), inc=dO, inc_n=16)

        p.barrier(ALL_SEMS)

        if EMIT_P2C:
            with ExitStack() as P2C:
                e3 = P2C.enter_context
                wob = e3(nc.sbuf_tensor("wob", [128, NDT, D], BF16))
                aos = e3(nc.sbuf_tensor("aos", [128, 2, NDT, 512], BF16))
                pstg = e3(nc.sbuf_tensor("pstg", [128, 2, 512], F32))
                atok = e3(nc.sbuf_tensor("atok", [128, 4, 2048], F32))
                lntmp = e3(nc.sbuf_tensor("lntmp", [128, 2048], F32))
                xre = e3(nc.sbuf_tensor("xre", [128, 2, 2048], F32))
                x2b = e3(nc.sbuf_tensor("x2b", [128, 2, 2048], F32))
                x2l = e3(nc.sbuf_tensor("x2l", [128, 2, 2048], BF16))
                x2tstg = e3(nc.sbuf_tensor("x2tstg", [128, 2, 2048], BF16))
                stats3 = e3(nc.sbuf_tensor("stats3", [128, 64], F32))
                prps = [e3(nc.psum_tensor("prps0", [128, 512], F32)),
                        e3(nc.psum_tensor("prps1", [128, 512], F32))]
                ttps = [e3(nc.psum_tensor("ttps0", [128, 512], F32)),
                        e3(nc.psum_tensor("ttps1", [128, 512], F32))]
                tbps = [e3(nc.psum_tensor("tbps0", [128, 1024], BF16)),
                        e3(nc.psum_tensor("tbps1", [128, 1024], BF16))]

                for dc in range(4):
                    p.emit("gpsimd", lambda g, dc=dc: g.dma_start(
                        out=wob[:, :, dc * 512:(dc + 1) * 512],
                        in_=w_out[:, dc * 512:(dc + 1) * 512].rearrange("(dt p) m -> p dt m", p=128)),
                        inc=dW[dc % 2], inc_n=16)
                wol_toks = [p.tok(dW[0]), p.tok(dW[1])]

                aos_free = [None, None]
                pstg_free = [None, None]
                prps_free = [None, None]
                ttps_free = [None, None]
                tbps_free = [None, None]
                xre_free = [None, None]
                x2b_free = [None, None]
                x2l_free = [None, None]
                x2tstg_free = [None, None]
                st3_prev = [None, None]
                st3b_prev = [None, None]
                atok_free = None

                for tg in range(4):
                    ap_ = tg % 2
                    p.wait("sync", aos_free[ap_])
                    al = p.emit("sync", lambda s, tg=tg, ap_=ap_: s.dma_start(
                        out=aos[:, ap_],
                        in_=ao_sc.ap()[:, :, tg * 512:(tg + 1) * 512].rearrange("dt p m -> p dt m")),
                        inc=dH[ap_], inc_n=16)
                    aos_last_mm = None
                    ev2 = None
                    for dpt in range(NDT):
                        ps_i = dpt % 2
                        ps = prps[ps_i]
                        p.wait("tensor", prps_free[ps_i])
                        p.wait("tensor", al)
                        for tok2 in wol_toks:
                            p.wait("tensor", tok2)
                        for dint in range(NDT):
                            mm = p.emit("tensor", lambda t, ps=ps, ap_=ap_, dint=dint, dpt=dpt: t.matmul(
                                ps[:], wob[:, dint, dpt * 128:(dpt + 1) * 128], aos[:, ap_, dint, :],
                                start=(dint == 0), stop=(dint == NDT - 1)),
                                inc=(sPE if dint == NDT - 1 else None))
                        aos_last_mm = mm
                        sp = dpt % 2
                        p.wait("vector", mm)
                        p.wait("vector", pstg_free[sp])
                        ev = p.emit("vector", lambda v, ps=ps, sp=sp: v.tensor_copy(pstg[:, sp, :], ps[:]), inc=sDV)
                        prps_free[ps_i] = ev
                        p.wait("tensor", ttps_free[sp])
                        p.wait("tensor", ev)
                        tp = None
                        for t4 in range(4):
                            tp = p.emit("tensor", lambda t, sp=sp, t4=t4: t.transpose(
                                ttps[sp][:, t4 * 128:(t4 + 1) * 128], pstg[:, sp, t4 * 128:(t4 + 1) * 128],
                                identf[:]), inc=(sPE if t4 == 3 else None))
                        pstg_free[sp] = tp
                        p.wait("vector", tp)
                        if atok_free is not None:
                            p.wait("vector", atok_free[0])
                            p.wait("vector", atok_free[1])
                        ev2 = p.emit("vector", lambda v, sp=sp, dpt=dpt: v.tensor_copy(
                            atok[:, :, dpt * 128:(dpt + 1) * 128],
                            ttps[sp].ap().rearrange("p (t m) -> p t m", t=4)), inc=sDV)
                        ttps_free[sp] = ev2
                    aos_free[ap_] = aos_last_mm
                    atok_ready = ev2
                    last_tmp = None
                    last_sq = None
                    x2st2 = None
                    for t4 in range(4):
                        lt = tg * 4 + t4
                        xp = t4 % 2
                        p.wait("sync", xre_free[xp])
                        xl = p.emit("sync", lambda s, lt=lt, xp=xp: s.dma_start(
                            out=xre[:, xp, :], in_=x_in[HALO + lt * 128: HALO + (lt + 1) * 128, :]),
                            inc=dX[xp], inc_n=16)
                        asl = atok[:, t4, :]
                        sl = stats3[:, (t4 % 2) * 16:(t4 % 2) * 16 + 16]
                        p.wait("vector", last_tmp)
                        ap_tok, sq_tok, r2t = ln_chain(atok_ready, asl, sl, lntmp[:],
                                                       slot_prev=st3_prev[t4 % 2])
                        st3_prev[t4 % 2] = r2t
                        last_tmp = ap_tok
                        last_sq = sq_tok
                        p.wait("vector", xl)
                        p.wait("vector", x2b_free[xp])
                        x2t = p.emit("vector", lambda v, xp=xp: v.tensor_tensor(
                            x2b[:, xp, :], lntmp[:], xre[:, xp, :], ALU.add), inc=sDV)
                        xre_free[xp] = x2t
                        p.wait("sync", x2t)
                        x2st = p.emit("sync", lambda s, lt=lt, xp=xp: s.dma_start(
                            out=x2_sc.ap()[lt * 128:(lt + 1) * 128, :], in_=x2b[:, xp, :]),
                            inc=dS[xp], inc_n=16)
                        sl3 = stats3[:, 32 + (t4 % 2) * 16: 32 + (t4 % 2) * 16 + 16]
                        p.wait("vector", x2l_free[xp])
                        ap3, sq3, r3t = ln_chain(x2t, x2b[:, xp, :], sl3, x2l[:, xp, :],
                                                 slot_prev=st3b_prev[t4 % 2])
                        st3b_prev[t4 % 2] = r3t
                        x2b_free[xp] = (ap3, sq3, x2st)
                        for half in range(2):
                            p.wait("tensor", tbps_free[half])
                            p.wait("tensor", ap3)
                            tp2 = None
                            for j in range(8):
                                dt = half * 8 + j
                                tp2 = p.emit("tensor", lambda t, xp=xp, half=half, j=j, dt=dt: t.transpose(
                                    tbps[half][:, j * 128:(j + 1) * 128], x2l[:, xp, dt * 128:(dt + 1) * 128],
                                    identb[:]), inc=(sPE if j == 7 else None))
                            p.wait("vector", tp2)
                            p.wait("vector", x2tstg_free[xp])
                            evt2 = p.emit("vector", lambda v, half=half, xp=xp: v.tensor_copy(
                                x2tstg[:, xp, half * 1024:(half + 1) * 1024], tbps[half].ap()), inc=sDV)
                            tbps_free[half] = evt2
                            p.wait("sync", evt2)
                            x2st2 = p.emit("sync", lambda s, lt=lt, half=half, xp=xp: s.dma_start(
                                out=x2ln_sc.ap()[half * 8:(half + 1) * 8, :, lt * 128:(lt + 1) * 128]
                                    .rearrange("dt p m -> p dt m"),
                                in_=x2tstg[:, xp, half * 1024:(half + 1) * 1024]
                                    .rearrange("p (dt m) -> p dt m", dt=8)), inc=dS[xp], inc_n=16)
                            x2tstg_free[xp] = x2st2
                        x2l_free[xp] = tp2
                    atok_free = (last_tmp, last_sq)

                if "x2" in DEBUG_DUMPS:
                    p.barrier(ALL_SEMS)
                    p.emit("sync", lambda s: s.dma_start(out=dbg["dbg_x2"], in_=x2_sc.ap()), inc=dO, inc_n=16)

        p.barrier(ALL_SEMS)

        p.barrier(ALL_SEMS)

        if EMIT_P3:
            with ExitStack() as P3:
                e4 = P3.enter_context
                x2T = e4(nc.sbuf_tensor("x2T", [128, 1, NDT, 512], BF16))
                gT = e4(nc.sbuf_tensor("gT", [128, NFT, 512], BF16))
                l1b = e4(nc.sbuf_tensor("l1b", [128, 2, NDT, 128], BF16))
                l2b = e4(nc.sbuf_tensor("l2b", [128, 2, 32, 128], BF16))
                o2T = e4(nc.sbuf_tensor("o2T", [128, NDT, 512], F32))
                o2tok = e4(nc.sbuf_tensor("o2tok", [128, 2048], F32))
                lntmp4 = e4(nc.sbuf_tensor("lntmp4", [128, 2048], F32))
                x2re = e4(nc.sbuf_tensor("x2re", [128, 2, 2048], F32))
                stats4 = e4(nc.sbuf_tensor("stats4", [128, 32], F32))
                hps = [e4(nc.psum_tensor("hps0", [128, 512], F32)),
                       e4(nc.psum_tensor("hps1", [128, 512], F32))]
                o2ps = [e4(nc.psum_tensor("o2ps0", [128, 512], F32)),
                        e4(nc.psum_tensor("o2ps1", [128, 512], F32))]
                t4ps = [e4(nc.psum_tensor("t4ps0", [128, 512], F32)),
                        e4(nc.psum_tensor("t4ps1", [128, 512], F32))]

                x2T_free = [None, None]
                l1b_free = [None, None]
                l2b_free = [None, None]
                hps_free = [None, None]
                o2ps_free = [None, None]
                t4ps_free = [None, None]
                gT_free_pe = None
                o2T_free = None
                o2tok_free = None
                st4_prev = [None, None]
                x2re_free = [None, None]
                obuf_free = [None, None]
                lntmp4_free = None

                for tg in range(4):
                    tp_ = 0
                    p.wait("sync", x2T_free[tp_])
                    xtl = p.emit("sync", lambda s, tg=tg, tp_=tp_: s.dma_start(
                        out=x2T[:, tp_],
                        in_=x2ln_sc.ap()[:, :, tg * 512:(tg + 1) * 512].rearrange("dt p m -> p dt m")),
                        inc=dH[tp_], inc_n=16)
                    p.wait("scalar", gT_free_pe)
                    hmm = None
                    for ft in range(NFT):
                        lp = ft % 2
                        p.wait("gpsimd", l1b_free[lp])
                        wl = p.emit("gpsimd", lambda g, ft=ft, lp=lp: g.dma_start(
                            out=l1b[:, lp],
                            in_=w1qT_sc.ap()[ft].rearrange("dt p m -> p dt m")), inc=dF[lp], inc_n=16)
                        ps_i = ft % 2
                        ps = hps[ps_i]
                        p.wait("tensor", hps_free[ps_i])
                        p.wait("tensor", wl)
                        p.wait("tensor", xtl)
                        for dt in range(NDT):
                            hmm = p.emit("tensor", lambda t, ps=ps, lp=lp, dt=dt, tp_=tp_: t.matmul(
                                ps[:], l1b[:, lp, dt, :], x2T[:, tp_, dt, :],
                                start=(dt == 0), stop=(dt == NDT - 1)),
                                inc=(sPE if dt == NDT - 1 else None))
                        l1b_free[lp] = hmm
                        p.wait("scalar", hmm)
                        gl = p.emit("scalar", lambda s, ps=ps, ft=ft: s.activation(
                            gT[:, ft, :], ps[:], AF.Gelu,
                            bias=b1_sb[:, ft:ft + 1], scale=consts[:, C_HALF:C_HALF + 1]), inc=sAC)
                        hps_free[ps_i] = gl
                    x2T_free[tp_] = hmm
                    gelu_done = p.tok(sAC)
                    p.wait("scalar", o2T_free)
                    omm = None
                    ol = None
                    for dpt in range(NDT):
                        for hlf in range(2):
                            lp = (dpt * 2 + hlf) % 2
                            p.wait("gpsimd", l2b_free[lp])
                            wl2 = p.emit("gpsimd", lambda g, dpt=dpt, hlf=hlf, lp=lp: g.dma_start(
                                out=l2b[:, lp],
                                in_=w2qT_sc.ap()[dpt, hlf * 32:(hlf + 1) * 32].rearrange("ft p m -> p ft m")),
                                inc=dF[lp], inc_n=16)
                            ps_i = dpt % 2
                            ps = o2ps[ps_i]
                            if hlf == 0:
                                p.wait("tensor", o2ps_free[ps_i])
                            p.wait("tensor", wl2)
                            p.wait("tensor", gelu_done)
                            for fi in range(32):
                                ft = hlf * 32 + fi
                                omm = p.emit("tensor", lambda t, ps=ps, lp=lp, fi=fi, ft=ft: t.matmul(
                                    ps[:], l2b[:, lp, fi, :], gT[:, ft, :],
                                    start=(ft == 0), stop=(ft == NFT - 1)),
                                    inc=(sPE if fi == 31 else None))
                            l2b_free[lp] = omm
                        p.wait("scalar", omm)
                        ol = p.emit("scalar", lambda s, ps=ps, dpt=dpt: s.activation(
                            o2T[:, dpt, :], ps[:], AF.Identity,
                            bias=b2_sb[:, dpt:dpt + 1], scale=consts[:, C_HALF:C_HALF + 1]), inc=sAC)
                        o2ps_free[dpt % 2] = ol
                    gT_free_pe = omm
                    o2_done = ol
                    if "ffn" in DEBUG_DUMPS and tg == 0:
                        p.wait("sync", p.tok(sAC))
                        p.emit("sync", lambda s: s.dma_start(out=dbg["dbg_g"], in_=gT[:]), inc=dO, inc_n=16)
                        p.emit("sync", lambda s: s.dma_start(out=dbg["dbg_o2"], in_=o2T[:]), inc=dO, inc_n=16)
                    last_tpq = None
                    for t4 in range(4):
                        lt = tg * 4 + t4
                        xp = t4 % 2
                        o2tok_ready = None
                        for q4 in range(4):
                            p.wait("tensor", t4ps_free[xp])
                            p.wait("tensor", o2_done)
                            tpq = None
                            for j in range(4):
                                dt = q4 * 4 + j
                                tpq = p.emit("tensor", lambda t, xp=xp, dt=dt, j=j, t4=t4: t.transpose(
                                    t4ps[xp][:, j * 128:(j + 1) * 128],
                                    o2T[:, dt, t4 * 128:(t4 + 1) * 128], identf[:]),
                                    inc=(sPE if j == 3 else None))
                            last_tpq = tpq
                            p.wait("vector", tpq)
                            p.wait("vector", o2tok_free)
                            evq4 = p.emit("vector", lambda v, q4=q4, xp=xp: v.tensor_copy(
                                o2tok[:, q4 * 512:(q4 + 1) * 512], t4ps[xp].ap()), inc=sDV)
                            t4ps_free[xp] = evq4
                            o2tok_ready = evq4
                        p.wait("sync", x2re_free[xp])
                        xrl = p.emit("sync", lambda s, lt=lt, xp=xp: s.dma_start(
                            out=x2re[:, xp, :], in_=x2_sc.ap()[lt * 128:(lt + 1) * 128, :]),
                            inc=dX[xp], inc_n=16)
                        p.wait("vector", lntmp4_free)
                        ap4, sq4, r4t = ln_chain(o2tok_ready, o2tok[:], stats4[:, xp * 16:(xp + 1) * 16],
                                                 lntmp4[:], slot_prev=st4_prev[xp])
                        st4_prev[xp] = r4t
                        o2tok_free = sq4
                        p.wait("vector", ap4)
                        p.wait("vector", xrl)
                        p.wait("vector", obuf_free[xp])
                        fo = p.emit("vector", lambda v, xp=xp: v.tensor_tensor(
                            x2re[:, xp, :], lntmp4[:], x2re[:, xp, :], ALU.add), inc=sDV)
                        lntmp4_free = fo
                        p.wait("sync", fo)
                        ost = p.emit("sync", lambda s, lt=lt, xp=xp: s.dma_start(
                            out=out[lt * 128:(lt + 1) * 128, :], in_=x2re[:, xp, :]), inc=dO, inc_n=16)
                        x2re_free[xp] = ost
                        obuf_free[xp] = ost
                    o2T_free = last_tpq

        if "ffn" in DEBUG_DUMPS:
            p.barrier(ALL_SEMS)
            p.emit("sync", lambda s: s.dma_start(out=dbg["dbg_x2ln"], in_=x2ln_sc.ap()), inc=dO, inc_n=16)

        p.wait("sync", p.tok(dO))

        with nc.Block() as block:
            @block.sync
            def _(s):
                p.replay("sync", s)

            @block.tensor
            def _(t):
                p.replay("tensor", t)

            @block.vector
            def _(v):
                p.replay("vector", v)

            @block.scalar
            def _(s):
                p.replay("scalar", s)

            @block.gpsimd
            def _(g):
                p.replay("gpsimd", g)

    return nc, dbg



def _host_inputs(L0_attn_x, L0_attn_w_qkv, L0_attn_w_out, L0_attn_q_gamma,
                 L0_attn_k_gamma, L0_ffn_w1, L0_ffn_b1, L0_ffn_w2, L0_ffn_b2):
    x = np.asarray(L0_attn_x, dtype=np.float32)
    inv_freq = 1.0 / THETA ** (np.arange(0, HD, 2, dtype=np.float32) / HD)

    ident_b = np.eye(128, dtype=np.float32).astype(ml_dtypes.bfloat16)
    ident_f = np.eye(128, dtype=np.float32)
    ones_f = np.ones((128, 128), np.float32)
    ones_b = np.ones((128, 128), np.float32).astype(ml_dtypes.bfloat16)
    oo128 = np.full((128, 128), 1.0 / 128.0, dtype=np.float32).astype(ml_dtypes.bfloat16)
    protm = np.zeros((128, 128), np.float32)
    for j in range(64):
        protm[j + 64, j] = -1.0
    for j in range(64, 128):
        protm[j - 64, j] = 1.0
    protm = protm.astype(ml_dtypes.bfloat16)
    consts = np.zeros((128, 16), np.float32)
    consts[:, C_EPSRMS] = 1e-6
    consts[:, C_EPSLN] = 1e-5
    consts[:, C_NHALF] = -0.5
    consts[:, C_NEG1] = -1.0
    consts[:, C_HALF] = 0.5
    consts[:, C_INV2048] = 1.0 / 2048.0

    wqkv = np.ascontiguousarray(np.asarray(L0_attn_w_qkv, np.float32))
    wout = np.ascontiguousarray(np.asarray(L0_attn_w_out, np.float32))
    w1 = np.ascontiguousarray(np.asarray(L0_ffn_w1, np.float32))
    w2 = np.ascontiguousarray(np.asarray(L0_ffn_w2, np.float32))
    b1 = np.ascontiguousarray(np.asarray(L0_ffn_b1, np.float32).reshape(NFT, 128))
    b2 = np.ascontiguousarray(np.asarray(L0_ffn_b2, np.float32).reshape(NDT, 128))
    qga = np.asarray(L0_attn_q_gamma, np.float32)
    kga = np.asarray(L0_attn_k_gamma, np.float32)
    gam4 = np.stack([qga, np.concatenate([qga[64:], qga[:64]]),
                     kga, np.concatenate([kga[64:], kga[:64]])], axis=1)
    gam4 = np.ascontiguousarray(gam4)

    maps = []
    for c in range(8):
        beta, i = divmod(c, 4)
        a = TOK * i
        lo = a - HALO
        xs = np.zeros((LTOK, D), np.float32)
        src_lo = max(lo, 0)
        xs[src_lo - lo:, :] = x[beta, src_lo:a + TOK, :]
        pos = np.arange(lo, a + TOK, dtype=np.float32)
        pos = np.maximum(pos, 0.0)
        ang = pos[None, :] * np.concatenate([inv_freq, inv_freq])[:, None]
        cos_t = np.cos(ang).astype(np.float32)
        sin_t = np.sin(ang).astype(np.float32)
        mkT = np.full((4, WTILES, 128, 512), NEG, np.float32)
        qq = np.arange(512)
        for g in range(4):
            qglob = a + 512 * g + qq
            iq = qq // 128
            for jj in range(WTILES):
                kglob = lo + 128 * (4 * g + jj) + np.arange(128)
                m = (qglob[None, :] >= kglob[:, None]) & \
                    (qglob[None, :] - kglob[:, None] <= WINDOW) & \
                    (kglob[:, None] >= 0) & \
                    (iq[None, :] <= jj) & (jj <= iq[None, :] + 32)
                mkT[g, jj] = np.where(m, 0.0, NEG)
        maps.append({
            "x_pad": xs, "w_qkv": wqkv, "w_out": wout, "gam4": gam4,
            "w1": w1, "b1": b1, "w2": w2, "b2": b2,
            "cos_t": cos_t, "sin_t": sin_t,
            "masksT": mkT.astype(ml_dtypes.bfloat16),
            "identb": ident_b, "identf": ident_f, "onesf": ones_f, "onesb": ones_b,
            "oo128": oo128, "prot": protm, "consts": consts,
        })
    return maps


_RES_HOOK = [None]


def kernel(**inputs):
    maps = _host_inputs(**inputs)
    nc, dbg = build_program()
    res = run_bass_kernel_spmd(nc, maps, core_ids=list(range(8)))
    _RES_HOOK[0] = res
    full = np.zeros((B, N, D), np.float32)
    for c in range(8):
        beta, i = divmod(c, 4)
        full[beta, TOK * i:TOK * (i + 1), :] = res.results[c]["out"]
    return full
